# version 14
# baseline (speedup 1.0000x reference)
import math
from contextlib import ExitStack

import numpy as np
import concourse.bass as bass
import concourse.mybir as mybir
from concourse.bass_utils import run_bass_kernel_spmd

F32 = mybir.dt.float32
BF16 = mybir.dt.bfloat16
AF = mybir.ActivationFunctionType
ALU = mybir.AluOpType
AX = mybir.AxisListType

ENGINES = ("sync", "tensor", "vector", "scalar", "gpsimd")



class Buf:
    __slots__ = ("name", "base_writers", "acc_writers", "readers")

    def __init__(self, name=""):
        self.name = name
        self.base_writers = []
        self.acc_writers = []
        self.readers = []


class Step:
    __slots__ = ("eng", "fn", "deps", "needs_inc", "val", "wait_val",
                 "is_dma", "pe_accum", "tag", "idx")

    def __init__(self, eng, fn, is_dma, tag=""):
        self.eng = eng
        self.fn = fn
        self.deps = []
        self.needs_inc = False
        self.val = None
        self.wait_val = None
        self.is_dma = is_dma
        self.pe_accum = False
        self.tag = tag
        self.idx = 0


class Sched:
    def __init__(self, nc):
        self.nc = nc
        self.streams = {e: [] for e in ENGINES}
        self.ctx = ExitStack()
        self.sems = {}
        for e in ENGINES:
            self.sems[e] = self.ctx.enter_context(nc.semaphore(f"S_{e}"))
            self.sems[e + "_dma"] = self.ctx.enter_context(
                nc.semaphore(f"D_{e}"))
        self.final_out_steps = []
        self.cov = {e: {} for e in ENGINES}

    def do(self, eng, fn, reads=(), writes=(), accwrites=(), is_dma=False,
           tag=""):
        st = Step(eng, fn, is_dma, tag)
        dep = set()
        for b in reads:
            dep.update(b.base_writers)
            dep.update(b.acc_writers)
        for b in writes:
            dep.update(b.base_writers)
            dep.update(b.acc_writers)
            dep.update(b.readers)
        for b in accwrites:
            dep.update(b.readers)
            dep.update(b.base_writers)
        cov = self.cov[eng]
        for d in dep:
            if d is st:
                continue
            if d.eng == "tensor" and eng == "tensor" and not d.is_dma:
                continue
            key = (d.eng, d.is_dma)
            if cov.get(key, -1) >= d.idx:
                continue
            st.deps.append(d)
            d.needs_inc = True
        for d in st.deps:
            key = (d.eng, d.is_dma)
            if cov.get(key, -1) < d.idx:
                cov[key] = d.idx
        for b in reads:
            b.readers.append(st)
        for b in writes:
            b.base_writers = [st]
            b.acc_writers = []
            b.readers = []
        for b in accwrites:
            b.acc_writers.append(st)
        st.idx = len(self.streams[eng])
        self.streams[eng].append(st)
        return st

    def barrier(self):
        lasts = [s[-1] for s in self.streams.values() if s]
        for e in ENGINES:
            for d in lasts:
                if d.eng == e and self.streams[e] and self.streams[e][-1] is d:
                    continue

        def nopfn(eng):
            return eng.nop()
        for e in ENGINES:
            st = Step(e, nopfn, False, "barrier")
            for d in lasts:
                if d.eng == e:
                    continue
                st.deps.append(d)
                d.needs_inc = True
            self.streams[e].append(st)

    def mark_output(self, step):
        step.needs_inc = True
        self.final_out_steps.append(step)

    def emit(self):
        for e in ENGINES:
            c = {False: 0, True: 0}
            for st in self.streams[e]:
                st.needs_inc = True
                c[st.is_dma] += 16 if st.is_dma else 1
                st.val = c[st.is_dma]

        for e in ENGINES:
            runc, rund = [], []

            def close(lst):
                vals = [s.val for s in lst if s.val is not None]
                if vals:
                    v = max(vals)
                    for s in lst:
                        if s.val is not None:
                            s.wait_val = v
                lst.clear()

            for st in self.streams[e]:
                if st.deps:
                    close(runc)
                    close(rund)
                (rund if st.is_dma else runc).append(st)
            close(runc)
            close(rund)

        def sem_key(st):
            return st.eng + "_dma" if st.is_dma else st.eng

        def wval(st):
            return st.wait_val if st.wait_val is not None else st.val

        observed = {}
        for e in ENGINES:
            for st in self.streams[e]:
                for d in st.deps:
                    observed.setdefault(sem_key(d), set()).add(wval(d))
        for st in self.final_out_steps:
            observed.setdefault(sem_key(st), set()).add(wval(st))
        obs_sorted = {k: sorted(v) for k, v in observed.items()}

        nc = self.nc

        def run_stream(engname):
            def _(eng):
                own_dma = engname + "_dma"
                own_c = engname
                obs_d = obs_sorted.get(own_dma, [])
                obs_c = obs_sorted.get(own_c, [])
                pd = [0]
                pc = [0]
                cum = {own_dma: 0, own_c: 0}
                sw = {own_dma: 0, own_c: 0}

                def maybe_selfwait(key, obs, ptr):
                    n = ptr[0]
                    while n < len(obs) and obs[n] <= cum[key]:
                        n += 1
                    ptr[0] = n
                    if n > 0 and obs[n - 1] > sw[key]:
                        eng.wait_ge(self.sems[key], cum[key])
                        sw[key] = cum[key]

                for st in self.streams[engname]:
                    if st.deps:
                        maybe_selfwait(own_dma, obs_d, pd)
                        maybe_selfwait(own_c, obs_c, pc)
                        waits = {}
                        for d in st.deps:
                            s = sem_key(d)
                            v = wval(d)
                            if waits.get(s, -1) < v:
                                waits[s] = v
                        for s, v in sorted(waits.items()):
                            eng.wait_ge(self.sems[s], v)
                    inst = st.fn(eng)
                    inst.then_inc(self.sems[sem_key(st)],
                                  16 if st.is_dma else 1)
                    cum[sem_key(st)] += 16 if st.is_dma else 1
                if engname == "sync":
                    for st in self.final_out_steps:
                        eng.wait_ge(self.sems[sem_key(st)], wval(st))
            return _

        with nc.Block() as block:
            block.sync(run_stream("sync"))
            block.tensor(run_stream("tensor"))
            block.vector(run_stream("vector"))
            block.scalar(run_stream("scalar"))
            block.gpsimd(run_stream("gpsimd"))
        self.ctx.close()


D = 2048
TOK = 2048
HALO = 128
HTOK = TOK + HALO
NH0 = 16
DH0 = 128
C = 128
NCH = TOK // C
NE = 8
H1, KV1, DH1 = 8, 2, 256
SW = 512
INTER = 5632
NIC = INTER // 128

GROUPS = [[0, 1, 2, 3], [4, 5, 6, 7]]



class PB:

    def __init__(self, name="prog"):
        self.nc = bass.Bass("TRN2", target_bir_lowering=False)
        self.S = Sched(self.nc)
        self.dram = {}
        self.dram_kind = {}
        self._consts = None
        self._const_vals = {}
        self._const_n = 0
        self._warmed = False
        self.uid = 0
        self.scopes = []
        self.state = {}

    def io(self, name, shape, dtype, kind):
        if name in self.dram:
            return self.dram[name][0]
        k = {"in": "ExternalInput", "out": "ExternalOutput",
             "internal": "Internal"}[kind]
        h = self.nc.dram_tensor(name, list(shape), dtype, kind=k)
        self.dram[name] = (h, Buf(name))
        self.dram_kind[name] = kind
        return h

    def dbuf(self, name):
        return self.dram[name][1]

    def sb(self, name, shape, dtype):
        self.uid += 1
        stack = self.scopes[-1] if self.scopes else self.S.ctx
        t = stack.enter_context(
            self.nc.sbuf_tensor(f"{name}_{self.uid}", list(shape), dtype))
        return t, Buf(name)

    def ps(self, name, shape, dtype=F32):
        self.uid += 1
        stack = self.scopes[-1] if self.scopes else self.S.ctx
        per = 2048 // {F32: 4, BF16: 2}[dtype]
        shape = [shape[0], max(shape[1], per)]
        t = stack.enter_context(
            self.nc.psum_tensor(f"{name}_{self.uid}", list(shape), dtype))
        return t, Buf(name)

    def push_scope(self):
        self.scopes.append(ExitStack())

    def pop_scope(self):
        self.scopes.pop().close()
        self.S.barrier()

    def boot(self):
        self._consts, self._cb = self.sb("csts", [128, 32], F32)
        self._warm, self._wb = self.sb("warm", [128, 8], F32)
        S = self.S
        cst = self._consts

        def mk(e):
            return e.memset(cst[:, 0:1], 0.0)
        s0 = S.do("vector", mk, writes=[self._cb], tag="cst0")
        S.do("scalar", lambda e: e.copy(self._warm[:, 0:1], cst[:, 0:1]),
             reads=[self._cb], writes=[self._wb], tag="warmup")
        self._const_vals[0.0] = 0
        self._const_n = 1

    def cv(self, val):
        v = float(val)
        if v in self._const_vals:
            j = self._const_vals[v]
        else:
            j = self._const_n
            assert j < 32, "consts tile full"
            self._const_n += 1
            self._const_vals[v] = j
            cst = self._consts
            self.S.do("vector", lambda e, j=j, v=v: e.memset(cst[:, j:j + 1], v),
                      writes=[self._cb], tag=f"cst{v}")
        return self._consts[:, j:j + 1]

    @property
    def cbuf(self):
        return self._cb



def p_layernorm(pb, src_name, dst_name, ntok, eps=1e-5, keep=True):
    S, nc = pb.S, pb.nc
    ntt = ntok // 128
    xsrc = pb.io(src_name, [ntok, D], F32, "in")
    xb = pb.dbuf(src_name)
    xdst = pb.io(dst_name, [16, 128, ntok], BF16, "out")
    db = pb.dbuf(dst_name)

    xnd = []
    for c in range(16):
        t, b = pb.sb(f"xnd{c}", [128, ntok], BF16)
        xnd.append((t, b))
    pb.state[dst_name] = xnd

    pb.push_scope()
    xt = [pb.sb("xt", [128, D], F32) for _ in range(2)]
    sq = [pb.sb("sq", [128, D], F32) for _ in range(2)]
    st_ = [pb.sb("st", [128, 8], F32) for _ in range(2)]
    xn = [pb.sb("xn", [128, D], BF16) for _ in range(2)]

    for tt in range(ntt):
        k = tt % 2
        xtile, xtb = xt[k]
        sqt, sqb = sq[k]
        stt_, stb = st_[k]
        xnt, xnb = xn[k]
        r0 = tt * 128
        S.do("sync", lambda e, xtile=xtile, r0=r0: e.dma_start(
            out=xtile[:], in_=xsrc.ap()[r0:r0 + 128, :]),
            reads=[xb], writes=[xtb], is_dma=True, tag=f"ln_ld{tt}")
        S.do("vector", lambda e, xtile=xtile, stt_=stt_: e.reduce_sum(
            stt_[:, 0:1], xtile[:], axis=AX.X),
            reads=[xtb], writes=[stb], tag="ln_sum")
        S.do("vector", lambda e, xtile=xtile, sqt=sqt:
             e.tensor_tensor(sqt[:], xtile[:], xtile[:], ALU.mult),
             reads=[xtb], writes=[sqb], tag="ln_sq")
        S.do("vector", lambda e, sqt=sqt, stt_=stt_:
             e.reduce_sum(stt_[:, 1:2], sqt[:], axis=AX.X),
             reads=[sqb], writes=[stb], tag="ln_sq2")
        S.do("vector", lambda e, stt_=stt_: e.tensor_scalar(
            stt_[:, 2:3], stt_[:, 0:1], 1.0 / D, None, ALU.mult),
            reads=[stb], writes=[stb], tag="ln_mu")
        S.do("vector", lambda e, stt_=stt_: e.tensor_scalar(
            stt_[:, 3:4], stt_[:, 1:2], 1.0 / D, None, ALU.mult),
            reads=[stb], writes=[stb], tag="ln_m2")
        S.do("vector", lambda e, stt_=stt_: e.tensor_tensor(
            stt_[:, 4:5], stt_[:, 2:3], stt_[:, 2:3], ALU.mult),
            reads=[stb], writes=[stb], tag="ln_mu2")
        S.do("vector", lambda e, stt_=stt_: e.tensor_tensor(
            stt_[:, 5:6], stt_[:, 3:4], stt_[:, 4:5], ALU.subtract),
            reads=[stb], writes=[stb], tag="ln_var")
        c_eps = pb.cv(eps)
        c_mh = pb.cv(-0.5)
        S.do("scalar", lambda e, stt_=stt_, c_eps=c_eps: e.activation(
            stt_[:, 6:7], stt_[:, 5:6], AF.Ln, bias=c_eps),
            reads=[stb, pb.cbuf], writes=[stb], tag="ln_ln")
        S.do("scalar", lambda e, stt_=stt_, c_mh=c_mh: e.activation(
            stt_[:, 7:8], stt_[:, 6:7], AF.Exp, scale=c_mh),
            reads=[stb, pb.cbuf], writes=[stb], tag="ln_exp")
        S.do("vector", lambda e, xtile=xtile, xnt=xnt, stt_=stt_:
             e.tensor_scalar(xnt[:], xtile[:], stt_[:, 2:3], stt_[:, 7:8],
                             ALU.subtract, ALU.mult),
             reads=[xtb, stb], writes=[xnb], tag="ln_app")
        for c in range(16):
            dstt, dstb = xnd[c]
            S.do("sync",
                 lambda e, dstt=dstt, xnt=xnt, c=c, tt=tt:
                 e.dma_start_transpose(dstt[:, tt * 128:(tt + 1) * 128],
                                       xnt[:, c * 128:(c + 1) * 128]),
                 reads=[xnb], accwrites=[dstb], is_dma=True, tag="ln_tp")
    for c in range(16):
        dstt, dstb = xnd[c]
        st2 = S.do("sync", lambda e, dstt=dstt, c=c: e.dma_start(
            out=xdst.ap()[c], in_=dstt[:]),
            reads=[dstb], writes=[db], is_dma=True, tag=f"ln_out{c}")
        S.mark_output(st2)
    pb.pop_scope()
    if not keep:
        pb.state.pop(dst_name)
    return xnd



def p_beta(pb):
    S = pb.S
    wb = pb.io("L0_attn_Wb", [D, NH0], F32, "in")
    wbb = pb.dbuf("L0_attn_Wb")
    bout = pb.io("BETA", [128, NH0 * NCH], F32, "out")
    bob = pb.dbuf("BETA")
    xnd = pb.state["XN_D"]

    beta_t, beta_b = pb.sb("beta", [128, NH0 * NCH], F32)
    pb.state["BETA_SB"] = (beta_t, beta_b)

    pb.push_scope()
    wbt, wbtb = pb.sb("wbs", [128, 16 * NH0], BF16)
    S.do("gpsimd", lambda e: e.dma_start(
        out=wbt[:].rearrange("p (k h) -> p k h", k=16),
        in_=wb.ap().rearrange("(k p) h -> p k h", p=128)),
        reads=[wbb], writes=[wbtb], is_dma=True, tag="wb_ld")
    pp, ppb = pb.ps("pbeta", [128, NH0])
    t1, t1b = pb.sb("bt1", [128, NH0], F32)
    t2, t2b = pb.sb("bt2", [128, NH0], F32)
    cm1 = pb.cv(-1.0)
    c1 = pb.cv(1.0)
    for tt in range(NCH):
        for k in range(16):
            S.do("tensor", lambda e, k=k, tt=tt: e.matmul(
                pp[:, 0:NH0],
                xnd[k][0][:, HALO + tt * 128: HALO + (tt + 1) * 128],
                wbt[:, k * 16:(k + 1) * 16],
                start=(k == 0), stop=(k == 15)),
                reads=[xnd[k][1], wbtb], writes=[ppb], tag="b_mm")
        S.do("scalar", lambda e: e.activation(t1[:], pp[:, 0:NH0], AF.Exp, scale=cm1),
             reads=[ppb, pb.cbuf], writes=[t1b], tag="b_exp")
        S.do("vector", lambda e: e.tensor_scalar(t2[:], t1[:], c1, None,
                                                 ALU.add),
             reads=[t1b, pb.cbuf], writes=[t2b], tag="b_add1")
        S.do("vector", lambda e, tt=tt: e.reciprocal(
            bass.AP(beta_t, tt * 4, [[beta_t.ap().ap[0][0], 128], [16 * 4, 16]])
            if False else beta_t[:, tt::16], t2[:]),
            reads=[t2b], accwrites=[beta_b], tag="b_rec")
    stb = S.do("sync", lambda e: e.dma_start(out=bout.ap(), in_=beta_t[:]),
               reads=[beta_b], writes=[bob], is_dma=True, tag="b_out")
    S.mark_output(stb)
    pb.pop_scope()



def p_delta1(pb, heads=range(NH0)):
    S = pb.S
    xnd = pb.state["XN_D"]
    beta_t, beta_b = pb.state["BETA_SB"]

    for nm in ("L0_attn_Wq", "L0_attn_Wk", "L0_attn_Wv"):
        pb.io(nm, [D, D], F32, "in")
    for nm in ("L0_attn_conv_wq", "L0_attn_conv_wk", "L0_attn_conv_wv"):
        pb.io(nm, [D, 4], F32, "in")
    spill = pb.io("SPILL", [NH0, NCH, 128, 512], BF16, "out")
    spb = pb.dbuf("SPILL")
    qsp = pb.io("QSP", [NH0, 128, TOK], BF16, "out")
    qspb = pb.dbuf("QSP")
    scal = pb.io("SCAL", [NH0, 128, 32], F32, "out")
    scb = pb.dbuf("SCAL")
    abg = pb.io("ABG", [NH0 * 128, 256], F32, "out")
    abb = pb.dbuf("ABG")
    identb = pb.io("IDENTB", [128, 128], BF16, "in")
    identf = pb.io("IDENTF", [128, 128], F32, "in")
    trilSb = pb.io("TRILS", [128, 128], BF16, "in")
    triuSb = pb.io("TRIUS", [128, 128], BF16, "in")
    triuIb = pb.io("TRIUI", [128, 128], BF16, "in")
    onesb = pb.io("ONESB", [128, 1], BF16, "in")

    pb.push_scope()
    cb = {}
    for nm, h_ in [("IDENTB", identb), ("IDENTF", identf), ("TRILS", trilSb),
                   ("TRIUS", triuSb), ("TRIUI", triuIb), ("ONESB", onesb)]:
        shape = [128, 1] if nm == "ONESB" else [128, 128]
        dt = F32 if nm == "IDENTF" else BF16
        t, b = pb.sb(nm.lower(), shape, dt)
        S.do("sync", lambda e, t=t, h_=h_: e.dma_start(out=t[:], in_=h_.ap()),
             reads=[pb.dbuf(nm)], writes=[b], is_dma=True, tag=f"c_{nm}")
        cb[nm] = (t, b)
    wide = {}
    for nm in ("IDENTB", "TRILS", "TRIUS"):
        t, b = pb.sb(nm.lower() + "4", [128, 512], BF16)
        for j in range(4):
            S.do("vector", lambda e, t=t, j=j, nm=nm: e.tensor_copy(
                t[:, j * 128:(j + 1) * 128], cb[nm][0][:]),
                reads=[cb[nm][1]],
                writes=[b] if j == 0 else [], accwrites=[b] if j else [],
                tag="wide_c")
        wide[nm] = (t, b)
    twoI4, twoI4b = pb.sb("twoI4", [128, 512], BF16)
    S.do("vector", lambda e: e.tensor_scalar(twoI4[:], wide["IDENTB"][0][:],
                                             2.0, None, ALU.mult),
         reads=[wide["IDENTB"][1]], writes=[twoI4b], tag="twoI")
    s01, s01b = pb.sb("s01", [128, 256], BF16)
    S.do("vector", lambda e: e.memset(s01[:, 0:128], 0.0),
         writes=[s01b], tag="s01z")
    S.do("vector", lambda e: e.tensor_copy(s01[:, 128:256], cb["IDENTB"][0][:]),
         reads=[cb["IDENTB"][1]], accwrites=[s01b], tag="s01i")
    s01f, s01fb = pb.sb("s01f", [128, 256], F32)
    S.do("vector", lambda e: e.memset(s01f[:, 0:128], 0.0),
         writes=[s01fb], tag="s01fz")
    S.do("vector", lambda e: e.tensor_copy(s01f[:, 128:256],
                                           cb["IDENTF"][0][:]),
         reads=[cb["IDENTF"][1]], accwrites=[s01fb], tag="s01fi")

    wbl = [pb.sb(f"w{t3}", [128, D], BF16) for t3 in range(3)]
    cwt = [pb.sb(f"cw{t3}", [128, 4], F32) for t3 in range(3)]
    lin = [pb.sb(f"lin{t3}", [128, HTOK], BF16) for t3 in range(2)]
    sg1, sg1b = pb.sb("sg1", [128, HTOK], F32)
    sg2, sg2b = pb.sb("sg2", [128, HTOK], BF16)
    vcs, vcsb = pb.sb("vcs", [128, HTOK], BF16)
    PLN = 3
    P = []
    for _ in range(PLN):
        P.append(dict(
            cvo=[pb.sb(f"cv{t3}", [128, HTOK], BF16) for t3 in range(2)],
            kk=pb.sb("kkal", [128, TOK], BF16),
            vp=pb.sb("vpal", [128, TOK], BF16),
            hsc={nm: pb.sb(nm, [128, 16], F32)
                 for nm in ("rnk", "rnq", "sqb", "bm12n", "sk", "alpha",
                            "isb")},
            tn1=pb.sb("tn1", [128, 32], F32),
            sc2=pb.sb("sc2", [128, 32], F32),
            nrmk=pb.sb("nrmk", [128, 16], F32),
        ))
    CLN = 2
    PKD = 6
    CL = []
    for _ in range(CLN):
        d = dict(
            gg=[pb.sb("gg", [128, 512], BF16) for _ in range(2)],
            ggt=[pb.sb("ggt", [128, 512], BF16) for _ in range(2)],
            yp=pb.sb("yp", [128, 512], BF16),
            alow=pb.sb("alow", [128, 512], BF16),
            araw4=pb.sb("araw4", [128, 512], BF16),
            diag4=pb.sb("diag4", [128, 512], BF16),
            a4=pb.sb("a4", [128, 512], BF16),
            pk=[pb.sb("pk", [128, 512], BF16) for _ in range(PKD)],
            upair=pb.sb("upair", [128, 256], BF16),
            stsb=pb.sb("stsb", [128, 256], BF16),
            stsf=pb.sb("stsf", [128, 256], F32),
            abf=[pb.sb("abf", [128, 128], F32) for _ in range(2)],
            pX=pb.ps("pX", [128, 512]),
            pY=pb.ps("pY", [128, 512]),
            pZ=pb.ps("pZ", [128, 512]),
        )
        CL.append(d)
    pPJ = [pb.ps("pPJ", [128, 512]) for _ in range(2)]

    WQ, WK, WV = (pb.dram["L0_attn_Wq"][0], pb.dram["L0_attn_Wk"][0],
                  pb.dram["L0_attn_Wv"][0])
    WQb, WKb, WVb = (pb.dbuf("L0_attn_Wq"), pb.dbuf("L0_attn_Wk"),
                     pb.dbuf("L0_attn_Wv"))
    CWs = [(pb.dram["L0_attn_conv_wq"][0], pb.dbuf("L0_attn_conv_wq")),
           (pb.dram["L0_attn_conv_wk"][0], pb.dbuf("L0_attn_conv_wk")),
           (pb.dram["L0_attn_conv_wv"][0], pb.dbuf("L0_attn_conv_wv"))]
    c1 = pb.cv(1.0)
    cm1 = pb.cv(-1.0)
    c0 = pb.cv(0.0)
    cmh = pb.cv(-0.5)
    ch_ = pb.cv(0.5)
    ceps = pb.cv(1e-6)
    calpha = pb.cv(1.0 / math.sqrt(128.0))
    heads = list(heads)

    def prep(hi, h):
        L = P[hi % PLN]
        hsc = L["hsc"]
        tn1, tn1b = L["tn1"]
        sc2, sc2b = L["sc2"]
        for t3, (W, Wb_) in enumerate([(WQ, WQb), (WK, WKb), (WV, WVb)]):
            wt, wtb = wbl[t3]
            S.do("gpsimd", lambda e, wt=wt, W=W, h=h: e.dma_start(
                out=wt[:].rearrange("p (k o) -> p k o", k=16),
                in_=W.ap()[:, h * 128:(h + 1) * 128].rearrange(
                    "(k p) o -> p k o", p=128)),
                reads=[Wb_], writes=[wtb], is_dma=True, tag=f"wld{t3}")
            ct, ctb = cwt[t3]
            S.do("gpsimd", lambda e, ct=ct, CW=CWs[t3][0], h=h: e.dma_start(
                out=ct[:], in_=CW.ap()[h * 128:(h + 1) * 128, :]),
                reads=[CWs[t3][1]], writes=[ctb], is_dma=True, tag=f"cwld{t3}")
        yield
        NTT = [(0, 512), (512, 512), (1024, 512), (1536, 512), (2048, 128)]
        for t3 in range(3):
            wt, wtb = wbl[t3]
            lint, linb = lin[t3 % 2]
            for ni, (n0, nl) in enumerate(NTT):
                pp, ppb2 = pPJ[ni % 2]
                for k in range(16):
                    S.do("tensor", lambda e, pp=pp, wt=wt, k=k, n0=n0, nl=nl:
                         e.matmul(pp[:, 0:nl],
                                  wt[:, k * 128:(k + 1) * 128],
                                  xnd[k][0][:, n0:n0 + nl],
                                  start=(k == 0), stop=(k == 15)),
                         reads=[wtb, xnd[k][1]], writes=[ppb2], tag="pj_mm")
                    if k == 7:
                        yield
                S.do("scalar", lambda e, lint=lint, pp=pp, n0=n0, nl=nl:
                     e.copy(lint[:, n0:n0 + nl], pp[:, 0:nl]),
                     reads=[ppb2], accwrites=[linb], tag="pj_ev")
                yield
            cvt, cvb = L["cvo"][t3] if t3 < 2 else (vcs, vcsb)
            ct, ctb = cwt[t3]
            Lc = HTOK - 3
            S.do("vector", lambda e, cvt=cvt, lint=lint, ct=ct, Lc=Lc:
                 e.tensor_scalar(cvt[:, 3:HTOK], lint[:, 0:Lc], ct[:, 0:1],
                                 None, ALU.mult),
                 reads=[linb, ctb], writes=[cvb], tag="cv0")
            for i in (1, 2, 3):
                S.do("vector", lambda e, cvt=cvt, lint=lint, ct=ct, i=i,
                     Lc=Lc: e.scalar_tensor_tensor(
                         cvt[:, 3:HTOK], lint[:, i:i + Lc],
                         ct[:, i:i + 1], cvt[:, 3:HTOK],
                         ALU.mult, ALU.add),
                     reads=[linb, ctb, cvb], writes=[cvb], tag=f"cv{i}")
            yield
            S.do("scalar", lambda e, cvt=cvt: e.activation(
                sg1[:, 3:HTOK], cvt[:, 3:HTOK], AF.Exp, scale=cm1),
                reads=[cvb, pb.cbuf], writes=[sg1b], tag="si_e")
            S.do("scalar", lambda e: e.activation(
                sg1[:, 3:HTOK], sg1[:, 3:HTOK], AF.Ln, bias=c1),
                reads=[pb.cbuf], accwrites=[sg1b], tag="si_l")
            S.do("scalar", lambda e: e.activation(
                sg2[:, 3:HTOK], sg1[:, 3:HTOK], AF.Exp, scale=cm1),
                reads=[sg1b, pb.cbuf], writes=[sg2b], tag="si_s")
            S.do("vector", lambda e, cvt=cvt: e.tensor_tensor(
                cvt[:, 3:HTOK], cvt[:, 3:HTOK], sg2[:, 3:HTOK], ALU.mult),
                reads=[sg2b, cvb], writes=[cvb], tag="si_m")
            yield
        qc_t, qc_b = L["cvo"][0]
        kc_t, kc_b = L["cvo"][1]
        vc_t, vc_b = vcs, vcsb
        st = S.do("gpsimd", lambda e, h=h, qc_t=qc_t: e.dma_start(
            out=qsp.ap()[h], in_=qc_t[:, HALO:HTOK]),
            reads=[qc_b], accwrites=[qspb], is_dma=True, tag="q_sp")
        S.mark_output(st)
        kkt, kkb = L["kk"]
        vpt, vpb = L["vp"]
        for c in range(NCH):
            col = HALO + c * 128
            S.do("scalar", lambda e, kkt=kkt, kc_t=kc_t, col=col, c=c:
                 e.dma_start_transpose(kkt[:, c * 128:(c + 1) * 128],
                                       kc_t[:, col:col + 128]),
                 reads=[kc_b], accwrites=[kkb], is_dma=True, tag="tp_k")
            S.do("sync", lambda e, vpt=vpt, vc_t=vc_t, col=col, c=c:
                 e.dma_start_transpose(vpt[:, c * 128:(c + 1) * 128],
                                       vc_t[:, col:col + 128]),
                 reads=[vc_b], accwrites=[vpb], is_dma=True, tag="tp_v")
        yield
        nk, nkb = L["nrmk"]
        S.do("vector", lambda e, kkt=kkt: e.tensor_tensor(
            sg2[:, 0:TOK], kkt[:], kkt[:], ALU.mult),
            reads=[kkb], writes=[sg2b], tag="nm_sq")
        for c in range(NCH):
            S.do("vector", lambda e, c=c, nk=nk: e.reduce_sum(
                nk[:, c:c + 1], sg2[:, c * 128:(c + 1) * 128], axis=AX.X),
                reads=[sg2b], accwrites=[nkb], tag="nm_rd")
        yield
        ppq, ppqb = pPJ[0]
        S.do("vector", lambda e, qc_t=qc_t: e.tensor_tensor(
            sg2[:, 0:TOK], qc_t[:, HALO:HTOK], qc_t[:, HALO:HTOK], ALU.mult),
            reads=[qc_b], writes=[sg2b], tag="nm_sqq")
        for c in range(NCH):
            S.do("tensor", lambda e, c=c, ppq=ppq: e.matmul(
                ppq[:, c:c + 1], sg2[:, c * 128:(c + 1) * 128],
                cb["ONESB"][0][:], start=True, stop=True),
                reads=[sg2b, cb["ONESB"][1]], writes=[ppqb], tag="nm_mmq")
        yield
        S.do("scalar", lambda e, tn1=tn1, nk=nk: e.activation(
            tn1[:, 0:16], nk[:], AF.Ln, bias=ceps),
            reads=[nkb, pb.cbuf], writes=[tn1b], tag="nm_lnk")
        S.do("scalar", lambda e, tn1=tn1, ppq=ppq: e.activation(
            tn1[:, 16:32], ppq[:, 0:16], AF.Ln, bias=ceps),
            reads=[ppqb, pb.cbuf], accwrites=[tn1b], tag="nm_lnq")
        S.do("scalar", lambda e, tn1=tn1, hsc=hsc: e.activation(
            hsc["rnk"][0][:], tn1[:, 0:16], AF.Exp, scale=cmh),
            reads=[tn1b, pb.cbuf], writes=[hsc["rnk"][1]], tag="nm_ek")
        S.do("scalar", lambda e, tn1=tn1, hsc=hsc: e.activation(
            hsc["rnq"][0][:], tn1[:, 16:32], AF.Exp, scale=cmh),
            reads=[tn1b, pb.cbuf], writes=[hsc["rnq"][1]], tag="nm_eq")
        bsl = beta_t[:, h * 16:(h + 1) * 16]
        S.do("scalar", lambda e, bsl=bsl, tn1=tn1: e.activation(
            tn1[:, 0:16], bsl, AF.Ln, bias=c0),
            reads=[beta_b, pb.cbuf], writes=[tn1b], tag="bd_ln")
        S.do("scalar", lambda e, tn1=tn1, hsc=hsc: e.activation(
            hsc["sqb"][0][:], tn1[:, 0:16], AF.Exp, scale=ch_),
            reads=[tn1b, pb.cbuf], writes=[hsc["sqb"][1]], tag="bd_sqb")
        S.do("scalar", lambda e, tn1=tn1, hsc=hsc: e.activation(
            hsc["isb"][0][:], tn1[:, 0:16], AF.Exp, scale=cmh),
            reads=[tn1b, pb.cbuf], writes=[hsc["isb"][1]], tag="bd_isb")
        S.do("vector", lambda e, hsc=hsc: e.tensor_scalar(
            hsc["bm12n"][0][:], hsc["sqb"][0][:], cm1, None, ALU.mult),
            reads=[hsc["sqb"][1], pb.cbuf], writes=[hsc["bm12n"][1]],
            tag="bd_neg")
        S.do("vector", lambda e, hsc=hsc: e.tensor_tensor(
            hsc["sk"][0][:], hsc["rnk"][0][:], hsc["sqb"][0][:], ALU.mult),
            reads=[hsc["rnk"][1], hsc["sqb"][1]], writes=[hsc["sk"][1]],
            tag="bd_sk")
        S.do("vector", lambda e, hsc=hsc: e.tensor_scalar(
            hsc["alpha"][0][:], hsc["rnq"][0][:], calpha, None, ALU.mult),
            reads=[hsc["rnq"][1], pb.cbuf], writes=[hsc["alpha"][1]],
            tag="bd_al")
        S.do("vector", lambda e, sc2=sc2, hsc=hsc: e.tensor_copy(
            sc2[:, 0:16], hsc["bm12n"][0][:]),
            reads=[hsc["bm12n"][1]], writes=[sc2b], tag="sp_s1")
        S.do("vector", lambda e, sc2=sc2, hsc=hsc: e.tensor_copy(
            sc2[:, 16:32], hsc["alpha"][0][:]),
            reads=[hsc["alpha"][1]], accwrites=[sc2b], tag="sp_s2")
        stsc = S.do("sync", lambda e, h=h, sc2=sc2: e.dma_start(
            out=scal.ap()[h], in_=sc2[:]),
            reads=[sc2b], accwrites=[scb], is_dma=True, tag="sp_sc")
        S.mark_output(stsc)
        yield
        for c in range(NCH):
            sl = slice(c * 128, (c + 1) * 128)
            S.do("scalar", lambda e, kkt=kkt, c=c, sl=sl, hsc=hsc:
                 e.activation(kkt[:, sl], kkt[:, sl], AF.Copy,
                              scale=hsc["sk"][0][:, c:c + 1]),
                 reads=[hsc["sk"][1], kkb], accwrites=[kkb], tag="sc_kk")
            S.do("scalar", lambda e, vpt=vpt, c=c, sl=sl, hsc=hsc:
                 e.activation(vpt[:, sl], vpt[:, sl], AF.Copy,
                              scale=hsc["sqb"][0][:, c:c + 1]),
                 reads=[hsc["sqb"][1], vpb], accwrites=[vpb], tag="sc_vp")
            if c % 4 == 3:
                yield
        yield

    def chain(hi, h):
        L = P[hi % PLN]
        C_ = CL[hi % CLN]
        hsc = L["hsc"]
        qc_t, qc_b = L["cvo"][0]
        kc_t, kc_b = L["cvo"][1]
        kkt, kkb = L["kk"]
        vpt, vpb = L["vp"]
        gg, ggt = C_["gg"], C_["ggt"]
        yp, ypb = C_["yp"]
        alow, alowb = C_["alow"]
        mup, mupb = yp, ypb
        mlow, mlowb = gg[1]
        upair, upairb = C_["upair"]
        stsb, stsbb = C_["stsb"]
        stsf, stsfb = C_["stsf"]
        pX, pXb = C_["pX"]
        pY, pYb = C_["pY"]
        pZ, pZb = C_["pZ"]
        araw4, araw4b = C_["araw4"]
        diag4, diag4b = C_["diag4"]
        a4, a4b = C_["a4"]

        def chain_step(c):
            pkt, pkb = C_["pk"][c % PKD]
            rhs_s, rhs_sb = (s01, s01b) if c == 0 else (stsb, stsbb)
            S.do("tensor", lambda e, pkt=pkt, rhs_s=rhs_s, pZ=pZ: e.matmul(
                pZ[:, 256:512], pkt[:, 128:256], rhs_s[:], start=True,
                stop=True),
                reads=[pkb, rhs_sb], writes=[pZb], tag="ch_ws")
            S.do("vector", lambda e, pkt=pkt, c=c, hsc=hsc, upair=upair,
                 pZ=pZ: e.scalar_tensor_tensor(
                     upair[:, 0:128], pZ[:, 256:384],
                     hsc["bm12n"][0][:, c:c + 1],
                     pkt[:, 256:384], ALU.mult, ALU.add),
                 reads=[pZb, hsc["bm12n"][1], pkb], writes=[upairb],
                 tag="ch_up0")
            S.do("vector", lambda e, c=c, hsc=hsc, upair=upair, pZ=pZ:
                 e.tensor_scalar(
                     upair[:, 128:256], pZ[:, 384:512],
                     hsc["bm12n"][0][:, c:c + 1], None, ALU.mult),
                 reads=[pZb, hsc["bm12n"][1]], accwrites=[upairb],
                 tag="ch_up1")
            S.do("tensor", lambda e, pkt=pkt, upair=upair, pZ=pZ: e.matmul(
                pZ[:, 0:256], pkt[:, 0:128], upair[:], start=True,
                stop=True),
                reads=[pkb, upairb], writes=[pZb], tag="ch_st")
            acc_in, acc_inb = (s01f, s01fb) if c == 0 else (stsf, stsfb)
            S.do("vector", lambda e, acc_in=acc_in, stsf=stsf, pZ=pZ:
                 e.tensor_tensor(stsf[:], acc_in[:], pZ[:, 0:256], ALU.add),
                 reads=[acc_inb, pZb], writes=[stsfb], tag="ch_acc")
            S.do("scalar", lambda e, stsb=stsb, stsf=stsf: e.copy(
                stsb[:], stsf[:]),
                reads=[stsfb], writes=[stsbb], tag="ch_ev")

        pend = []
        for g in range(4):
            cs = [4 * g + j for j in range(4)]
            for j, c in enumerate(cs):
                col = HALO + c * 128
                jj = slice(j * 128, (j + 1) * 128)
                S.do("tensor", lambda e, kc_t=kc_t, col=col, jj=jj, pX=pX:
                     e.matmul(pX[:, jj], kc_t[:, col:col + 128],
                              kc_t[:, col:col + 128], start=True, stop=True),
                     reads=[kc_b], writes=[pXb], tag="A_mm")
                S.do("scalar", lambda e, c=c, jj=jj, hsc=hsc, diag4=diag4:
                     e.activation(diag4[:, jj], cb["IDENTB"][0][:], AF.Copy,
                                  scale=hsc["sk"][0][:, c:c + 1]),
                     reads=[cb["IDENTB"][1], hsc["sk"][1]],
                     accwrites=[diag4b], tag="dg_b")
            S.do("vector", lambda e, araw4=araw4, pX=pX: e.tensor_copy(
                araw4[:], pX[:]),
                reads=[pXb], writes=[araw4b], tag="A_ev")
            for j, c in enumerate(cs):
                jj = slice(j * 128, (j + 1) * 128)
                S.do("tensor", lambda e, araw4=araw4, diag4=diag4, jj=jj,
                     pY=pY: e.matmul(pY[:, jj], araw4[:, jj], diag4[:, jj],
                                     start=True, stop=True),
                     reads=[araw4b, diag4b], writes=[pYb], tag="AD_mm")
                S.do("scalar", lambda e, c=c, jj=jj, hsc=hsc, a4=a4, pY=pY:
                     e.activation(a4[:, jj], pY[:, jj], AF.Copy,
                                  scale=hsc["sk"][0][:, c:c + 1]),
                     reads=[pYb, hsc["sk"][1]], accwrites=[a4b], tag="A_sc")
            S.do("vector", lambda e, mup=mup, a4=a4: e.tensor_tensor(
                mup[:], a4[:], wide["TRIUS"][0][:], ALU.mult),
                reads=[a4b, wide["TRIUS"][1]], writes=[mupb], tag="nb_up")
            S.do("vector", lambda e, mlow=mlow, a4=a4: e.tensor_tensor(
                mlow[:], a4[:], wide["TRILS"][0][:], ALU.mult),
                reads=[a4b, wide["TRILS"][1]], writes=[mlowb], tag="nb_lo")
            g0t, g0tb = gg[0]
            gt0t, gt0tb = ggt[0]
            S.do("vector", lambda e, g0t=g0t, mup=mup: e.tensor_tensor(
                g0t[:], wide["IDENTB"][0][:], mup[:], ALU.subtract),
                reads=[mupb, wide["IDENTB"][1]], writes=[g0tb], tag="nb_g0")
            S.do("vector", lambda e, gt0t=gt0t, mlow=mlow: e.tensor_tensor(
                gt0t[:], wide["IDENTB"][0][:], mlow[:], ALU.subtract),
                reads=[mlowb, wide["IDENTB"][1]], writes=[gt0tb], tag="nb_gt0")
            S.do("vector", lambda e, alow=alow, mlow=mlow: e.tensor_tensor(
                alow[:], wide["IDENTB"][0][:], mlow[:], ALU.add),
                reads=[mlowb, wide["IDENTB"][1]], writes=[alowb], tag="nb_al")
            yield
            for it in range(6):
                gsrc, gsrcb = gg[it % 2] if it > 0 else gg[0]
                gtsrc, gtsrcb = ggt[it % 2] if it > 0 else ggt[0]
                gdst, gdstb = gg[(it + 1) % 2]
                gtdst, gtdstb = ggt[(it + 1) % 2]
                for j in range(4):
                    jj = slice(j * 128, (j + 1) * 128)
                    S.do("tensor", lambda e, gsrc=gsrc, alow=alow, jj=jj,
                         pX=pX: e.matmul(pX[:, jj], alow[:, jj], gsrc[:, jj],
                                         start=True, stop=True),
                         reads=[alowb, gsrcb], writes=[pXb], tag="nw_a")
                S.do("vector", lambda e, yp=yp, pX=pX: e.tensor_tensor(
                    yp[:], twoI4[:], pX[:], ALU.subtract),
                    reads=[pXb, twoI4b], writes=[ypb], tag="nw_y")
                for j in range(4):
                    jj = slice(j * 128, (j + 1) * 128)
                    S.do("tensor", lambda e, gtsrc=gtsrc, yp=yp, jj=jj,
                         pY=pY: e.matmul(pY[:, jj], gtsrc[:, jj], yp[:, jj],
                                         start=True, stop=True),
                         reads=[gtsrcb, ypb], writes=[pYb], tag="nw_b")
                S.do("scalar", lambda e, gdst=gdst, pY=pY: e.copy(
                    gdst[:], pY[:]),
                    reads=[pYb], writes=[gdstb], tag="nw_be")
                if it < 5:
                    for j in range(4):
                        jj = slice(j * 128, (j + 1) * 128)
                        S.do("tensor", lambda e, gtsrc=gtsrc, yp=yp, jj=jj,
                             pX=pX: e.matmul(pX[:, jj], yp[:, jj],
                                             gtsrc[:, jj],
                                             start=True, stop=True),
                             reads=[ypb, gtsrcb], writes=[pXb], tag="nw_c")
                    S.do("vector", lambda e, gtdst=gtdst, pX=pX:
                         e.tensor_copy(gtdst[:], pX[:]),
                         reads=[pXb], writes=[gtdstb], tag="nw_ce")
                if 1 <= it <= 4 and pend:
                    chain_step(pend.pop(0))
                yield
            gfin, gfinb = gg[0]
            for j, c in enumerate(cs):
                sl = slice(c * 128, (c + 1) * 128)
                jj = slice(j * 128, (j + 1) * 128)
                pkt, pkb = C_["pk"][c % PKD]
                S.do("vector", lambda e, pkt=pkt, kkt=kkt, c=c, sl=sl,
                     hsc=hsc: e.tensor_scalar(
                         pkt[:, 0:128], kkt[:, sl],
                         hsc["isb"][0][:, c:c + 1], None, ALU.mult),
                     reads=[kkb, hsc["isb"][1]], writes=[pkb], tag="pk_kn")
                S.do("tensor", lambda e, gfin=gfin, vpt=vpt, sl=sl, jj=jj,
                     pX=pX: e.matmul(pX[:, jj], gfin[:, jj], vpt[:, sl],
                                     start=True, stop=True),
                     reads=[gfinb, vpb], writes=[pXb], tag="u0_mm")
                S.do("tensor", lambda e, kkt=kkt, gfin=gfin, sl=sl, jj=jj,
                     pY=pY: e.matmul(pY[:, jj], kkt[:, sl], gfin[:, jj],
                                     start=True, stop=True),
                     reads=[kkb, gfinb], writes=[pYb], tag="wd_mm")
            yield
            for j, c in enumerate(cs):
                jj = slice(j * 128, (j + 1) * 128)
                pkt, pkb = C_["pk"][c % PKD]
                S.do("scalar", lambda e, pkt=pkt, c=c, hsc=hsc, jj=jj, pX=pX:
                     e.activation(pkt[:, 256:384], pX[:, jj], AF.Copy,
                                  scale=hsc["sqb"][0][:, c:c + 1]),
                     reads=[pXb, hsc["sqb"][1]], accwrites=[pkb], tag="u0_ev")
                S.do("scalar", lambda e, pkt=pkt, jj=jj, pY=pY: e.copy(
                    pkt[:, 128:256], pY[:, jj]),
                    reads=[pYb], accwrites=[pkb], tag="wd_ev")
            for j, c in enumerate(cs):
                col = HALO + c * 128
                jj = slice(j * 128, (j + 1) * 128)
                S.do("tensor", lambda e, kc_t=kc_t, qc_t=qc_t, col=col,
                     jj=jj, pX=pX: e.matmul(
                         pX[:, jj], kc_t[:, col:col + 128],
                         qc_t[:, col:col + 128], start=True, stop=True),
                     reads=[kc_b, qc_b], writes=[pXb], tag="lq_mm")
            yield
            for j, c in enumerate(cs):
                jj = slice(j * 128, (j + 1) * 128)
                pkt, pkb = C_["pk"][c % PKD]
                S.do("vector", lambda e, pkt=pkt, c=c, hsc=hsc, jj=jj,
                     pX=pX: e.scalar_tensor_tensor(
                         pkt[:, 384:512], pX[:, jj],
                         hsc["rnk"][0][:, c:c + 1],
                         cb["TRIUI"][0][:], ALU.mult, ALU.mult),
                     reads=[pXb, hsc["rnk"][1], cb["TRIUI"][1]],
                     accwrites=[pkb], tag="lq_ev")
                stp = S.do("sync", lambda e, pkt=pkt, h=h, c=c: e.dma_start(
                    out=spill.ap()[h, c], in_=pkt[:]),
                    reads=[pkb], accwrites=[spb], is_dma=True, tag="pk_sp")
                S.mark_output(stp)
            pend.extend(cs)
            yield
        while pend:
            chain_step(pend.pop(0))
            yield
        bf_t, bf_b = C_["abf"][0]
        af_t, af_b = C_["abf"][1]
        S.do("vector", lambda e, bf_t=bf_t, stsf=stsf: e.tensor_copy(
            bf_t[:], stsf[:, 0:128]),
            reads=[stsfb], writes=[bf_b], tag="ab_b")
        S.do("vector", lambda e, af_t=af_t, stsf=stsf: e.tensor_copy(
            af_t[:], stsf[:, 128:256]),
            reads=[stsfb], writes=[af_b], tag="ab_a")
        S.do("tensor", lambda e, af_t=af_t, pX=pX: e.transpose(
            pX[:, 0:128], af_t[:], cb["IDENTF"][0][:]),
            reads=[af_b, cb["IDENTF"][1]], writes=[pXb], tag="ab_tp")
        S.do("vector", lambda e, af_t=af_t, pX=pX: e.tensor_copy(
            af_t[:], pX[:, 0:128]),
            reads=[pXb], writes=[af_b], tag="ab_te")
        sta = S.do("sync", lambda e, af_t=af_t, h=h: e.dma_start(
            out=abg.ap()[h * 128:(h + 1) * 128, 0:128], in_=af_t[:]),
            reads=[af_b], accwrites=[abb], is_dma=True, tag="ab_oa")
        stb2 = S.do("sync", lambda e, bf_t=bf_t, h=h: e.dma_start(
            out=abg.ap()[h * 128:(h + 1) * 128, 128:256], in_=bf_t[:]),
            reads=[bf_b], accwrites=[abb], is_dma=True, tag="ab_ob")
        S.mark_output(sta)
        S.mark_output(stb2)
        yield

    n = len(heads)
    preps = [prep(i, heads[i]) for i in range(n)]
    chains = [chain(i, heads[i]) for i in range(n)]
    for _ in preps[0]:
        pass
    for h in range(n):
        finish = chains[h - 1] if h >= 1 else None
        cur = chains[h]
        nxt = preps[h + 1] if h + 1 < n else None
        alive = {0: finish is not None, 1: True, 2: nxt is not None}
        gens = {0: finish, 1: cur, 2: nxt}
        while alive[0] or alive[2]:
            for j in (0, 1, 2):
                if alive[j]:
                    try:
                        next(gens[j])
                    except StopIteration:
                        alive[j] = False
    for _ in chains[n - 1]:
        pass
    pb.pop_scope()




def p_moe(pb):
    S = pb.S
    xnd = pb.state["XN_D"]
    xin = pb.io("XIN", [HTOK, D], F32, "in")
    xinb = pb.dbuf("XIN")
    gw = pb.io("L0_ffn_gate_w", [D, NE], F32, "in")
    gb_ = pb.io("L0_ffn_gate_b", [1, NE], F32, "in")
    ew = pb.io("L0_ffn_expert_w", [NE, D, D], F32, "in")
    eb_ = pb.io("L0_ffn_expert_b", [1, NE * D], F32, "in")
    x1p = pb.io("X1P", [TOK, D], F32, "out")
    x1pb = pb.dbuf("X1P")
    onesr = pb.io("ONESR", [1, 128], BF16, "in")

    pb.push_scope()
    gwt, gwtb = pb.sb("gw", [128, 16 * NE], BF16)
    S.do("gpsimd", lambda e: e.dma_start(
        out=gwt[:].rearrange("p (k h) -> p k h", k=16),
        in_=gw.ap().rearrange("(k p) h -> p k h", p=128)),
        reads=[pb.dbuf("L0_ffn_gate_w")], writes=[gwtb], is_dma=True,
        tag="mg_w")
    gbt, gbtb = pb.sb("gb", [1, NE], BF16)
    S.do("gpsimd", lambda e: e.dma_start(out=gbt[:], in_=gb_.ap()),
         reads=[pb.dbuf("L0_ffn_gate_b")], writes=[gbtb], is_dma=True,
         tag="mg_b")
    ebt, ebtb = pb.sb("eb", [1, NE * D], BF16)
    S.do("gpsimd", lambda e: e.dma_start(out=ebt[:], in_=eb_.ap()),
         reads=[pb.dbuf("L0_ffn_expert_b")], writes=[ebtb], is_dma=True,
         tag="mg_eb")
    onr, onrb = pb.sb("onr", [1, 128], BF16)
    S.do("sync", lambda e: e.dma_start(out=onr[:], in_=onesr.ap()),
         reads=[pb.dbuf("ONESR")], writes=[onrb], is_dma=True, tag="mg_on")
    pg, pgb = pb.ps("pg", [128, NE])
    wsel = [pb.sb(f"ws{tt}", [128, NE], F32) for tt in range(NCH)]
    gt1, gt1b = pb.sb("gt1", [128, NE], F32)
    gt2, gt2b = pb.sb("gt2", [128, NE], F32)
    gs, gsb = pb.sb("gs", [128, 4], F32)
    cm1 = pb.cv(-1.0)
    c1 = pb.cv(1.0)
    for tt in range(NCH):
        for k in range(16):
            S.do("tensor", lambda e, k=k, tt=tt: e.matmul(
                pg[:, 0:NE],
                xnd[k][0][:, HALO + tt * 128: HALO + (tt + 1) * 128],
                gwt[:, k * NE:(k + 1) * NE],
                start=(k == 0), stop=False),
                reads=[xnd[k][1], gwtb], writes=[pgb], tag="mg_mm")
        S.do("tensor", lambda e: e.matmul(
            pg[:, 0:NE], onr[:], gbt[:], start=False, stop=True),
            reads=[onrb, gbtb], writes=[pgb], tag="mg_bias")
        S.do("vector", lambda e: e.reduce_max(gs[:, 0:1], pg[:, 0:NE],
                                              axis=AX.X),
             reads=[pgb], writes=[gsb], tag="mg_mx")
        S.do("vector", lambda e: e.tensor_scalar(gs[:, 1:2], gs[:, 0:1], cm1,
                                                 None, ALU.mult),
             reads=[gsb, pb.cbuf], accwrites=[gsb], tag="mg_nmx")
        S.do("scalar", lambda e: e.activation(gt1[:], pg[:, 0:NE], AF.Exp,
                                              bias=gs[:, 1:2]),
             reads=[pgb, gsb], writes=[gt1b], tag="mg_exp")
        S.do("vector", lambda e: e.reduce_sum(gs[:, 2:3], gt1[:], axis=AX.X),
             reads=[gt1b], writes=[gsb], tag="mg_sm")
        S.do("vector", lambda e: e.reciprocal(gs[:, 3:4], gs[:, 2:3]),
             reads=[gsb], accwrites=[gsb], tag="mg_rc")
        S.do("vector", lambda e: e.tensor_scalar(gt1[:], gt1[:], gs[:, 3:4],
                                                 None, ALU.mult),
             reads=[gt1b, gsb], writes=[gt1b], tag="mg_pr")
        S.do("vector", lambda e: e.reduce_max(gs[:, 0:1], gt1[:], axis=AX.X),
             reads=[gt1b], writes=[gsb], tag="mt_m1")
        S.do("vector", lambda e: e.tensor_scalar(gt2[:], gt1[:], gs[:, 0:1],
                                                 None, ALU.is_ge),
             reads=[gt1b, gsb], writes=[gt2b], tag="mt_ge1")
        S.do("vector", lambda e: e.tensor_scalar(gt2[:], gt2[:], cm1, c1,
                                                 ALU.mult, ALU.add),
             reads=[gt2b, pb.cbuf], writes=[gt2b], tag="mt_inv")
        S.do("vector", lambda e: e.tensor_tensor(gt2[:], gt1[:], gt2[:],
                                                 ALU.mult),
             reads=[gt1b, gt2b], writes=[gt2b], tag="mt_pm")
        S.do("vector", lambda e: e.reduce_max(gs[:, 1:2], gt2[:], axis=AX.X),
             reads=[gt2b], writes=[gsb], tag="mt_m2")
        S.do("vector", lambda e: e.tensor_scalar(gt2[:], gt1[:], gs[:, 1:2],
                                                 None, ALU.is_ge),
             reads=[gt1b, gsb], writes=[gt2b], tag="mt_ge2")
        S.do("vector", lambda e, tt=tt: e.tensor_tensor(
            wsel[tt][0][:], gt1[:], gt2[:], ALU.mult),
            reads=[gt1b, gt2b], writes=[wsel[tt][1]], tag="mt_w")
    ewt = [pb.sb("ew", [128, 16 * 512], BF16) for _ in range(2)]
    acc = [pb.sb(f"acc{tt}", [128, 512], F32) for tt in range(NCH)]
    xt = [pb.sb("xt", [128, 512], F32) for _ in range(2)]
    pe_ = [pb.ps("pe", [128, 512]) for _ in range(2)]
    for oq in range(4):
        oc0 = oq * 512
        for ei in range(NE):
            wt, wtb = ewt[ei % 2]
            S.do("gpsimd", lambda e, wt=wt, ei=ei, oc0=oc0: e.dma_start(
                out=wt[:].rearrange("p (k o) -> p k o", k=16),
                in_=ew.ap()[ei, :, oc0:oc0 + 512].rearrange(
                    "(k p) o -> p k o", p=128)),
                reads=[pb.dbuf("L0_ffn_expert_w")], writes=[wtb], is_dma=True,
                tag="me_w")
            for tt in range(NCH):
                pp, ppb = pe_[tt % 2]
                for k in range(16):
                    S.do("tensor", lambda e, pp=pp, wt=wt, k=k, tt=tt:
                         e.matmul(pp[:],
                                  xnd[k][0][:, HALO + tt * 128:
                                            HALO + (tt + 1) * 128],
                                  wt[:, k * 512:(k + 1) * 512],
                                  start=(k == 0), stop=False),
                         reads=[xnd[k][1], wtb], writes=[ppb], tag="me_mm")
                S.do("tensor", lambda e, pp=pp, ei=ei, oc0=oc0: e.matmul(
                    pp[:], onr[:], ebt[:, ei * D + oc0:ei * D + oc0 + 512],
                    start=False, stop=True),
                    reads=[onrb, ebtb], writes=[ppb], tag="me_b")
                at, ab = acc[tt]
                if ei == 0:
                    xtt, xttb = xt[tt % 2]
                    S.do("sync", lambda e, xtt=xtt, tt=tt, oc0=oc0:
                         e.dma_start(out=xtt[:], in_=xin.ap()[
                             HALO + tt * 128:HALO + (tt + 1) * 128,
                             oc0:oc0 + 512]),
                         reads=[xinb], writes=[xttb], is_dma=True, tag="me_x")
                    S.do("vector", lambda e, at=at, pp=pp, xtt=xtt, tt=tt, ei=ei:
                         e.scalar_tensor_tensor(
                             at[:], pp[:], wsel[tt][0][:, ei:ei + 1], xtt[:],
                             ALU.mult, ALU.add),
                         reads=[ppb, wsel[tt][1], xttb], writes=[ab],
                         tag="me_a0")
                else:
                    S.do("vector", lambda e, at=at, pp=pp, tt=tt, ei=ei:
                         e.scalar_tensor_tensor(
                             at[:], pp[:], wsel[tt][0][:, ei:ei + 1], at[:],
                             ALU.mult, ALU.add),
                         reads=[ppb, wsel[tt][1], ab], writes=[ab],
                         tag="me_acc")
        for tt in range(NCH):
            at, ab = acc[tt]
            sto = S.do("sync", lambda e, at=at, tt=tt, oc0=oc0: e.dma_start(
                out=x1p.ap()[tt * 128:(tt + 1) * 128, oc0:oc0 + 512],
                in_=at[:]),
                reads=[ab], accwrites=[x1pb], is_dma=True, tag="me_o")
            S.mark_output(sto)
    pb.pop_scope()



def p_gproj(pb):
    S = pb.S
    xnd = pb.state["XN_D"]
    wg = pb.io("L0_attn_Wg", [D, D], F32, "in")
    gtd = pb.io("GT", [NCH, 128, D], BF16, "out")
    gtdb = pb.dbuf("GT")
    pb.push_scope()
    gt = []
    for tt in range(NCH):
        t, b = pb.sb(f"gt{tt}", [128, D], BF16)
        gt.append((t, b))
    wgt2 = [pb.sb("wg2", [128, 16 * 512], BF16) for _ in range(2)]
    pp_ = [pb.ps("pgj", [128, 512]) for _ in range(2)]
    for oc in range(4):
        wt, wtb = wgt2[oc % 2]
        S.do("gpsimd", lambda e, wt=wt, oc=oc: e.dma_start(
            out=wt[:].rearrange("p (k o) -> p k o", k=16),
            in_=wg.ap()[:, oc * 512:(oc + 1) * 512].rearrange(
                "(k p) o -> p k o", p=128)),
            reads=[pb.dbuf("L0_attn_Wg")], writes=[wtb], is_dma=True,
            tag="gp_w2")
        for tt in range(NCH):
            pp, ppb = pp_[tt % 2]
            for k in range(16):
                S.do("tensor", lambda e, pp=pp, wt=wt, k=k, tt=tt: e.matmul(
                    pp[:],
                    xnd[k][0][:, HALO + tt * 128: HALO + (tt + 1) * 128],
                    wt[:, k * 512:(k + 1) * 512],
                    start=(k == 0), stop=(k == 15)),
                    reads=[xnd[k][1], wtb], writes=[ppb], tag="gp_mm")
            t, b = gt[tt]
            S.do("scalar", lambda e, t=t, pp=pp, oc=oc: e.copy(
                t[:, oc * 512:(oc + 1) * 512], pp[:]),
                reads=[ppb], accwrites=[b], tag="gp_ev")
    for tt in range(NCH):
        t, b = gt[tt]
        sg = S.do("sync", lambda e, t=t, tt=tt: e.dma_start(
            out=gtd.ap()[tt], in_=t[:]),
            reads=[b], accwrites=[gtdb], is_dma=True, tag="gp_out")
        S.mark_output(sg)
    pb.pop_scope()




def p_prefix(pb, heads=range(NH0), dump=False):
    S = pb.S
    aba = pb.io("AB_ALL", [4, NH0 * 128, 256], F32, "in")
    abab = pb.dbuf("AB_ALL")
    sel4 = pb.io("SEL4", [128, 4], F32, "in")
    sel4b_ = pb.dbuf("SEL4")
    sins = []
    for h in range(NH0):
        t, b = pb.sb(f"sin{h}", [128, 128], BF16)
        sins.append((t, b))
    pb.state["SIN"] = sins
    pb.push_scope()
    selt, selb = pb.sb("sel", [128, 4], F32)
    S.do("sync", lambda e: e.dma_start(out=selt[:], in_=sel4.ap()),
         reads=[sel4b_], writes=[selb], is_dma=True, tag="px_sel")
    ab = [pb.sb(f"ab{p}", [128, 256], F32) for p in range(3)]
    tf = [pb.sb(f"tf{q}", [128, 128], F32) for q in range(3)]
    sf, sfb = pb.sb("sf", [128, 128], F32)
    pp, ppb = pb.ps("ppx", [128, 128])
    for h in heads:
        for p in range(3):
            t, b = ab[p]
            S.do("sync", lambda e, t=t, p=p, h=h: e.dma_start(
                out=t[:], in_=aba.ap()[p, h * 128:(h + 1) * 128, :]),
                reads=[abab], writes=[b], is_dma=True, tag="px_ld")
        S.do("vector", lambda e: e.tensor_copy(tf[0][0][:], ab[0][0][:, 128:256]),
             reads=[ab[0][1]], writes=[tf[0][1]], tag="px_t1")
        for q in (1, 2):
            S.do("tensor", lambda e, q=q: e.matmul(
                pp[:, 0:128], ab[q][0][:, 0:128], tf[q - 1][0][:], start=True,
                stop=True),
                reads=[ab[q][1], tf[q - 1][1]], writes=[ppb], tag="px_mm")
            S.do("vector", lambda e, q=q: e.tensor_tensor(
                tf[q][0][:], ab[q][0][:, 128:256], pp[:, 0:128], ALU.add),
                reads=[ab[q][1], ppb], writes=[tf[q][1]], tag="px_t")
        S.do("vector", lambda e: e.tensor_scalar(
            sf[:], tf[0][0][:], selt[:, 1:2], None, ALU.mult),
            reads=[tf[0][1], selb], writes=[sfb], tag="px_s1")
        for q in (1, 2):
            S.do("vector", lambda e, q=q: e.scalar_tensor_tensor(
                sf[:], tf[q][0][:], selt[:, q + 1:q + 2], sf[:],
                ALU.mult, ALU.add),
                reads=[tf[q][1], selb, sfb], writes=[sfb], tag=f"px_s{q+1}")
        t, b = sins[h]
        S.do("vector", lambda e, t=t: e.tensor_copy(t[:], sf[:]),
             reads=[sfb], writes=[b], tag="px_out")
        if dump:
            sd = pb.io("SIN_DBG", [NH0, 128, 128], F32, "out")
            std = S.do("sync", lambda e, h=h: e.dma_start(
                out=sd.ap()[h], in_=sf[:]),
                reads=[sfb], accwrites=[pb.dbuf("SIN_DBG")], is_dma=True,
                tag="px_dbg")
            S.mark_output(std)
    pb.pop_scope()



def drive(gens):
    gens = [g for g in gens if g is not None]
    while gens:
        nxt = []
        for g in gens:
            try:
                next(g)
                nxt.append(g)
            except StopIteration:
                pass
        gens = nxt


def p_delta2(pb, heads=range(NH0)):
    S = pb.S
    spill = pb.io("SPILL", [NH0, NCH, 128, 512], BF16, "in")
    spb = pb.dbuf("SPILL")
    qsp = pb.io("QSP", [NH0, 128, TOK], BF16, "in")
    qspb = pb.dbuf("QSP")
    scal = pb.io("SCAL", [NH0, 128, 32], F32, "in")
    scb = pb.dbuf("SCAL")
    ogd = pb.io("OG_D", [NH0, 128, TOK], BF16, "out")
    ogdb = pb.dbuf("OG_D")
    sins = pb.state["SIN"]
    gtd = pb.io("GT", [NCH, 128, D], BF16, "in")
    gtdb = pb.dbuf("GT")

    pb.push_scope()
    gt = []
    for tt in range(NCH):
        t, b = pb.sb(f"gtl{tt}", [128, D], BF16)
        S.do("sync", lambda e, t=t, tt=tt: e.dma_start(
            out=t[:], in_=gtd.ap()[tt]),
            reads=[gtdb], writes=[b], is_dma=True, tag="gt_ld")
        gt.append((t, b))
    LN = 2
    lanes = []
    for _ in range(LN):
        ln = dict(
            pk=[pb.sb("pk2", [128, 512], BF16) for _ in range(2)],
            qc2=pb.sb("qc2", [128, TOK], BF16),
            sc=pb.sb("sc", [128, 32], F32),
            ssf=pb.sb("ssf", [128, 128], F32),
            ssb=pb.sb("ssb", [128, 128], BF16),
            ut=pb.sb("ut", [128, 128], BF16),
            oall=pb.sb("oall", [128, TOK], BF16),
            nrm=pb.sb("nrm", [128, 16], F32),
            sqs=pb.sb("sqs", [128, 128], F32),
            fsc=pb.sb("fsc", [128, 16], F32),
            e1=pb.sb("e1", [128, 128], F32),
            e2=pb.sb("e2", [128, 128], F32),
            ogt=pb.sb("ogt", [128, 128], F32),
            ogdt=pb.sb("ogdt", [128, TOK], BF16),
            pW=pb.ps("pW2", [128, 128]),
            pS=pb.ps("pS2", [128, 128]),
            pO=pb.ps("pO2", [128, 128]),
        )
        lanes.append(ln)
    cm1 = pb.cv(-1.0)
    c1 = pb.cv(1.0)
    ceps = pb.cv(1e-5)
    cmh = pb.cv(-0.5)
    ci128 = pb.cv(1.0 / 128.0)

    def run_head(hi, h):
        ln = lanes[hi % LN]
        qc2, qc2b = ln["qc2"]
        sc, scb2 = ln["sc"]
        ssf, ssfb = ln["ssf"]
        ssb, ssbb = ln["ssb"]
        ut, utb = ln["ut"]
        oall, oallb = ln["oall"]
        nrm, nrmb = ln["nrm"]
        sqs, sqsb = ln["sqs"]
        fsc, fscb = ln["fsc"]
        e1, e1b = ln["e1"]
        e2, e2b = ln["e2"]
        ogt, ogtb = ln["ogt"]
        ogdt, ogdtb = ln["ogdt"]
        pW, pWb = ln["pW"]
        pS, pSb = ln["pS"]
        pO, pOb = ln["pO"]
        S.do("sync", lambda e, h=h, qc2=qc2: e.dma_start(out=qc2[:],
                                                         in_=qsp.ap()[h]),
             reads=[qspb], writes=[qc2b], is_dma=True, tag="d2_q")
        S.do("sync", lambda e, h=h, sc=sc: e.dma_start(out=sc[:],
                                                       in_=scal.ap()[h]),
             reads=[scb], writes=[scb2], is_dma=True, tag="d2_sc")
        sint, sinb = sins[h]
        for c in range(NCH):
            pkt, pkb = ln["pk"][c % 2]
            S.do("sync", lambda e, pkt=pkt, h=h, c=c: e.dma_start(
                out=pkt[:], in_=spill.ap()[h, c]),
                reads=[spb], writes=[pkb], is_dma=True, tag="d2_pk")
            s0t, s0b = (sint, sinb) if c == 0 else (ssb, ssbb)
            S.do("tensor", lambda e, pkt=pkt, s0t=s0t, pW=pW: e.matmul(
                pW[:, 0:128], pkt[:, 128:256], s0t[:], start=True, stop=True),
                reads=[pkb, s0b], writes=[pWb], tag="d2_ws")
            S.do("vector", lambda e, pkt=pkt, c=c, ut=ut, pW=pW, sc=sc:
                 e.scalar_tensor_tensor(
                     ut[:], pW[:, 0:128], sc[:, c:c + 1], pkt[:, 256:384],
                     ALU.mult, ALU.add),
                 reads=[pWb, scb2, pkb], writes=[utb], tag="d2_u")
            S.do("tensor", lambda e, s0t=s0t, c=c, qc2=qc2, pO=pO: e.matmul(
                pO[:, 0:128], qc2[:, c * 128:(c + 1) * 128], s0t[:],
                start=True, stop=False),
                reads=[qc2b, s0b], writes=[pOb], tag="d2_o1")
            S.do("tensor", lambda e, pkt=pkt, ut=ut, pO=pO: e.matmul(
                pO[:, 0:128], pkt[:, 384:512], ut[:], start=False, stop=True),
                reads=[pkb, utb], writes=[pOb], tag="d2_o2")
            S.do("tensor", lambda e, pkt=pkt, ut=ut, pS=pS: e.matmul(
                pS[:, 0:128], pkt[:, 0:128], ut[:], start=True, stop=True),
                reads=[pkb, utb], writes=[pSb], tag="d2_su")
            if c == 0:
                S.do("vector", lambda e, sint=sint, ssf=ssf: e.tensor_copy(
                    ssf[:], sint[:]),
                    reads=[sinb], writes=[ssfb], tag="d2_wid")
            S.do("vector", lambda e, ssf=ssf, pS=pS: e.tensor_tensor(
                ssf[:], ssf[:], pS[:, 0:128], ALU.add),
                reads=[ssfb, pSb], writes=[ssfb], tag="d2_sa")
            S.do("scalar", lambda e, ssb=ssb, ssf=ssf: e.copy(ssb[:], ssf[:]),
                 reads=[ssfb], writes=[ssbb], tag="d2_se")
            S.do("scalar", lambda e, c=c, oall=oall, pO=pO: e.copy(
                oall[:, c * 128:(c + 1) * 128], pO[:, 0:128]),
                reads=[pOb], accwrites=[oallb], tag="d2_oe")
            S.do("vector", lambda e, c=c, oall=oall, sqs=sqs: e.tensor_tensor(
                sqs[:], oall[:, c * 128:(c + 1) * 128],
                oall[:, c * 128:(c + 1) * 128], ALU.mult),
                reads=[oallb], writes=[sqsb], tag="d2_nr")
            S.do("vector", lambda e, c=c, sqs=sqs, nrm=nrm: e.reduce_sum(
                nrm[:, c:c + 1], sqs[:], axis=AX.X),
                reads=[sqsb], accwrites=[nrmb], tag="d2_nr2")
            yield
        S.do("vector", lambda e, fsc=fsc, sc=sc: e.tensor_tensor(
            fsc[:], sc[:, 16:32], sc[:, 16:32], ALU.mult),
            reads=[scb2], writes=[fscb], tag="d2_a2")
        S.do("vector", lambda e, fsc=fsc, nrm=nrm: e.tensor_tensor(
            fsc[:], fsc[:], nrm[:], ALU.mult),
            reads=[fscb, nrmb], writes=[fscb], tag="d2_m2")
        S.do("vector", lambda e, fsc=fsc: e.tensor_scalar(
            fsc[:], fsc[:], ci128, None, ALU.mult),
            reads=[fscb, pb.cbuf], writes=[fscb], tag="d2_d128")
        S.do("scalar", lambda e, fsc=fsc: e.activation(fsc[:], fsc[:], AF.Ln,
                                                       bias=ceps),
             reads=[fscb, pb.cbuf], writes=[fscb], tag="d2_ln")
        S.do("scalar", lambda e, fsc=fsc: e.activation(fsc[:], fsc[:], AF.Exp,
                                                       scale=cmh),
             reads=[fscb, pb.cbuf], writes=[fscb], tag="d2_ex")
        S.do("vector", lambda e, fsc=fsc, sc=sc: e.tensor_tensor(
            fsc[:], fsc[:], sc[:, 16:32], ALU.mult),
            reads=[fscb, scb2], writes=[fscb], tag="d2_fa")
        for c in range(NCH):
            gtt, gtb = gt[c]
            S.do("scalar", lambda e, gtt=gtt, h=h, e1=e1: e.activation(
                e1[:], gtt[:, h * 128:(h + 1) * 128], AF.Exp, scale=cm1),
                reads=[gtb, pb.cbuf], writes=[e1b], tag="d2_ge")
            S.do("scalar", lambda e, e1=e1, e2=e2: e.activation(
                e2[:], e1[:], AF.Ln, bias=c1),
                reads=[e1b, pb.cbuf], writes=[e2b], tag="d2_ga")
            S.do("scalar", lambda e, e1=e1, e2=e2: e.activation(
                e1[:], e2[:], AF.Exp, scale=cm1),
                reads=[e2b, pb.cbuf], writes=[e1b], tag="d2_gr")
            S.do("vector", lambda e, c=c, ogt=ogt, oall=oall, fsc=fsc, e1=e1:
                 e.scalar_tensor_tensor(
                     ogt[:], oall[:, c * 128:(c + 1) * 128], fsc[:, c:c + 1],
                     e1[:], ALU.mult, ALU.mult),
                 reads=[oallb, fscb, e1b], writes=[ogtb], tag="d2_og")
            slot = (hi % 2) * 2 + (c % 2)
            S.do("tensor", lambda e, ogt=ogt, slot=slot: e.transpose(
                pTT[:, slot * 128:(slot + 1) * 128], ogt[:], idb_t[:]),
                reads=[ogtb, idb_b], writes=[pTTs[slot]], tag="d2_tpm")
            S.do("scalar", lambda e, c=c, ogdt=ogdt, slot=slot: e.copy(
                ogdt[:, c * 128:(c + 1) * 128],
                pTT[:, slot * 128:(slot + 1) * 128]),
                reads=[pTTs[slot]], accwrites=[ogdtb], tag="d2_tpe")
            if c % 4 == 3:
                yield
        sto = S.do("sync", lambda e, h=h, ogdt=ogdt: e.dma_start(
            out=ogd.ap()[h], in_=ogdt[:]),
            reads=[ogdtb], accwrites=[ogdb], is_dma=True, tag="d2_out")
        S.mark_output(sto)
        yield

    pTT, _pTTb = pb.ps("pTT", [128, 512])
    pTTs = [Buf(f"pTT{j}") for j in range(4)]
    idb_t, idb_b = pb.sb("idfd", [128, 128], F32)
    pb.io("IDENTF", [128, 128], F32, "in")
    S.do("sync", lambda e: e.dma_start(out=idb_t[:],
                                       in_=pb.dram["IDENTF"][0].ap()),
         reads=[pb.dbuf("IDENTF")], writes=[idb_b], is_dma=True, tag="d2_id")
    heads = list(heads)
    for i in range(0, len(heads), LN):
        drive([run_head(i + j, heads[i + j])
               for j in range(LN) if i + j < len(heads)])
    pb.pop_scope()




def p_wo0(pb):
    S = pb.S
    ogd = pb.io("OG_D", [NH0, 128, TOK], BF16, "in")
    ogdb = pb.dbuf("OG_D")
    wo = pb.io("L0_attn_Wo", [D, D], F32, "in")
    nw = pb.io("L0_attn_norm_w", [128, 1], F32, "in")
    x1p = pb.io("X1P", [TOK, D], F32, "in")
    x1pb = pb.dbuf("X1P")
    x1 = pb.io("X1", [TOK, D], F32, "out")
    x1b = pb.dbuf("X1")

    pb.push_scope()
    nwt, nwtb = pb.sb("nw", [128, 1], F32)
    S.do("sync", lambda e: e.dma_start(out=nwt[:], in_=nw.ap()),
         reads=[pb.dbuf("L0_attn_norm_w")], writes=[nwtb], is_dma=True,
         tag="wo_nw")
    og = []
    for k in range(NH0):
        t, b = pb.sb(f"og{k}", [128, TOK], BF16)
        S.do("sync", lambda e, t=t, k=k: e.dma_start(out=t[:],
                                                     in_=ogd.ap()[k]),
             reads=[ogdb], writes=[b], is_dma=True, tag="wo_og")
        og.append((t, b))
    wot = []
    for k in range(16):
        t, b = pb.sb(f"wo{k}", [128, D], BF16)
        S.do("gpsimd", lambda e, t=t, k=k: e.dma_start(
            out=t[:], in_=wo.ap()[k * 128:(k + 1) * 128, :]),
            reads=[pb.dbuf("L0_attn_Wo")], writes=[b], is_dma=True,
            tag="wo_w")
        S.do("vector", lambda e, t=t: e.tensor_scalar(
            t[:], t[:], nwt[:, 0:1], None, ALU.mult),
            reads=[nwtb], writes=[b], tag="wo_nws")
        wot.append((t, b))
    pp_ = [pb.ps("pwo", [128, 512]) for _ in range(2)]
    xpt = [pb.sb("xp", [128, 512], F32) for _ in range(2)]
    x1t = [pb.sb("x1t", [128, 512], F32) for _ in range(2)]
    for tt in range(NCH):
        for oc in range(4):
            i = (tt * 4 + oc) % 2
            pp, ppb = pp_[i]
            xpt_t, xptb = xpt[i]
            x1t_t, x1tb = x1t[i]
            for k in range(16):
                wt, wtb = wot[k]
                ot, ob = og[k]
                S.do("tensor", lambda e, pp=pp, ot=ot, wt=wt, k=k, tt=tt,
                     oc=oc: e.matmul(
                         pp[:], ot[:, tt * 128:(tt + 1) * 128],
                         wt[:, oc * 512:(oc + 1) * 512],
                         start=(k == 0), stop=(k == 15)),
                     reads=[ob, wtb], writes=[ppb], tag="wo_mm")
            S.do("sync", lambda e, xpt_t=xpt_t, tt=tt, oc=oc: e.dma_start(
                out=xpt_t[:], in_=x1p.ap()[tt * 128:(tt + 1) * 128,
                                           oc * 512:(oc + 1) * 512]),
                reads=[x1pb], writes=[xptb], is_dma=True, tag="wo_xp")
            S.do("vector", lambda e, x1t_t=x1t_t, pp=pp, xpt_t=xpt_t:
                 e.tensor_tensor(x1t_t[:], pp[:], xpt_t[:], ALU.add),
                 reads=[ppb, xptb], writes=[x1tb], tag="wo_add")
            sto = S.do("sync", lambda e, x1t_t=x1t_t, tt=tt, oc=oc:
                       e.dma_start(
                           out=x1.ap()[tt * 128:(tt + 1) * 128,
                                       oc * 512:(oc + 1) * 512],
                           in_=x1t_t[:]),
                       reads=[x1tb], accwrites=[x1b], is_dma=True,
                       tag="wo_x1")
            S.mark_output(sto)
    pb.pop_scope()




def p_qkv1(pb):
    S = pb.S
    xnd = pb.state["XN1_D"]
    wqT = pb.io("L1_WqT", [D, D], F32, "in")
    wkT = pb.io("L1_WkT", [D, 512], F32, "in")
    wvT = pb.io("L1_WvT", [D, 512], F32, "in")
    rcos = pb.io("RCOS", [TOK, 128], F32, "in")
    rsin = pb.io("RSIN", [TOK, 128], F32, "in")
    e127 = pb.io("E127", [128, 1], F32, "in")
    onesf = pb.io("ONESF", [1, 128], F32, "in")
    triuIb_ = pb.io("TRIUI", [128, 128], BF16, "in")
    qd1 = pb.io("QD1", [16, 128, TOK], BF16, "out")
    kd1 = pb.io("KD1", [4, 128, TOK], BF16, "out")
    vtok = pb.io("VTOK", [16, 128, 512], BF16, "out")
    kvcat = pb.io("KVCAT", [1280, 512], BF16, "out")
    vsum = pb.io("VSUM", [1, 512], F32, "out")
    pfx = pb.io("PFX", [16, 128, 512], F32, "out")

    pb.push_scope()
    tri, trib = pb.sb("tri", [128, 128], BF16)
    S.do("sync", lambda e: e.dma_start(out=tri[:], in_=triuIb_.ap()),
         reads=[pb.dbuf("TRIUI")], writes=[trib], is_dma=True, tag="q1_tri")
    e127t, e127b = pb.sb("e127", [128, 1], F32)
    S.do("sync", lambda e: e.dma_start(out=e127t[:], in_=e127.ap()),
         reads=[pb.dbuf("E127")], writes=[e127b], is_dma=True, tag="q1_e127")
    onft, onfb = pb.sb("onf", [1, 128], F32)
    S.do("sync", lambda e: e.dma_start(out=onft[:], in_=onesf.ap()),
         reads=[pb.dbuf("ONESF")], writes=[onfb], is_dma=True, tag="q1_onf")
    wsl = [pb.sb("w1", [128, 16 * 512], BF16) for _ in range(2)]
    cst_, cstb_ = pb.sb("rc", [128, 128], F32)
    snt, sntb = pb.sb("rs", [128, 128], F32)
    pe1 = [pb.ps("pq1", [128, 512]) for _ in range(2)]
    ev = [pb.sb("ev", [128, 512], F32) for _ in range(2)]
    rp, rpb = pb.sb("rp", [128, 512], F32)
    rtmp, rtmpb = pb.sb("rtmp", [128, 128], F32)
    sqv, sqvb = pb.sb("sqv", [128, 256], F32)
    lnst, lnstb = pb.sb("lnst", [128, 16], F32)
    outb_ = [pb.sb("ob", [128, 512], BF16) for _ in range(2)]
    tps, tpsb = pb.sb("tps", [128, 512], BF16)
    pfxp, pfxpb = pb.ps("ppfx", [128, 512])
    pfxt, pfxtb = pb.sb("pfxt", [128, 512], F32)
    offr, offrb = pb.sb("offr", [1, 512], F32)
    prow, prowb = pb.ps("prow", [1, 512])
    c1 = pb.cv(1.0)
    cmh = pb.cv(-0.5)
    c256i = pb.cv(1.0 / 256.0)
    ceps6 = pb.cv(1e-6)
    c256 = pb.cv(256.0)

    S.do("vector", lambda e: e.memset(offr[:], 0.0), writes=[offrb],
         tag="q1_off0")

    def ln_unit(src, srcb, dst, dstb, hslice, scale_q, tag):
        h0 = hslice
        S.do("vector", lambda e: e.reduce_sum(
            lnst[:, 0:1], src[:, h0:h0 + 256], axis=AX.X),
            reads=[srcb], writes=[lnstb], tag=tag + "_s")
        S.do("vector", lambda e: e.tensor_tensor(
            sqv[:], src[:, h0:h0 + 256], src[:, h0:h0 + 256], ALU.mult),
            reads=[srcb], writes=[sqvb], tag=tag + "_q")
        S.do("vector", lambda e: e.reduce_sum(
            lnst[:, 1:2], sqv[:], axis=AX.X),
            reads=[sqvb], writes=[lnstb], tag=tag + "_q2")
        S.do("vector", lambda e: e.tensor_scalar(
            lnst[:, 2:3], lnst[:, 0:1], c256i, None, ALU.mult),
            reads=[lnstb, pb.cbuf], writes=[lnstb], tag=tag + "_mu")
        S.do("vector", lambda e: e.tensor_scalar(
            lnst[:, 3:4], lnst[:, 1:2], c256i, None, ALU.mult),
            reads=[lnstb, pb.cbuf], writes=[lnstb], tag=tag + "_m2")
        S.do("vector", lambda e: e.tensor_tensor(
            lnst[:, 4:5], lnst[:, 2:3], lnst[:, 2:3], ALU.mult),
            reads=[lnstb], writes=[lnstb], tag=tag + "_u2")
        S.do("vector", lambda e: e.tensor_tensor(
            lnst[:, 5:6], lnst[:, 3:4], lnst[:, 4:5], ALU.subtract),
            reads=[lnstb], writes=[lnstb], tag=tag + "_v")
        S.do("scalar", lambda e: e.activation(
            lnst[:, 6:7], lnst[:, 5:6], AF.Ln, bias=ceps6),
            reads=[lnstb, pb.cbuf], writes=[lnstb], tag=tag + "_l")
        S.do("scalar", lambda e: e.activation(
            lnst[:, 7:8], lnst[:, 6:7], AF.Exp, scale=cmh),
            reads=[lnstb, pb.cbuf], writes=[lnstb], tag=tag + "_e")
        if scale_q:
            S.do("vector", lambda e: e.tensor_scalar(
                lnst[:, 7:8], lnst[:, 7:8], c256, None, ALU.mult),
                reads=[lnstb, pb.cbuf], writes=[lnstb], tag=tag + "_x")
        S.do("vector", lambda e: e.tensor_scalar(
            dst[:, h0:h0 + 256], src[:, h0:h0 + 256], lnst[:, 2:3],
            lnst[:, 7:8], ALU.subtract, ALU.mult),
            reads=[srcb, lnstb], writes=[dstb], tag=tag + "_a")

    passes = [("q", oc) for oc in range(4)] + [("k", 0), ("v", 0)]
    for pi, (kind, oc) in enumerate(passes):
        wsrc = {"q": wqT, "k": wkT, "v": wvT}[kind]
        wname = {"q": "L1_WqT", "k": "L1_WkT", "v": "L1_WvT"}[kind]
        wt, wtb = wsl[pi % 2]
        col0 = oc * 512
        S.do("gpsimd", lambda e, wt=wt, wsrc=wsrc, col0=col0: e.dma_start(
            out=wt[:].rearrange("p (k o) -> p k o", k=16),
            in_=wsrc.ap()[:, col0:col0 + 512].rearrange(
                "(k p) o -> p k o", p=128)),
            reads=[pb.dbuf(wname)], writes=[wtb], is_dma=True, tag="q1_w")
        for tt in range(NCH):
            i = tt % 2
            pp, ppb = pe1[i]
            evt, evb = ev[i]
            obt, obb = outb_[i]
            for k in range(16):
                S.do("tensor", lambda e, pp=pp, wt=wt, k=k, tt=tt: e.matmul(
                    pp[:], xnd[k][0][:, tt * 128:(tt + 1) * 128],
                    wt[:, k * 512:(k + 1) * 512],
                    start=(k == 0), stop=(k == 15)),
                    reads=[xnd[k][1], wtb], writes=[ppb], tag="q1_mm")
            S.do("scalar", lambda e, evt=evt, pp=pp: e.copy(evt[:], pp[:]),
                 reads=[ppb], writes=[evb], tag="q1_ev")
            if kind in ("q", "k"):
                S.do("sync", lambda e, tt=tt: e.dma_start(
                    out=cst_[:], in_=rcos.ap()[tt * 128:(tt + 1) * 128, :]),
                    reads=[pb.dbuf("RCOS")], writes=[cstb_], is_dma=True,
                    tag="q1_c")
                S.do("sync", lambda e, tt=tt: e.dma_start(
                    out=snt[:], in_=rsin.ap()[tt * 128:(tt + 1) * 128, :]),
                    reads=[pb.dbuf("RSIN")], writes=[sntb], is_dma=True,
                    tag="q1_s")
                for hh in range(2):
                    b0 = hh * 256
                    S.do("vector", lambda e, evt=evt, b0=b0: e.tensor_tensor(
                        rtmp[:], evt[:, b0 + 128:b0 + 256], snt[:], ALU.mult),
                        reads=[evb, sntb], writes=[rtmpb], tag="q1_r1")
                    S.do("vector", lambda e, evt=evt, b0=b0: e.tensor_tensor(
                        rp[:, b0:b0 + 128], evt[:, b0:b0 + 128], cst_[:],
                        ALU.mult),
                        reads=[evb, cstb_], accwrites=[rpb], tag="q1_r2")
                    S.do("vector", lambda e, b0=b0: e.tensor_tensor(
                        rp[:, b0:b0 + 128], rp[:, b0:b0 + 128], rtmp[:],
                        ALU.subtract),
                        reads=[rpb, rtmpb], accwrites=[rpb], tag="q1_r3")
                    S.do("vector", lambda e, evt=evt, b0=b0: e.tensor_tensor(
                        rtmp[:], evt[:, b0:b0 + 128], snt[:], ALU.mult),
                        reads=[evb, sntb], writes=[rtmpb], tag="q1_r4")
                    S.do("vector", lambda e, evt=evt, b0=b0: e.tensor_tensor(
                        rp[:, b0 + 128:b0 + 256], evt[:, b0 + 128:b0 + 256],
                        cst_[:], ALU.mult),
                        reads=[evb, cstb_], accwrites=[rpb], tag="q1_r5")
                    S.do("vector", lambda e, b0=b0: e.tensor_tensor(
                        rp[:, b0 + 128:b0 + 256], rp[:, b0 + 128:b0 + 256],
                        rtmp[:], ALU.add),
                        reads=[rpb, rtmpb], accwrites=[rpb], tag="q1_r6")
                src, srcb = rp, rpb
            else:
                src, srcb = evt, evb
            for hh in range(2):
                ln_unit(src, srcb, obt, obb, hh * 256, kind == "q",
                        f"q1ln{pi}_{tt}_{hh}")
            if kind == "v":
                sto = S.do("sync", lambda e, obt=obt, tt=tt: e.dma_start(
                    out=vtok.ap()[tt], in_=obt[:]),
                    reads=[obb], accwrites=[pb.dbuf("VTOK")], is_dma=True,
                    tag="q1_vo")
                S.mark_output(sto)
                S.do("tensor", lambda e, obt=obt: e.matmul(
                    pfxp[:], tri[:], obt[:], start=True, stop=False),
                    reads=[trib, obb], writes=[pfxpb], tag="q1_pm")
                S.do("tensor", lambda e: e.matmul(
                    pfxp[:], onft[:], offr[:], start=False, stop=True),
                    reads=[onfb, offrb], writes=[pfxpb], tag="q1_po")
                S.do("vector", lambda e: e.tensor_copy(pfxt[:], pfxp[:]),
                     reads=[pfxpb], writes=[pfxtb], tag="q1_pe")
                sto = S.do("sync", lambda e, tt=tt: e.dma_start(
                    out=pfx.ap()[tt], in_=pfxt[:]),
                    reads=[pfxtb], accwrites=[pb.dbuf("PFX")], is_dma=True,
                    tag="q1_pfo")
                S.mark_output(sto)
                S.do("tensor", lambda e: e.matmul(
                    prow[:, 0:512], e127t[:], pfxt[:], start=True, stop=True),
                    reads=[e127b, pfxtb], writes=[prowb], tag="q1_pr")
                S.do("vector", lambda e: e.tensor_copy(offr[:], prow[:, 0:512]),
                     reads=[prowb], writes=[offrb], tag="q1_oe")
                if tt >= 11:
                    sto = S.do("sync", lambda e, obt=obt, tt=tt:
                               e.dma_start(
                                   out=kvcat.ap()[640 + (tt - 11) * 128:
                                                  640 + (tt - 10) * 128, :],
                                   in_=obt[:]),
                               reads=[obb], accwrites=[pb.dbuf("KVCAT")],
                               is_dma=True, tag="q1_vt")
                    S.mark_output(sto)
            else:
                for pc in range(4):
                    S.do("sync",
                         lambda e, obt=obt, pc=pc: e.dma_start_transpose(
                        tps[:, pc * 128:(pc + 1) * 128],
                        obt[:, pc * 128:(pc + 1) * 128]),
                        reads=[obb], accwrites=[tpsb], is_dma=True,
                        tag="q1_tp")
                dst = qd1 if kind == "q" else kd1
                dstb = pb.dbuf("QD1" if kind == "q" else "KD1")
                for pc in range(4):
                    ch = oc * 4 + pc
                    sto = S.do("sync", lambda e, tps=tps, dst=dst, ch=ch,
                               tt=tt, pc=pc: e.dma_start(
                                   out=dst.ap()[ch][:,
                                                    tt * 128:(tt + 1) * 128],
                                   in_=tps[:, pc * 128:(pc + 1) * 128]),
                               reads=[tpsb], accwrites=[dstb], is_dma=True,
                               tag="q1_do")
                    S.mark_output(sto)
                if kind == "k" and tt >= 11:
                    sto = S.do("sync", lambda e, obt=obt, tt=tt: e.dma_start(
                        out=kvcat.ap()[(tt - 11) * 128:(tt - 10) * 128, :],
                        in_=obt[:]),
                        reads=[obb], accwrites=[pb.dbuf("KVCAT")],
                        is_dma=True, tag="q1_kt")
                    S.mark_output(sto)
    sto = S.do("sync", lambda e: e.dma_start(out=vsum.ap(), in_=offr[:]),
               reads=[offrb], writes=[pb.dbuf("VSUM")], is_dma=True,
               tag="q1_vs")
    S.mark_output(sto)
    pb.pop_scope()



def p_load_xnd(pb, name, ntok):
    S = pb.S
    src = pb.io(name, [16, 128, ntok], BF16, "in")
    srcb = pb.dbuf(name)
    tiles = []
    for c in range(16):
        t, b = pb.sb(f"{name.lower()}{c}", [128, ntok], BF16)
        S.do("sync", lambda e, t=t, c=c: e.dma_start(out=t[:],
                                                     in_=src.ap()[c]),
             reads=[srcb], writes=[b], is_dma=True, tag=f"ldx{c}")
        tiles.append((t, b))
    pb.state[name] = tiles
    return tiles




def p_ffn1(pb):
    S = pb.S
    xnd = pb.state["XN1_D"]
    gwT = pb.io("L1_gateT", [D, INTER], F32, "in")
    uwT = pb.io("L1_upT", [D, INTER], F32, "in")
    dwT = pb.io("L1_downT", [INTER, D], F32, "in")
    gs_ = pb.io("L1_gate_s", [128, NIC], F32, "in")
    us_ = pb.io("L1_up_s", [128, NIC], F32, "in")
    ds_ = pb.io("L1_down_s", [128, 16], F32, "in")
    ffnd = pb.io("FFN_D", [16, 128, TOK], BF16, "out")
    ffndb = pb.dbuf("FFN_D")

    pb.push_scope()
    gst, gstb = pb.sb("gs1", [128, NIC], F32)
    S.do("sync", lambda e: e.dma_start(out=gst[:], in_=gs_.ap()),
         reads=[pb.dbuf("L1_gate_s")], writes=[gstb], is_dma=True, tag="f_gs")
    gsn, gsnb = pb.sb("gsn", [128, NIC], F32)
    cm1 = pb.cv(-1.0)
    c1 = pb.cv(1.0)
    S.do("vector", lambda e: e.tensor_scalar(gsn[:], gst[:], cm1, None,
                                             ALU.mult),
         reads=[gstb, pb.cbuf], writes=[gsnb], tag="f_gsn")
    ust, ustb = pb.sb("us1", [128, NIC], F32)
    S.do("sync", lambda e: e.dma_start(out=ust[:], in_=us_.ap()),
         reads=[pb.dbuf("L1_up_s")], writes=[ustb], is_dma=True, tag="f_us")
    dst_, dstb_ = pb.sb("ds1", [128, 16], F32)
    S.do("sync", lambda e: e.dma_start(out=dst_[:], in_=ds_.ap()),
         reads=[pb.dbuf("L1_down_s")], writes=[dstb_], is_dma=True,
         tag="f_ds")
    hsb = [pb.sb(f"h{ic}", [128, 1024], BF16) for ic in range(NIC)]
    wgu = [pb.sb("wgu", [128, 16 * 128], BF16) for _ in range(4)]
    wdn = [pb.sb("wdn", [128, NIC * 128], BF16) for _ in range(2)]
    pg_ = [pb.ps("pfg", [128, 1024], F32) for _ in range(2)]
    pu_, pub_ = pb.ps("pfu", [128, 1024], F32)
    pd_ = [pb.ps("pfd", [128, 512]) for _ in range(2)]
    ex1, ex1b = pb.sb("ex1", [128, 1024], F32)
    ex2, ex2b = pb.sb("ex2", [128, 1024], F32)
    gg_, ggb_ = pb.sb("ggf", [128, 1024], BF16)
    dev = [pb.sb("dev", [128, 512], BF16) for _ in range(2)]
    for half in range(2):
        t0 = half * 1024
        for ic in range(NIC):
            wtg, wtgb = wgu[(2 * ic) % 4]
            wtu, wtub = wgu[(2 * ic + 1) % 4]
            S.do("gpsimd", lambda e, wtg=wtg, ic=ic: e.dma_start(
                out=wtg[:].rearrange("p (k o) -> p k o", k=16),
                in_=gwT.ap()[:, ic * 128:(ic + 1) * 128].rearrange(
                    "(k p) o -> p k o", p=128)),
                reads=[pb.dbuf("L1_gateT")], writes=[wtgb], is_dma=True,
                tag="f_wg")
            S.do("gpsimd", lambda e, wtu=wtu, ic=ic: e.dma_start(
                out=wtu[:].rearrange("p (k o) -> p k o", k=16),
                in_=uwT.ap()[:, ic * 128:(ic + 1) * 128].rearrange(
                    "(k p) o -> p k o", p=128)),
                reads=[pb.dbuf("L1_upT")], writes=[wtub], is_dma=True,
                tag="f_wu")
            pgt, pgtb = pg_[ic % 2]
            for nt in range(2):
                for k in range(16):
                    S.do("tensor", lambda e, pgt=pgt, wtg=wtg, k=k, nt=nt,
                         t0=t0: e.matmul(
                             pgt[:, nt * 512:(nt + 1) * 512],
                             wtg[:, k * 128:(k + 1) * 128],
                             xnd[k][0][:, t0 + nt * 512:t0 + (nt + 1) * 512],
                             start=(k == 0), stop=(k == 15)),
                         reads=[wtgb, xnd[k][1]], writes=[pgtb], tag="f_gmm")
            for nt in range(2):
                for k in range(16):
                    S.do("tensor", lambda e, wtu=wtu, k=k, nt=nt, t0=t0:
                         e.matmul(
                             pu_[:, nt * 512:(nt + 1) * 512],
                             wtu[:, k * 128:(k + 1) * 128],
                             xnd[k][0][:, t0 + nt * 512:t0 + (nt + 1) * 512],
                             start=(k == 0), stop=(k == 15)),
                         reads=[wtub, xnd[k][1]], writes=[pub_], tag="f_umm")
            S.do("scalar", lambda e, pgt=pgt, ic=ic: e.activation(
                ex1[:], pgt[:, 0:1024], AF.Exp, scale=gsn[:, ic:ic + 1]),
                reads=[pgtb, gsnb], writes=[ex1b], tag="f_se")
            S.do("vector", lambda e: e.tensor_scalar(
                ex2[:], ex1[:], c1, None, ALU.add),
                reads=[ex1b, pb.cbuf], writes=[ex2b], tag="f_sa")
            S.do("vector", lambda e: e.reciprocal(ex1[:], ex2[:]),
                 reads=[ex2b], writes=[ex1b], tag="f_sr")
            S.do("vector", lambda e, pgt=pgt, ic=ic: e.tensor_scalar(
                gg_[:], pgt[:, 0:1024], gst[:, ic:ic + 1], None, ALU.mult),
                reads=[pgtb, gstb], writes=[ggb_], tag="f_gg")
            S.do("vector", lambda e: e.tensor_tensor(
                gg_[:], gg_[:], ex1[:], ALU.mult),
                reads=[ggb_, ex1b], writes=[ggb_], tag="f_sm")
            ht, hb = hsb[ic]
            S.do("vector", lambda e, ht=ht, ic=ic: e.scalar_tensor_tensor(
                ht[:], pu_[:, 0:1024], ust[:, ic:ic + 1], gg_[:],
                ALU.mult, ALU.mult),
                reads=[pub_, ustb, ggb_], writes=[hb], tag="f_h")
        for od in range(16):
            wt, wtb = wdn[od % 2]
            S.do("gpsimd", lambda e, wt=wt, od=od: e.dma_start(
                out=wt[:].rearrange("p (k o) -> p k o", k=NIC),
                in_=dwT.ap()[:, od * 128:(od + 1) * 128].rearrange(
                    "(k p) o -> p k o", p=128)),
                reads=[pb.dbuf("L1_downT")], writes=[wtb], is_dma=True,
                tag="f_wd")
            for nt in range(2):
                pdt, pdtb = pd_[nt]
                for ic in range(NIC):
                    ht, hb = hsb[ic]
                    S.do("tensor", lambda e, pdt=pdt, wt=wt, ht=ht, ic=ic,
                         nt=nt: e.matmul(
                             pdt[:], wt[:, ic * 128:(ic + 1) * 128],
                             ht[:, nt * 512:(nt + 1) * 512],
                             start=(ic == 0), stop=(ic == NIC - 1)),
                         reads=[wtb, hb], writes=[pdtb], tag="f_dmm")
                dt_, db_ = dev[nt]
                S.do("vector", lambda e, dt_=dt_, pdt=pdt, od=od:
                     e.tensor_scalar(dt_[:], pdt[:], dst_[:, od:od + 1],
                                     None, ALU.mult),
                     reads=[pdtb, dstb_], writes=[db_], tag="f_dev")
                sto = S.do("sync", lambda e, dt_=dt_, od=od, half=half,
                           nt=nt: e.dma_start(
                               out=ffnd.ap()[od][:, half * 1024 + nt * 512:
                                                 half * 1024 + (nt + 1) * 512],
                               in_=dt_[:]),
                           reads=[db_], accwrites=[ffndb], is_dma=True,
                           tag="f_do")
                S.mark_output(sto)
    pb.pop_scope()



def p_att1(pb):
    S = pb.S
    qd1 = pb.io("QD1", [16, 128, TOK], BF16, "in")
    kd1 = pb.io("KD1", [4, 128, TOK], BF16, "in")
    vtok = pb.io("VTOK", [16, 128, 512], BF16, "in")
    pfxd = pb.io("PFX", [16, 128, 512], F32, "in")
    kvall = pb.io("KV_ALL", [4, 1280, 512], BF16, "in")
    vsall = pb.io("VS_ALL", [4, 1, 512], F32, "in")
    selp = pb.io("SELP", [128, 4], F32, "in")
    sels = pb.io("SELS", [1, 4], F32, "in")
    masks = pb.io("MASKS", [16, 128, 640], BF16, "in")
    onesf = pb.io("ONESF", [1, 128], F32, "in")
    e127 = pb.io("E127", [128, 1], F32, "in")
    triuIf = pb.io("TRIUIF", [128, 128], F32, "in")
    pb.io("TRIUI", [128, 128], BF16, "in")

    qd = p_load_xnd(pb, "QD1", TOK)
    pb.state["ATT_D"] = qd
    pb.push_scope()
    kf = [[pb.sb(f"kf{kv}{dc}", [128, 2688], BF16) for dc in range(2)]
          for kv in range(2)]
    vf = [pb.sb(f"vf{t}", [128, 512], BF16) for t in range(21)]
    pfx = [pb.sb(f"px{t}", [128, 512], F32) for t in range(21)]
    selt, selb = pb.sb("selp", [128, 4], F32)
    selst, selsb = pb.sb("sels", [1, 4], F32)
    onft, onfb = pb.sb("onf1", [1, 128], F32)
    e127t, e127b = pb.sb("e1271", [128, 1], F32)
    trit, trib = pb.sb("trif", [128, 128], F32)
    trib16, trib16b = pb.sb("trib16", [128, 128], BF16)
    for t, b, srcn in [(selt, selb, "SELP"), (selst, selsb, "SELS"),
                       (onft, onfb, "ONESF"),
                       (e127t, e127b, "E127"), (trit, trib, "TRIUIF"),
                       (trib16, trib16b, "TRIUI")]:
        S.do("sync", lambda e, t=t, srcn=srcn: e.dma_start(
            out=t[:], in_=pb.dram[srcn][0].ap()),
            reads=[pb.dbuf(srcn)], writes=[b], is_dma=True, tag=f"a_{srcn}")
    htmp, htmpb = pb.sb("htmp", [128, 512], BF16)
    hacc, haccb = pb.sb("hacc", [128, 512], F32)
    hbf, hbfb = pb.sb("hbf", [128, 512], BF16)
    rows, rowsb = pb.sb("rows", [1, 512 * 3], F32)
    prow2, prow2b = pb.ps("prow2", [1, 512])
    pb1, pb1b = pb.ps("pb1", [128, 512])
    for kv_v, base in (("k", 0), ("v", 640)):
        for t5 in range(5):
            for p in range(4):
                tg = f"a_h{kv_v}{t5}_{p}"
                if p == 0:
                    S.do("vector", lambda e: e.memset(hacc[:], 0.0),
                         writes=[haccb], tag=tg + "z")
                S.do("sync", lambda e, p=p, base=base, t5=t5: e.dma_start(
                    out=htmp[:], in_=kvall.ap()[p, base + t5 * 128:
                                                base + (t5 + 1) * 128, :]),
                    reads=[pb.dbuf("KV_ALL")], writes=[htmpb], is_dma=True,
                    tag=tg)
                S.do("vector", lambda e, p=p: e.scalar_tensor_tensor(
                    hacc[:], htmp[:], selt[:, p:p + 1], hacc[:],
                    ALU.mult, ALU.add),
                    reads=[htmpb, selb, haccb], writes=[haccb], tag=tg + "a")
            S.do("vector", lambda e: e.tensor_copy(hbf[:], hacc[:]),
                 reads=[haccb], writes=[hbfb], tag=tg + "c")
            if kv_v == "k":
                for kv in range(2):
                    for dc in range(2):
                        kt, kb = kf[kv][dc]
                        S.do("sync",
                             lambda e, kt=kt, kv=kv, dc=dc, t5=t5:
                             e.dma_start_transpose(
                                 kt[:, t5 * 128:(t5 + 1) * 128],
                                 hbf[:, kv * 256 + dc * 128:
                                     kv * 256 + (dc + 1) * 128]),
                             reads=[hbfb], accwrites=[kb], is_dma=True,
                             tag=tg + "t")
            else:
                vt, vb = vf[t5]
                S.do("vector", lambda e, vt=vt: e.tensor_copy(vt[:], hbf[:]),
                     reads=[hbfb], writes=[vb], tag=tg + "v")
    for kv in range(2):
        for dc in range(2):
            kt, kb = kf[kv][dc]
            S.do("sync", lambda e, kt=kt, kv=kv, dc=dc: e.dma_start(
                out=kt[:, 640:2688], in_=kd1.ap()[kv * 2 + dc]),
                reads=[pb.dbuf("KD1")], accwrites=[kb], is_dma=True,
                tag="a_ko")
    for tt in range(16):
        vt, vb = vf[5 + tt]
        S.do("sync", lambda e, vt=vt, tt=tt: e.dma_start(
            out=vt[:], in_=vtok.ap()[tt]),
            reads=[pb.dbuf("VTOK")], writes=[vb], is_dma=True, tag="a_vo")
    S.do("vector", lambda e: e.memset(rows[:], 0.0), writes=[rowsb],
         tag="a_r0")
    for p in range(4):
        S.do("sync", lambda e, p=p: e.dma_start(
            out=rows[:, 1024:1536], in_=vsall.ap()[p]),
            reads=[pb.dbuf("VS_ALL")], accwrites=[rowsb], is_dma=True,
            tag=f"a_vs{p}")
        S.do("vector", lambda e, p=p: e.scalar_tensor_tensor(
            rows[:, 0:512], rows[:, 1024:1536], selst[:, p:p + 1],
            rows[:, 0:512], ALU.mult, ALU.add),
            reads=[rowsb, selsb], accwrites=[rowsb], tag=f"a_go{p}")
        S.do("vector", lambda e, p=p: e.tensor_tensor(
            rows[:, 512:1024], rows[:, 512:1024], rows[:, 1024:1536],
            ALU.add),
            reads=[rowsb], accwrites=[rowsb], tag=f"a_vt{p}")
    offh, offhb = pb.sb("offh", [1, 512], F32)
    S.do("vector", lambda e: e.memset(offh[:], 0.0), writes=[offhb],
         tag="a_oh0")
    for t5 in range(5):
        vt, vb = vf[t5]
        pt, ptb = pfx[t5]
        S.do("vector", lambda e, vt=vt: e.tensor_copy(htmp[:], vt[:]),
             reads=[vb], writes=[htmpb], tag=f"a_pw{t5}")
        S.do("tensor", lambda e: e.matmul(
            pb1[:], trib16[:], htmp[:], start=True, stop=False),
            reads=[trib16b, htmpb], writes=[pb1b], tag=f"a_pm{t5}")
        S.do("tensor", lambda e: e.matmul(
            pb1[:], onft[:], offh[:], start=False, stop=True),
            reads=[onfb, offhb], writes=[pb1b], tag=f"a_po{t5}")
        S.do("vector", lambda e, pt=pt: e.tensor_copy(pt[:], pb1[:]),
             reads=[pb1b], writes=[ptb], tag=f"a_pe{t5}")
        S.do("tensor", lambda e, pt=pt: e.matmul(
            prow2[:, 0:512], e127t[:], pt[:], start=True, stop=True),
            reads=[e127b, ptb], writes=[prow2b], tag=f"a_pr{t5}")
        S.do("vector", lambda e: e.tensor_copy(offh[:], prow2[:, 0:512]),
             reads=[prow2b], writes=[offhb], tag=f"a_oe{t5}")
    S.do("vector", lambda e: e.tensor_tensor(
        rows[:, 1024:1536], rows[:, 0:512], offh[:], ALU.subtract),
        reads=[rowsb, offhb], accwrites=[rowsb], tag="a_o2")
    S.do("tensor", lambda e: e.matmul(
        pb1[:], onft[:], rows[:, 1024:1536], start=True, stop=True),
        reads=[onfb, rowsb], writes=[pb1b], tag="a_o2b")
    ob2, ob2b = pb.sb("ob2", [128, 512], F32)
    S.do("vector", lambda e: e.tensor_copy(ob2[:], pb1[:]),
         reads=[pb1b], writes=[ob2b], tag="a_o2e")
    for t5 in range(5):
        pt, ptb = pfx[t5]
        S.do("vector", lambda e, pt=pt: e.tensor_tensor(
            pt[:], pt[:], ob2[:], ALU.add),
            reads=[ptb, ob2b], writes=[ptb], tag=f"a_ph{t5}")
    S.do("tensor", lambda e: e.matmul(
        pb1[:], onft[:], rows[:, 0:512], start=True, stop=True),
        reads=[onfb, rowsb], writes=[pb1b], tag="a_gob")
    S.do("vector", lambda e: e.tensor_copy(ob2[:], pb1[:]),
         reads=[pb1b], writes=[ob2b], tag="a_gobe")
    for tt in range(16):
        pt, ptb = pfx[5 + tt]
        S.do("sync", lambda e, pt=pt, tt=tt: e.dma_start(
            out=pt[:], in_=pfxd.ap()[tt]),
            reads=[pb.dbuf("PFX")], writes=[ptb], is_dma=True, tag="a_pl")
        S.do("vector", lambda e, pt=pt: e.tensor_tensor(
            pt[:], pt[:], ob2[:], ALU.add),
            reads=[ptb, ob2b], writes=[ptb], tag="a_pa")
    vtb, vtbb = pb.sb("vtb", [128, 512], F32)
    S.do("tensor", lambda e: e.matmul(
        pb1[:], onft[:], rows[:, 512:1024], start=True, stop=True),
        reads=[onfb, rowsb], writes=[pb1b], tag="a_vtb")
    S.do("vector", lambda e: e.tensor_copy(vtb[:], pb1[:]),
         reads=[pb1b], writes=[vtbb], tag="a_vtbe")

    pSC, pSCb = pb.ps("pSC", [128, 512])
    pS2, pS2b = pb.ps("pS2a", [128, 512])
    pOV, pOVb = pb.ps("pOV", [128, 512])
    pPT, _pPTb = pb.ps("pPT", [128, 512])
    pPTs = [Buf(f"pPT{j}") for j in range(4)]
    idb_t, idb_b = pb.sb("idfa", [128, 128], F32)
    pb.io("IDENTF", [128, 128], F32, "in")
    S.do("sync", lambda e: e.dma_start(out=idb_t[:],
                                       in_=pb.dram["IDENTF"][0].ap()),
         reads=[pb.dbuf("IDENTF")], writes=[idb_b], is_dma=True, tag="a_id")
    tsc, tscb = pb.sb("tsc", [128, 640], F32)
    zt, ztb = pb.sb("zt", [128, 640], F32)
    mrow, mrowb = pb.sb("mrow", [128, 8], F32)
    Pt, Ptb = pb.sb("Pt", [128, 640], F32)
    Pb_, Pbb_ = pb.sb("Pb", [128, 640], BF16)
    PT_, PTb_ = pb.sb("PT", [128, 640], BF16)
    mkt = [pb.sb("mk", [128, 640], BF16) for _ in range(2)]
    vct, vctb = pb.sb("vct", [128, 512], F32)
    oat, oatb = pb.sb("oat", [128, 256], F32)
    oab, oabb = pb.sb("oab", [128, 256], BF16)
    c30 = pb.cv(30.0)
    cm30 = pb.cv(-30.0)
    c1_480 = pb.cv(1.0 / 480.0)
    c1 = pb.cv(1.0)
    c7552 = pb.cv(float(SEQ - 640))
    att = qd
    pb.state["ATT_D"] = att
    for qt in range(16):
        mt, mtb = mkt[qt % 2]
        S.do("sync", lambda e, mt=mt, qt=qt: e.dma_start(
            out=mt[:], in_=masks.ap()[qt]),
            reads=[pb.dbuf("MASKS")], writes=[mtb], is_dma=True, tag="a_ml")
        phi, phib = pfx[5 + qt]
        plo, plob = pfx[qt]
        S.do("tensor", lambda e, phi=phi: e.matmul(
            prow2[:, 0:512], e127t[:], phi[:], start=True, stop=True),
            reads=[e127b, phib], writes=[prow2b], tag="a_vr1")
        S.do("vector", lambda e: e.tensor_tensor(
            rows[:, 1024:1536], rows[:, 512:1024], prow2[:, 0:512],
            ALU.subtract),
            reads=[rowsb, prow2b], accwrites=[rowsb], tag="a_vr2")
        S.do("tensor", lambda e, plo=plo: e.matmul(
            prow2[:, 0:512], e127t[:], plo[:], start=True, stop=True),
            reads=[e127b, plob], writes=[prow2b], tag="a_vr3")
        S.do("vector", lambda e: e.tensor_tensor(
            rows[:, 1024:1536], rows[:, 1024:1536], prow2[:, 0:512],
            ALU.add),
            reads=[rowsb, prow2b], accwrites=[rowsb], tag="a_vr4")
        S.do("tensor", lambda e: e.matmul(
            pb1[:], onft[:], rows[:, 1024:1536], start=True, stop=True),
            reads=[onfb, rowsb], writes=[pb1b], tag="a_vr5")
        S.do("vector", lambda e: e.tensor_copy(vct[:], pb1[:]),
            reads=[pb1b], writes=[vctb], tag="a_vr6")
        for h in range(H1):
            kv = h // 4
            c0 = 128 + qt * 128
            for dc in range(2):
                qtl, qlb = qd[2 * h + dc]
                kt, kb = kf[kv][dc]
                S.do("tensor", lambda e, qtl=qtl, kt=kt, dc=dc, c0=c0, qt=qt:
                     e.matmul(pSC[:], qtl[:, qt * 128:(qt + 1) * 128],
                              kt[:, c0:c0 + 512],
                              start=(dc == 0), stop=(dc == 1)),
                     reads=[qlb, kb], writes=[pSCb], tag="a_s1")
            for dc in range(2):
                qtl, qlb = qd[2 * h + dc]
                kt, kb = kf[kv][dc]
                S.do("tensor", lambda e, qtl=qtl, kt=kt, dc=dc, c0=c0, qt=qt:
                     e.matmul(pS2[:, 0:128],
                              qtl[:, qt * 128:(qt + 1) * 128],
                              kt[:, c0 + 512:c0 + 640],
                              start=(dc == 0), stop=(dc == 1)),
                     reads=[qlb, kb], writes=[pS2b], tag="a_s2")
            S.do("scalar", lambda e: e.activation(
                tsc[:, 0:512], pSC[:], AF.Tanh, scale=c1_480),
                reads=[pSCb, pb.cbuf], writes=[tscb], tag="a_th1")
            S.do("scalar", lambda e: e.activation(
                tsc[:, 512:640], pS2[:, 0:128], AF.Tanh, scale=c1_480),
                reads=[pS2b, pb.cbuf], accwrites=[tscb], tag="a_th2")
            S.do("vector", lambda e, mt=mt: e.scalar_tensor_tensor(
                zt[:], tsc[:], c1, mt[:], ALU.add, ALU.mult),
                reads=[tscb, mtb, pb.cbuf], writes=[ztb], tag="a_z")
            S.do("vector", lambda e: e.reduce_max(
                mrow[:, 0:1], zt[:], axis=AX.X),
                reads=[ztb], writes=[mrowb], tag="a_mx")
            S.do("vector", lambda e: e.tensor_scalar(
                mrow[:, 1:2], mrow[:, 0:1], cm30, None, ALU.mult),
                reads=[mrowb, pb.cbuf], accwrites=[mrowb], tag="a_mb")
            S.do("scalar", lambda e: e.activation(
                Pt[:], zt[:], AF.Exp, bias=mrow[:, 1:2], scale=c30),
                reads=[ztb, mrowb, pb.cbuf], writes=[Ptb], tag="a_ex")
            S.do("scalar", lambda e: e.activation(
                mrow[:, 2:3], mrow[:, 0:1], AF.Exp, scale=cm30),
                reads=[mrowb, pb.cbuf], accwrites=[mrowb], tag="a_pv")
            S.do("vector", lambda e: e.reduce_sum(
                mrow[:, 3:4], Pt[:], axis=AX.X),
                reads=[Ptb], writes=[mrowb], tag="a_rs")
            S.do("vector", lambda e: e.scalar_tensor_tensor(
                mrow[:, 4:5], mrow[:, 2:3], c7552,
                mrow[:, 3:4], ALU.mult, ALU.add),
                reads=[mrowb, pb.cbuf], accwrites=[mrowb], tag="a_den")
            S.do("vector", lambda e: e.reciprocal(
                mrow[:, 5:6], mrow[:, 4:5]),
                reads=[mrowb], accwrites=[mrowb], tag="a_rd")
            S.do("vector", lambda e: e.tensor_copy(Pb_[:], Pt[:]),
                 reads=[Ptb], writes=[Pbb_], tag="a_pc")
            for kt5 in range(5):
                S.do("sync", lambda e, kt5=kt5: e.dma_start_transpose(
                    PT_[:, kt5 * 128:(kt5 + 1) * 128],
                    Pb_[:, kt5 * 128:(kt5 + 1) * 128]),
                    reads=[Pbb_], accwrites=[PTb_], is_dma=True, tag="a_pt")
            for kt5 in range(5):
                vt, vb = vf[qt + 1 + kt5]
                S.do("tensor", lambda e, vt=vt, kt5=kt5, kv=kv: e.matmul(
                    pOV[:, 0:256], PT_[:, kt5 * 128:(kt5 + 1) * 128],
                    vt[:, kv * 256:(kv + 1) * 256],
                    start=(kt5 == 0), stop=(kt5 == 4)),
                    reads=[PTb_, vb], writes=[pOVb], tag="a_pv2")
            S.do("vector", lambda e, kv=kv: e.scalar_tensor_tensor(
                oat[:], vct[:, kv * 256:(kv + 1) * 256], mrow[:, 2:3],
                pOV[:, 0:256], ALU.mult, ALU.add),
                reads=[vctb, mrowb, pOVb], writes=[oatb], tag="a_oc")
            S.do("vector", lambda e: e.tensor_scalar(
                oab[:], oat[:], mrow[:, 5:6], None, ALU.mult),
                reads=[oatb, mrowb], writes=[oabb], tag="a_on")
            for dc in range(2):
                at, ab = att[2 * h + dc]
                S.do("sync",
                     lambda e, at=at, dc=dc, qt=qt: e.dma_start_transpose(
                    at[:, qt * 128:(qt + 1) * 128],
                    oab[:, dc * 128:(dc + 1) * 128]),
                    reads=[oabb], accwrites=[ab], is_dma=True, tag="a_ot")
    pb.pop_scope()




def p_wo1(pb):
    S = pb.S
    att = pb.state["ATT_D"]
    woT = pb.io("L1_WoT", [D, D], F32, "in")
    x1 = pb.io("X1", [TOK, D], F32, "in")
    x1b = pb.dbuf("X1")
    ffnd = pb.io("FFN_D", [16, 128, TOK], BF16, "in")
    x2 = pb.io("X2", [TOK, D], F32, "out")
    x2b = pb.dbuf("X2")

    pb.push_scope()
    fd = p_load_xnd(pb, "FFN_D", TOK)
    wsl = [pb.sb("wo1", [128, 16 * 512], BF16) for _ in range(2)]
    pp_ = [pb.ps("pw1", [128, 512]) for _ in range(2)]
    ftok, ftokb = pb.sb("ftok", [128, 512], BF16)
    x1t = [pb.sb("x1l", [128, 512], F32) for _ in range(2)]
    x2t = [pb.sb("x2t", [128, 512], F32) for _ in range(2)]
    for hc in range(4):
        wt, wtb = wsl[hc % 2]
        S.do("gpsimd", lambda e, wt=wt, hc=hc: e.dma_start(
            out=wt[:].rearrange("p (k o) -> p k o", k=16),
            in_=woT.ap()[:, hc * 512:(hc + 1) * 512].rearrange(
                "(k p) o -> p k o", p=128)),
            reads=[pb.dbuf("L1_WoT")], writes=[wtb], is_dma=True, tag="w1_w")
        for tt in range(16):
            i = tt % 2
            pp, ppb = pp_[i]
            for k in range(16):
                at, ab = att[k]
                S.do("tensor", lambda e, pp=pp, at=at, wt=wt, k=k, tt=tt:
                     e.matmul(pp[:], at[:, tt * 128:(tt + 1) * 128],
                              wt[:, k * 512:(k + 1) * 512],
                              start=(k == 0), stop=(k == 15)),
                     reads=[ab, wtb], writes=[ppb], tag="w1_mm")
            for j in range(4):
                ch = hc * 4 + j
                ft, fb = fd[ch]
                S.do("sync",
                     lambda e, ft=ft, j=j, tt=tt: e.dma_start_transpose(
                    ftok[:, j * 128:(j + 1) * 128],
                    ft[:, tt * 128:(tt + 1) * 128]),
                    reads=[fb], accwrites=[ftokb], is_dma=True, tag="w1_ft")
            xt, xtb = x1t[i]
            S.do("sync", lambda e, xt=xt, tt=tt, hc=hc: e.dma_start(
                out=xt[:], in_=x1.ap()[tt * 128:(tt + 1) * 128,
                                       hc * 512:(hc + 1) * 512]),
                reads=[x1b], writes=[xtb], is_dma=True, tag="w1_x1")
            x2tt, x2tb = x2t[i]
            S.do("vector", lambda e, x2tt=x2tt, pp=pp, xt=xt: e.tensor_tensor(
                x2tt[:], pp[:], xt[:], ALU.add),
                reads=[ppb, xtb], writes=[x2tb], tag="w1_a1")
            S.do("vector", lambda e, x2tt=x2tt: e.tensor_tensor(
                x2tt[:], x2tt[:], ftok[:], ALU.add),
                reads=[x2tb, ftokb], writes=[x2tb], tag="w1_a2")
            sto = S.do("sync", lambda e, x2tt=x2tt, tt=tt, hc=hc: e.dma_start(
                out=x2.ap()[tt * 128:(tt + 1) * 128,
                            hc * 512:(hc + 1) * 512], in_=x2tt[:]),
                reads=[x2tb], accwrites=[x2b], is_dma=True, tag="w1_o")
            S.mark_output(sto)
    pb.pop_scope()

SEQ = 8192


def build_L1():
    pb = PB("L1")
    pb.boot()
    p_layernorm(pb, "XIN", "XN_D", HTOK)
    p_beta(pb)
    p_delta1(pb)
    pb.S.emit()
    return pb


def build_L2():
    pb = PB("L2")
    pb.boot()
    p_load_xnd(pb, "XN_D", HTOK)
    p_moe(pb)
    p_gproj(pb)
    p_prefix(pb)
    p_delta2(pb)
    pb.S.emit()
    return pb


def build_L3():
    pb = PB("L3")
    pb.boot()
    p_wo0(pb)
    p_layernorm(pb, "X1", "XN1_D", TOK)
    p_qkv1(pb)
    pb.S.emit()
    return pb


def build_L4():
    pb = PB("L4")
    pb.boot()
    pb.push_scope()
    p_load_xnd(pb, "XN1_D", TOK)
    p_ffn1(pb)
    pb.pop_scope()
    p_att1(pb)
    p_wo1(pb)
    pb.S.emit()
    return pb




def make_consts():
    import ml_dtypes
    bf = ml_dtypes.bfloat16
    c = {}
    c["IDENTB"] = np.eye(128, dtype=bf)
    c["IDENTF"] = np.eye(128, dtype=np.float32)
    c["TRILS"] = np.tril(np.ones((128, 128), np.float32), -1).astype(bf)
    c["TRIUS"] = np.triu(np.ones((128, 128), np.float32), 1).astype(bf)
    c["TRIUI"] = np.triu(np.ones((128, 128), np.float32), 0).astype(bf)
    c["TRIUIF"] = np.triu(np.ones((128, 128), np.float32), 0)
    c["ONESB"] = np.ones((128, 1), dtype=bf)
    c["ONESR"] = np.ones((1, 128), dtype=bf)
    c["ONESF"] = np.ones((1, 128), dtype=np.float32)
    e = np.zeros((128, 1), np.float32)
    e[127, 0] = 1.0
    c["E127"] = e
    return c


def make_rope(t0):
    inv = 1.0 / (1e6 ** (np.arange(0, DH1, 2, dtype=np.float32) / DH1))
    pos = np.arange(t0, t0 + TOK, dtype=np.float32)
    fr = pos[:, None] * inv[None, :]
    return np.cos(fr).astype(np.float32), np.sin(fr).astype(np.float32)


def make_masks(t0, seq):
    import ml_dtypes
    out = np.zeros((16, 128, 640), np.float32)
    for qt in range(16):
        i = t0 + qt * 128 + np.arange(128)[:, None]
        j = t0 + qt * 128 - 512 + np.arange(640)[None, :]
        out[qt] = ((j <= i) & (j > i - 512) & (j >= 0)).astype(np.float32)
    return out.astype(ml_dtypes.bfloat16)


def stage_core(inputs, core):
    g = core % 4
    b = core // 4
    t0 = g * TOK
    d = {}
    x = np.asarray(inputs["L0_attn_hidden_states"])[b]
    xs = np.zeros((HTOK, D), np.float32)
    xs[HALO:] = x[t0:t0 + TOK]
    if g > 0:
        xs[:HALO] = x[t0 - HALO:t0]
    d["XIN"] = xs
    for nm in ("L0_attn_Wq", "L0_attn_Wk", "L0_attn_Wv", "L0_attn_Wg",
               "L0_attn_Wb", "L0_attn_conv_wq", "L0_attn_conv_wk",
               "L0_attn_conv_wv", "L0_ffn_gate_w", "L0_ffn_expert_w",
               "L0_attn_Wo"):
        d[nm] = np.ascontiguousarray(np.asarray(inputs[nm]), dtype=np.float32)
    d["L0_ffn_gate_b"] = np.asarray(inputs["L0_ffn_gate_b"]).reshape(1, NE)
    d["L0_ffn_expert_b"] = np.ascontiguousarray(
        np.asarray(inputs["L0_ffn_expert_b"])).reshape(1, NE * D)
    d["L0_attn_norm_w"] = np.asarray(
        inputs["L0_attn_norm_w"]).reshape(128, 1).astype(np.float32)
    d["L1_WqT"] = np.ascontiguousarray(np.asarray(inputs["L1_attn_Wq"]).T)
    d["L1_WkT"] = np.ascontiguousarray(np.asarray(inputs["L1_attn_Wk"]).T)
    d["L1_WvT"] = np.ascontiguousarray(np.asarray(inputs["L1_attn_Wv"]).T)
    d["L1_WoT"] = np.ascontiguousarray(np.asarray(inputs["L1_attn_Wo"]).T)
    d["L1_gateT"] = np.ascontiguousarray(np.asarray(inputs["L1_ffn_gate_w"]).T)
    d["L1_upT"] = np.ascontiguousarray(np.asarray(inputs["L1_ffn_up_w"]).T)
    d["L1_downT"] = np.ascontiguousarray(np.asarray(inputs["L1_ffn_down_w"]).T)
    d["L1_gate_s"] = np.ascontiguousarray(
        np.asarray(inputs["L1_ffn_gate_s"]).reshape(NIC, 128).T)
    d["L1_up_s"] = np.ascontiguousarray(
        np.asarray(inputs["L1_ffn_up_s"]).reshape(NIC, 128).T)
    d["L1_down_s"] = np.ascontiguousarray(
        np.asarray(inputs["L1_ffn_down_s"]).reshape(16, 128).T)
    d.update(make_consts())
    rc, rs = make_rope(t0)
    d["RCOS"], d["RSIN"] = rc, rs
    d["MASKS"] = make_masks(t0, SEQ)
    sel4 = np.zeros((128, 4), np.float32)
    sel4[:, g] = 1.0
    d["SEL4"] = sel4
    selp = np.zeros((128, 4), np.float32)
    if g > 0:
        selp[:, g - 1] = 1.0
    d["SELP"] = selp
    sels = np.zeros((1, 4), np.float32)
    sels[0, :g] = 1.0
    d["SELS"] = sels
    return d


def _in_map_for(pb, staged, relayed):
    m = {}
    for name, kind in pb.dram_kind.items():
        if kind != "in":
            continue
        if name in relayed:
            m[name] = relayed[name]
        else:
            m[name] = staged[name]
    return m


_PROGS = None


def kernel_unfused(**inputs):
    global _PROGS
    if _PROGS is None:
        _PROGS = (build_L1(), build_L2(), build_L3(), build_L4())
    L1p, L2p, L3p, L4p = _PROGS
    staged = [stage_core(inputs, c) for c in range(8)]
    relays = [dict() for _ in range(8)]
    outs_all = None
    for li, pbp in enumerate((L1p, L2p, L3p, L4p)):
        in_maps = [_in_map_for(pbp, staged[c], relays[c]) for c in range(8)]
        res = run_bass_kernel_spmd(pbp.nc, in_maps, core_ids=list(range(8)))
        for c in range(8):
            for k, v in res.results[c].items():
                relays[c][k] = v
        if li == 0:
            for c in range(8):
                g0 = (c // 4) * 4
                relays[c]["AB_ALL"] = np.stack(
                    [relays[g0 + p]["ABG"] for p in range(4)])
        if li == 2:
            for c in range(8):
                g0 = (c // 4) * 4
                relays[c]["KV_ALL"] = np.stack(
                    [relays[g0 + p]["KVCAT"] for p in range(4)])
                relays[c]["VS_ALL"] = np.stack(
                    [relays[g0 + p]["VSUM"] for p in range(4)])
        outs_all = res
    out = np.zeros((2, SEQ, D), np.float32)
    for c in range(8):
        g = c % 4
        b = c // 4
        out[b, g * TOK:(g + 1) * TOK] = relays[c]["X2"]
    return out



def p_collective(pb, kind, src_name, dst_name, src_shape, dst_shape, dtype):
    S = pb.S
    src = pb.io(src_name, src_shape, dtype, "internal")
    dst = pb.io(dst_name, dst_shape, dtype, "internal")

    def to2d(ap):
        if len(ap.shape) == 3:
            return ap.rearrange("a b c -> (a b) c")
        return ap

    def fn(e):
        return e.collective_compute(
            kind, mybir.AluOpType.bypass, replica_groups=GROUPS,
            ins=[to2d(src.ap()).opt()], outs=[to2d(dst.ap()).opt()])
    S.do("gpsimd", fn, reads=[pb.dbuf(src_name)], writes=[pb.dbuf(dst_name)],
         tag=f"coll_{src_name}")


def build_fused():
    pb = PB("FUSED")
    pb.boot()
    internal = ["XN_D", "BETA", "SPILL", "QSP", "SCAL", "ABG", "AB_ALL",
                "X1P", "OG_D", "GT", "X1", "XN1_D", "QD1", "KD1", "VTOK",
                "KVCAT", "VSUM", "PFX", "KV_ALL", "VS_ALL", "FFN_D"]
    shapes = {
        "XN_D": ([16, 128, HTOK], BF16), "BETA": ([128, NH0 * NCH], F32),
        "SPILL": ([NH0, NCH, 128, 512], BF16), "QSP": ([NH0, 128, TOK], BF16),
        "SCAL": ([NH0, 128, 32], F32), "ABG": ([NH0 * 128, 256], F32),
        "AB_ALL": ([4, NH0 * 128, 256], F32), "X1P": ([TOK, D], F32),
        "OG_D": ([NH0, 128, TOK], BF16), "GT": ([NCH, 128, D], BF16),
        "X1": ([TOK, D], F32), "XN1_D": ([16, 128, TOK], BF16),
        "QD1": ([16, 128, TOK], BF16), "KD1": ([4, 128, TOK], BF16),
        "VTOK": ([16, 128, 512], BF16), "KVCAT": ([1280, 512], BF16),
        "VSUM": ([1, 512], F32), "PFX": ([16, 128, 512], F32),
        "KV_ALL": ([4, 1280, 512], BF16), "VS_ALL": ([4, 1, 512], F32),
        "FFN_D": ([16, 128, TOK], BF16),
    }
    for nm in internal:
        pb.io(nm, shapes[nm][0], shapes[nm][1], "internal")
    pb.push_scope()
    p_layernorm(pb, "XIN", "XN_D", HTOK)
    p_beta(pb)
    p_delta1(pb)
    p_collective(pb, "AllGather", "ABG", "AB_ALL",
                 [NH0 * 128, 256], [4 * NH0 * 128, 256], F32)
    p_moe(pb)
    p_gproj(pb)
    pb.pop_scope()
    pb.push_scope()
    p_prefix(pb)
    p_delta2(pb)
    pb.pop_scope()
    p_wo0(pb)
    pb.push_scope()
    p_layernorm(pb, "X1", "XN1_D", TOK)
    p_qkv1(pb)
    pb.pop_scope()
    p_collective(pb, "AllGather", "KVCAT", "KV_ALL",
                 [1280, 512], [4 * 1280, 512], BF16)
    p_collective(pb, "AllGather", "VSUM", "VS_ALL",
                 [1, 512], [4, 512], F32)
    pb.push_scope()
    p_load_xnd(pb, "XN1_D", TOK)
    p_ffn1(pb)
    pb.pop_scope()
    p_att1(pb)
    p_wo1(pb)
    pb.S.emit()
    return pb


_FUSED = None


def kernel(**inputs):
    global _FUSED
    if _FUSED is None:
        _FUSED = build_fused()
    pbp = _FUSED
    staged = [stage_core(inputs, c) for c in range(8)]
    in_maps = [_in_map_for(pbp, staged[c], {}) for c in range(8)]
    res = run_bass_kernel_spmd(pbp.nc, in_maps, core_ids=list(range(8)))
    out = np.zeros((2, SEQ, D), np.float32)
    for c in range(8):
        g = c % 4
        b = c // 4
        out[b, g * TOK:(g + 1) * TOK] = res.results[c]["X2"]
    return out


# version 15 (final)
# speedup vs baseline: 1.0243x; 1.0243x over previous
import math
from contextlib import ExitStack

import numpy as np
import concourse.bass as bass
import concourse.mybir as mybir
from concourse.bass_utils import run_bass_kernel_spmd

F32 = mybir.dt.float32
BF16 = mybir.dt.bfloat16
AF = mybir.ActivationFunctionType
ALU = mybir.AluOpType
AX = mybir.AxisListType

ENGINES = ("sync", "tensor", "vector", "scalar", "gpsimd")



class Buf:
    __slots__ = ("name", "base_writers", "acc_writers", "readers")

    def __init__(self, name=""):
        self.name = name
        self.base_writers = []
        self.acc_writers = []
        self.readers = []


class Step:
    __slots__ = ("eng", "fn", "deps", "needs_inc", "val", "wait_val",
                 "is_dma", "pe_accum", "tag", "idx")

    def __init__(self, eng, fn, is_dma, tag=""):
        self.eng = eng
        self.fn = fn
        self.deps = []
        self.needs_inc = False
        self.val = None
        self.wait_val = None
        self.is_dma = is_dma
        self.pe_accum = False
        self.tag = tag
        self.idx = 0


class Sched:
    def __init__(self, nc):
        self.nc = nc
        self.streams = {e: [] for e in ENGINES}
        self.ctx = ExitStack()
        self.sems = {}
        for e in ENGINES:
            self.sems[e] = self.ctx.enter_context(nc.semaphore(f"S_{e}"))
            self.sems[e + "_dma"] = self.ctx.enter_context(
                nc.semaphore(f"D_{e}"))
        self.final_out_steps = []
        self.cov = {e: {} for e in ENGINES}

    def do(self, eng, fn, reads=(), writes=(), accwrites=(), is_dma=False,
           tag=""):
        st = Step(eng, fn, is_dma, tag)
        dep = set()
        for b in reads:
            dep.update(b.base_writers)
            dep.update(b.acc_writers)
        for b in writes:
            dep.update(b.base_writers)
            dep.update(b.acc_writers)
            dep.update(b.readers)
        for b in accwrites:
            dep.update(b.readers)
            dep.update(b.base_writers)
        cov = self.cov[eng]
        for d in dep:
            if d is st:
                continue
            if d.eng == "tensor" and eng == "tensor" and not d.is_dma:
                continue
            key = (d.eng, d.is_dma)
            if cov.get(key, -1) >= d.idx:
                continue
            st.deps.append(d)
            d.needs_inc = True
        for d in st.deps:
            key = (d.eng, d.is_dma)
            if cov.get(key, -1) < d.idx:
                cov[key] = d.idx
        for b in reads:
            b.readers.append(st)
        for b in writes:
            b.base_writers = [st]
            b.acc_writers = []
            b.readers = []
        for b in accwrites:
            b.acc_writers.append(st)
        st.idx = len(self.streams[eng])
        self.streams[eng].append(st)
        return st

    def barrier(self):
        lasts = [s[-1] for s in self.streams.values() if s]
        for e in ENGINES:
            for d in lasts:
                if d.eng == e and self.streams[e] and self.streams[e][-1] is d:
                    continue

        def nopfn(eng):
            return eng.nop()
        for e in ENGINES:
            st = Step(e, nopfn, False, "barrier")
            for d in lasts:
                if d.eng == e:
                    continue
                st.deps.append(d)
                d.needs_inc = True
            self.streams[e].append(st)

    def mark_output(self, step):
        step.needs_inc = True
        self.final_out_steps.append(step)

    def emit(self):
        for e in ENGINES:
            c = {False: 0, True: 0}
            for st in self.streams[e]:
                st.needs_inc = True
                c[st.is_dma] += 16 if st.is_dma else 1
                st.val = c[st.is_dma]

        for e in ENGINES:
            runc, rund = [], []

            def close(lst):
                vals = [s.val for s in lst if s.val is not None]
                if vals:
                    v = max(vals)
                    for s in lst:
                        if s.val is not None:
                            s.wait_val = v
                lst.clear()

            for st in self.streams[e]:
                if st.deps:
                    close(runc)
                    close(rund)
                (rund if st.is_dma else runc).append(st)
            close(runc)
            close(rund)

        def sem_key(st):
            return st.eng + "_dma" if st.is_dma else st.eng

        def wval(st):
            return st.wait_val if st.wait_val is not None else st.val

        observed = {}
        for e in ENGINES:
            for st in self.streams[e]:
                for d in st.deps:
                    observed.setdefault(sem_key(d), set()).add(wval(d))
        for st in self.final_out_steps:
            observed.setdefault(sem_key(st), set()).add(wval(st))
        obs_sorted = {k: sorted(v) for k, v in observed.items()}

        nc = self.nc

        def run_stream(engname):
            def _(eng):
                own_dma = engname + "_dma"
                own_c = engname
                obs_d = obs_sorted.get(own_dma, [])
                obs_c = obs_sorted.get(own_c, [])
                pd = [0]
                pc = [0]
                cum = {own_dma: 0, own_c: 0}
                sw = {own_dma: 0, own_c: 0}

                def maybe_selfwait(key, obs, ptr):
                    n = ptr[0]
                    while n < len(obs) and obs[n] <= cum[key]:
                        n += 1
                    ptr[0] = n
                    if n > 0 and obs[n - 1] > sw[key]:
                        eng.wait_ge(self.sems[key], cum[key])
                        sw[key] = cum[key]

                for st in self.streams[engname]:
                    if st.deps:
                        maybe_selfwait(own_dma, obs_d, pd)
                        maybe_selfwait(own_c, obs_c, pc)
                        waits = {}
                        for d in st.deps:
                            s = sem_key(d)
                            v = wval(d)
                            if waits.get(s, -1) < v:
                                waits[s] = v
                        for s, v in sorted(waits.items()):
                            eng.wait_ge(self.sems[s], v)
                    inst = st.fn(eng)
                    inst.then_inc(self.sems[sem_key(st)],
                                  16 if st.is_dma else 1)
                    cum[sem_key(st)] += 16 if st.is_dma else 1
                if engname == "sync":
                    for st in self.final_out_steps:
                        eng.wait_ge(self.sems[sem_key(st)], wval(st))
            return _

        with nc.Block() as block:
            block.sync(run_stream("sync"))
            block.tensor(run_stream("tensor"))
            block.vector(run_stream("vector"))
            block.scalar(run_stream("scalar"))
            block.gpsimd(run_stream("gpsimd"))
        self.ctx.close()


D = 2048
TOK = 2048
HALO = 128
HTOK = TOK + HALO
NH0 = 16
DH0 = 128
C = 128
NCH = TOK // C
NE = 8
H1, KV1, DH1 = 8, 2, 256
SW = 512
INTER = 5632
NIC = INTER // 128

GROUPS = [[0, 1, 2, 3], [4, 5, 6, 7]]



class PB:

    def __init__(self, name="prog"):
        self.nc = bass.Bass("TRN2", target_bir_lowering=False)
        self.S = Sched(self.nc)
        self.dram = {}
        self.dram_kind = {}
        self._consts = None
        self._const_vals = {}
        self._const_n = 0
        self._warmed = False
        self.uid = 0
        self.scopes = []
        self.state = {}

    def io(self, name, shape, dtype, kind):
        if name in self.dram:
            return self.dram[name][0]
        k = {"in": "ExternalInput", "out": "ExternalOutput",
             "internal": "Internal"}[kind]
        h = self.nc.dram_tensor(name, list(shape), dtype, kind=k)
        self.dram[name] = (h, Buf(name))
        self.dram_kind[name] = kind
        return h

    def dbuf(self, name):
        return self.dram[name][1]

    def sb(self, name, shape, dtype):
        self.uid += 1
        stack = self.scopes[-1] if self.scopes else self.S.ctx
        t = stack.enter_context(
            self.nc.sbuf_tensor(f"{name}_{self.uid}", list(shape), dtype))
        return t, Buf(name)

    def ps(self, name, shape, dtype=F32):
        self.uid += 1
        stack = self.scopes[-1] if self.scopes else self.S.ctx
        per = 2048 // {F32: 4, BF16: 2}[dtype]
        shape = [shape[0], max(shape[1], per)]
        t = stack.enter_context(
            self.nc.psum_tensor(f"{name}_{self.uid}", list(shape), dtype))
        return t, Buf(name)

    def push_scope(self):
        self.scopes.append(ExitStack())

    def pop_scope(self):
        self.scopes.pop().close()
        self.S.barrier()

    def boot(self):
        self._consts, self._cb = self.sb("csts", [128, 32], F32)
        self._warm, self._wb = self.sb("warm", [128, 8], F32)
        S = self.S
        cst = self._consts

        def mk(e):
            return e.memset(cst[:, 0:1], 0.0)
        s0 = S.do("vector", mk, writes=[self._cb], tag="cst0")
        S.do("scalar", lambda e: e.copy(self._warm[:, 0:1], cst[:, 0:1]),
             reads=[self._cb], writes=[self._wb], tag="warmup")
        self._const_vals[0.0] = 0
        self._const_n = 1

    def cv(self, val):
        v = float(val)
        if v in self._const_vals:
            j = self._const_vals[v]
        else:
            j = self._const_n
            assert j < 32, "consts tile full"
            self._const_n += 1
            self._const_vals[v] = j
            cst = self._consts
            self.S.do("vector", lambda e, j=j, v=v: e.memset(cst[:, j:j + 1], v),
                      writes=[self._cb], tag=f"cst{v}")
        return self._consts[:, j:j + 1]

    @property
    def cbuf(self):
        return self._cb



def p_layernorm(pb, src_name, dst_name, ntok, eps=1e-5, keep=True):
    S, nc = pb.S, pb.nc
    ntt = ntok // 128
    xsrc = pb.io(src_name, [ntok, D], F32, "in")
    xb = pb.dbuf(src_name)
    xdst = pb.io(dst_name, [16, 128, ntok], BF16, "out")
    db = pb.dbuf(dst_name)

    xnd = []
    for c in range(16):
        t, b = pb.sb(f"xnd{c}", [128, ntok], BF16)
        xnd.append((t, b))
    pb.state[dst_name] = xnd

    pb.push_scope()
    xt = [pb.sb("xt", [128, D], F32) for _ in range(2)]
    sq = [pb.sb("sq", [128, D], F32) for _ in range(2)]
    st_ = [pb.sb("st", [128, 8], F32) for _ in range(2)]
    xn = [pb.sb("xn", [128, D], BF16) for _ in range(2)]

    for tt in range(ntt):
        k = tt % 2
        xtile, xtb = xt[k]
        sqt, sqb = sq[k]
        stt_, stb = st_[k]
        xnt, xnb = xn[k]
        r0 = tt * 128
        S.do("sync", lambda e, xtile=xtile, r0=r0: e.dma_start(
            out=xtile[:], in_=xsrc.ap()[r0:r0 + 128, :]),
            reads=[xb], writes=[xtb], is_dma=True, tag=f"ln_ld{tt}")
        S.do("vector", lambda e, xtile=xtile, stt_=stt_: e.reduce_sum(
            stt_[:, 0:1], xtile[:], axis=AX.X),
            reads=[xtb], writes=[stb], tag="ln_sum")
        S.do("vector", lambda e, xtile=xtile, sqt=sqt:
             e.tensor_tensor(sqt[:], xtile[:], xtile[:], ALU.mult),
             reads=[xtb], writes=[sqb], tag="ln_sq")
        S.do("vector", lambda e, sqt=sqt, stt_=stt_:
             e.reduce_sum(stt_[:, 1:2], sqt[:], axis=AX.X),
             reads=[sqb], writes=[stb], tag="ln_sq2")
        S.do("vector", lambda e, stt_=stt_: e.tensor_scalar(
            stt_[:, 2:3], stt_[:, 0:1], 1.0 / D, None, ALU.mult),
            reads=[stb], writes=[stb], tag="ln_mu")
        S.do("vector", lambda e, stt_=stt_: e.tensor_scalar(
            stt_[:, 3:4], stt_[:, 1:2], 1.0 / D, None, ALU.mult),
            reads=[stb], writes=[stb], tag="ln_m2")
        S.do("vector", lambda e, stt_=stt_: e.tensor_tensor(
            stt_[:, 4:5], stt_[:, 2:3], stt_[:, 2:3], ALU.mult),
            reads=[stb], writes=[stb], tag="ln_mu2")
        S.do("vector", lambda e, stt_=stt_: e.tensor_tensor(
            stt_[:, 5:6], stt_[:, 3:4], stt_[:, 4:5], ALU.subtract),
            reads=[stb], writes=[stb], tag="ln_var")
        c_eps = pb.cv(eps)
        c_mh = pb.cv(-0.5)
        S.do("scalar", lambda e, stt_=stt_, c_eps=c_eps: e.activation(
            stt_[:, 6:7], stt_[:, 5:6], AF.Ln, bias=c_eps),
            reads=[stb, pb.cbuf], writes=[stb], tag="ln_ln")
        S.do("scalar", lambda e, stt_=stt_, c_mh=c_mh: e.activation(
            stt_[:, 7:8], stt_[:, 6:7], AF.Exp, scale=c_mh),
            reads=[stb, pb.cbuf], writes=[stb], tag="ln_exp")
        S.do("vector", lambda e, xtile=xtile, xnt=xnt, stt_=stt_:
             e.tensor_scalar(xnt[:], xtile[:], stt_[:, 2:3], stt_[:, 7:8],
                             ALU.subtract, ALU.mult),
             reads=[xtb, stb], writes=[xnb], tag="ln_app")
        for c in range(16):
            dstt, dstb = xnd[c]
            S.do("sync",
                 lambda e, dstt=dstt, xnt=xnt, c=c, tt=tt:
                 e.dma_start_transpose(dstt[:, tt * 128:(tt + 1) * 128],
                                       xnt[:, c * 128:(c + 1) * 128]),
                 reads=[xnb], accwrites=[dstb], is_dma=True, tag="ln_tp")
    for c in range(16):
        dstt, dstb = xnd[c]
        st2 = S.do("sync", lambda e, dstt=dstt, c=c: e.dma_start(
            out=xdst.ap()[c], in_=dstt[:]),
            reads=[dstb], writes=[db], is_dma=True, tag=f"ln_out{c}")
        S.mark_output(st2)
    pb.pop_scope()
    if not keep:
        pb.state.pop(dst_name)
    return xnd



def p_beta(pb):
    S = pb.S
    wb = pb.io("L0_attn_Wb", [D, NH0], F32, "in")
    wbb = pb.dbuf("L0_attn_Wb")
    bout = pb.io("BETA", [128, NH0 * NCH], F32, "out")
    bob = pb.dbuf("BETA")
    xnd = pb.state["XN_D"]

    beta_t, beta_b = pb.sb("beta", [128, NH0 * NCH], F32)
    pb.state["BETA_SB"] = (beta_t, beta_b)

    pb.push_scope()
    wbt, wbtb = pb.sb("wbs", [128, 16 * NH0], BF16)
    S.do("gpsimd", lambda e: e.dma_start(
        out=wbt[:].rearrange("p (k h) -> p k h", k=16),
        in_=wb.ap().rearrange("(k p) h -> p k h", p=128)),
        reads=[wbb], writes=[wbtb], is_dma=True, tag="wb_ld")
    pp, ppb = pb.ps("pbeta", [128, NH0])
    t1, t1b = pb.sb("bt1", [128, NH0], F32)
    t2, t2b = pb.sb("bt2", [128, NH0], F32)
    cm1 = pb.cv(-1.0)
    c1 = pb.cv(1.0)
    for tt in range(NCH):
        for k in range(16):
            S.do("tensor", lambda e, k=k, tt=tt: e.matmul(
                pp[:, 0:NH0],
                xnd[k][0][:, HALO + tt * 128: HALO + (tt + 1) * 128],
                wbt[:, k * 16:(k + 1) * 16],
                start=(k == 0), stop=(k == 15)),
                reads=[xnd[k][1], wbtb], writes=[ppb], tag="b_mm")
        S.do("scalar", lambda e: e.activation(t1[:], pp[:, 0:NH0], AF.Exp, scale=cm1),
             reads=[ppb, pb.cbuf], writes=[t1b], tag="b_exp")
        S.do("vector", lambda e: e.tensor_scalar(t2[:], t1[:], c1, None,
                                                 ALU.add),
             reads=[t1b, pb.cbuf], writes=[t2b], tag="b_add1")
        S.do("vector", lambda e, tt=tt: e.reciprocal(
            bass.AP(beta_t, tt * 4, [[beta_t.ap().ap[0][0], 128], [16 * 4, 16]])
            if False else beta_t[:, tt::16], t2[:]),
            reads=[t2b], accwrites=[beta_b], tag="b_rec")
    stb = S.do("sync", lambda e: e.dma_start(out=bout.ap(), in_=beta_t[:]),
               reads=[beta_b], writes=[bob], is_dma=True, tag="b_out")
    S.mark_output(stb)
    pb.pop_scope()



def p_delta1(pb, heads=range(NH0)):
    S = pb.S
    xnd = pb.state["XN_D"]
    beta_t, beta_b = pb.state["BETA_SB"]

    for nm in ("L0_attn_Wq", "L0_attn_Wk", "L0_attn_Wv"):
        pb.io(nm, [D, D], F32, "in")
    for nm in ("L0_attn_conv_wq", "L0_attn_conv_wk", "L0_attn_conv_wv"):
        pb.io(nm, [D, 4], F32, "in")
    spill = pb.io("SPILL", [NH0, NCH, 128, 512], BF16, "out")
    spb = pb.dbuf("SPILL")
    qsp = pb.io("QSP", [NH0, 128, TOK], BF16, "out")
    qspb = pb.dbuf("QSP")
    scal = pb.io("SCAL", [NH0, 128, 32], F32, "out")
    scb = pb.dbuf("SCAL")
    abg = pb.io("ABG", [NH0 * 128, 256], F32, "out")
    abb = pb.dbuf("ABG")
    identb = pb.io("IDENTB", [128, 128], BF16, "in")
    identf = pb.io("IDENTF", [128, 128], F32, "in")
    trilSb = pb.io("TRILS", [128, 128], BF16, "in")
    triuSb = pb.io("TRIUS", [128, 128], BF16, "in")
    triuIb = pb.io("TRIUI", [128, 128], BF16, "in")
    onesb = pb.io("ONESB", [128, 1], BF16, "in")

    pb.push_scope()
    cb = {}
    for nm, h_ in [("IDENTB", identb), ("IDENTF", identf), ("TRILS", trilSb),
                   ("TRIUS", triuSb), ("TRIUI", triuIb), ("ONESB", onesb)]:
        shape = [128, 1] if nm == "ONESB" else [128, 128]
        dt = F32 if nm == "IDENTF" else BF16
        t, b = pb.sb(nm.lower(), shape, dt)
        S.do("sync", lambda e, t=t, h_=h_: e.dma_start(out=t[:], in_=h_.ap()),
             reads=[pb.dbuf(nm)], writes=[b], is_dma=True, tag=f"c_{nm}")
        cb[nm] = (t, b)
    wide = {}
    for nm in ("IDENTB", "TRILS", "TRIUS"):
        t, b = pb.sb(nm.lower() + "4", [128, 512], BF16)
        for j in range(4):
            S.do("vector", lambda e, t=t, j=j, nm=nm: e.tensor_copy(
                t[:, j * 128:(j + 1) * 128], cb[nm][0][:]),
                reads=[cb[nm][1]],
                writes=[b] if j == 0 else [], accwrites=[b] if j else [],
                tag="wide_c")
        wide[nm] = (t, b)
    twoI4, twoI4b = pb.sb("twoI4", [128, 512], BF16)
    S.do("vector", lambda e: e.tensor_scalar(twoI4[:], wide["IDENTB"][0][:],
                                             2.0, None, ALU.mult),
         reads=[wide["IDENTB"][1]], writes=[twoI4b], tag="twoI")
    s01, s01b = pb.sb("s01", [128, 256], BF16)
    S.do("vector", lambda e: e.memset(s01[:, 0:128], 0.0),
         writes=[s01b], tag="s01z")
    S.do("vector", lambda e: e.tensor_copy(s01[:, 128:256], cb["IDENTB"][0][:]),
         reads=[cb["IDENTB"][1]], accwrites=[s01b], tag="s01i")
    s01f, s01fb = pb.sb("s01f", [128, 256], F32)
    S.do("vector", lambda e: e.memset(s01f[:, 0:128], 0.0),
         writes=[s01fb], tag="s01fz")
    S.do("vector", lambda e: e.tensor_copy(s01f[:, 128:256],
                                           cb["IDENTF"][0][:]),
         reads=[cb["IDENTF"][1]], accwrites=[s01fb], tag="s01fi")

    wbl = [pb.sb(f"w{t3}", [128, D], BF16) for t3 in range(3)]
    cwt = [pb.sb(f"cw{t3}", [128, 4], F32) for t3 in range(3)]
    lin = [pb.sb(f"lin{t3}", [128, HTOK], BF16) for t3 in range(2)]
    sg1, sg1b = pb.sb("sg1", [128, HTOK], F32)
    sg2, sg2b = pb.sb("sg2", [128, HTOK], BF16)
    vcs, vcsb = pb.sb("vcs", [128, HTOK], BF16)
    PLN = 3
    P = []
    for _ in range(PLN):
        P.append(dict(
            cvo=[pb.sb(f"cv{t3}", [128, HTOK], BF16) for t3 in range(2)],
            kk=pb.sb("kkal", [128, TOK], BF16),
            vp=pb.sb("vpal", [128, TOK], BF16),
            hsc={nm: pb.sb(nm, [128, 16], F32)
                 for nm in ("rnk", "rnq", "sqb", "bm12n", "sk", "alpha",
                            "isb")},
            tn1=pb.sb("tn1", [128, 32], F32),
            sc2=pb.sb("sc2", [128, 32], F32),
            nrmk=pb.sb("nrmk", [128, 16], F32),
        ))
    CLN = 2
    PKD = 6
    CL = []
    for _ in range(CLN):
        d = dict(
            gg=[pb.sb("gg", [128, 512], BF16) for _ in range(2)],
            ggt=[pb.sb("ggt", [128, 512], BF16) for _ in range(2)],
            yp=pb.sb("yp", [128, 512], BF16),
            alow=pb.sb("alow", [128, 512], BF16),
            araw4=pb.sb("araw4", [128, 512], BF16),
            diag4=pb.sb("diag4", [128, 512], BF16),
            a4=pb.sb("a4", [128, 512], BF16),
            pk=[pb.sb("pk", [128, 512], BF16) for _ in range(PKD)],
            upair=pb.sb("upair", [128, 256], BF16),
            stsb=pb.sb("stsb", [128, 256], BF16),
            stsf=pb.sb("stsf", [128, 256], F32),
            abf=[pb.sb("abf", [128, 128], F32) for _ in range(2)],
            pX=pb.ps("pX", [128, 512]),
            pY=pb.ps("pY", [128, 512]),
            pZ=pb.ps("pZ", [128, 512]),
        )
        CL.append(d)
    pPJ = [pb.ps("pPJ", [128, 512]) for _ in range(2)]

    WQ, WK, WV = (pb.dram["L0_attn_Wq"][0], pb.dram["L0_attn_Wk"][0],
                  pb.dram["L0_attn_Wv"][0])
    WQb, WKb, WVb = (pb.dbuf("L0_attn_Wq"), pb.dbuf("L0_attn_Wk"),
                     pb.dbuf("L0_attn_Wv"))
    CWs = [(pb.dram["L0_attn_conv_wq"][0], pb.dbuf("L0_attn_conv_wq")),
           (pb.dram["L0_attn_conv_wk"][0], pb.dbuf("L0_attn_conv_wk")),
           (pb.dram["L0_attn_conv_wv"][0], pb.dbuf("L0_attn_conv_wv"))]
    c1 = pb.cv(1.0)
    cm1 = pb.cv(-1.0)
    c0 = pb.cv(0.0)
    cmh = pb.cv(-0.5)
    ch_ = pb.cv(0.5)
    ceps = pb.cv(1e-6)
    calpha = pb.cv(1.0 / math.sqrt(128.0))
    heads = list(heads)

    def prep(hi, h):
        L = P[hi % PLN]
        hsc = L["hsc"]
        tn1, tn1b = L["tn1"]
        sc2, sc2b = L["sc2"]
        for t3, (W, Wb_) in enumerate([(WQ, WQb), (WK, WKb), (WV, WVb)]):
            wt, wtb = wbl[t3]
            S.do("gpsimd", lambda e, wt=wt, W=W, h=h: e.dma_start(
                out=wt[:].rearrange("p (k o) -> p k o", k=16),
                in_=W.ap()[:, h * 128:(h + 1) * 128].rearrange(
                    "(k p) o -> p k o", p=128)),
                reads=[Wb_], writes=[wtb], is_dma=True, tag=f"wld{t3}")
            ct, ctb = cwt[t3]
            S.do("gpsimd", lambda e, ct=ct, CW=CWs[t3][0], h=h: e.dma_start(
                out=ct[:], in_=CW.ap()[h * 128:(h + 1) * 128, :]),
                reads=[CWs[t3][1]], writes=[ctb], is_dma=True, tag=f"cwld{t3}")
        yield
        NTT = [(0, 512), (512, 512), (1024, 512), (1536, 512), (2048, 128)]
        for t3 in range(3):
            wt, wtb = wbl[t3]
            lint, linb = lin[t3 % 2]
            for ni, (n0, nl) in enumerate(NTT):
                pp, ppb2 = pPJ[ni % 2]
                for k in range(16):
                    S.do("tensor", lambda e, pp=pp, wt=wt, k=k, n0=n0, nl=nl:
                         e.matmul(pp[:, 0:nl],
                                  wt[:, k * 128:(k + 1) * 128],
                                  xnd[k][0][:, n0:n0 + nl],
                                  start=(k == 0), stop=(k == 15)),
                         reads=[wtb, xnd[k][1]], writes=[ppb2], tag="pj_mm")
                    if k == 7:
                        yield
                S.do("scalar", lambda e, lint=lint, pp=pp, n0=n0, nl=nl:
                     e.copy(lint[:, n0:n0 + nl], pp[:, 0:nl]),
                     reads=[ppb2], accwrites=[linb], tag="pj_ev")
                yield
            cvt, cvb = L["cvo"][t3] if t3 < 2 else (vcs, vcsb)
            ct, ctb = cwt[t3]
            Lc = HTOK - 3
            S.do("vector", lambda e, cvt=cvt, lint=lint, ct=ct, Lc=Lc:
                 e.tensor_scalar(cvt[:, 3:HTOK], lint[:, 0:Lc], ct[:, 0:1],
                                 None, ALU.mult),
                 reads=[linb, ctb], writes=[cvb], tag="cv0")
            for i in (1, 2, 3):
                S.do("vector", lambda e, cvt=cvt, lint=lint, ct=ct, i=i,
                     Lc=Lc: e.scalar_tensor_tensor(
                         cvt[:, 3:HTOK], lint[:, i:i + Lc],
                         ct[:, i:i + 1], cvt[:, 3:HTOK],
                         ALU.mult, ALU.add),
                     reads=[linb, ctb, cvb], writes=[cvb], tag=f"cv{i}")
            yield
            S.do("scalar", lambda e, cvt=cvt: e.activation(
                sg1[:, 3:HTOK], cvt[:, 3:HTOK], AF.Exp, scale=cm1),
                reads=[cvb, pb.cbuf], writes=[sg1b], tag="si_e")
            S.do("scalar", lambda e: e.activation(
                sg1[:, 3:HTOK], sg1[:, 3:HTOK], AF.Ln, bias=c1),
                reads=[pb.cbuf], accwrites=[sg1b], tag="si_l")
            S.do("scalar", lambda e: e.activation(
                sg2[:, 3:HTOK], sg1[:, 3:HTOK], AF.Exp, scale=cm1),
                reads=[sg1b, pb.cbuf], writes=[sg2b], tag="si_s")
            S.do("vector", lambda e, cvt=cvt: e.tensor_tensor(
                cvt[:, 3:HTOK], cvt[:, 3:HTOK], sg2[:, 3:HTOK], ALU.mult),
                reads=[sg2b, cvb], writes=[cvb], tag="si_m")
            yield
        qc_t, qc_b = L["cvo"][0]
        kc_t, kc_b = L["cvo"][1]
        vc_t, vc_b = vcs, vcsb
        st = S.do("gpsimd", lambda e, h=h, qc_t=qc_t: e.dma_start(
            out=qsp.ap()[h], in_=qc_t[:, HALO:HTOK]),
            reads=[qc_b], accwrites=[qspb], is_dma=True, tag="q_sp")
        S.mark_output(st)
        kkt, kkb = L["kk"]
        vpt, vpb = L["vp"]
        for c in range(NCH):
            col = HALO + c * 128
            S.do("scalar", lambda e, kkt=kkt, kc_t=kc_t, col=col, c=c:
                 e.dma_start_transpose(kkt[:, c * 128:(c + 1) * 128],
                                       kc_t[:, col:col + 128]),
                 reads=[kc_b], accwrites=[kkb], is_dma=True, tag="tp_k")
            S.do("sync", lambda e, vpt=vpt, vc_t=vc_t, col=col, c=c:
                 e.dma_start_transpose(vpt[:, c * 128:(c + 1) * 128],
                                       vc_t[:, col:col + 128]),
                 reads=[vc_b], accwrites=[vpb], is_dma=True, tag="tp_v")
        yield
        nk, nkb = L["nrmk"]
        S.do("vector", lambda e, kkt=kkt: e.tensor_tensor(
            sg2[:, 0:TOK], kkt[:], kkt[:], ALU.mult),
            reads=[kkb], writes=[sg2b], tag="nm_sq")
        for c in range(NCH):
            S.do("vector", lambda e, c=c, nk=nk: e.reduce_sum(
                nk[:, c:c + 1], sg2[:, c * 128:(c + 1) * 128], axis=AX.X),
                reads=[sg2b], accwrites=[nkb], tag="nm_rd")
        yield
        ppq, ppqb = pPJ[0]
        S.do("vector", lambda e, qc_t=qc_t: e.tensor_tensor(
            sg2[:, 0:TOK], qc_t[:, HALO:HTOK], qc_t[:, HALO:HTOK], ALU.mult),
            reads=[qc_b], writes=[sg2b], tag="nm_sqq")
        for c in range(NCH):
            S.do("tensor", lambda e, c=c, ppq=ppq: e.matmul(
                ppq[:, c:c + 1], sg2[:, c * 128:(c + 1) * 128],
                cb["ONESB"][0][:], start=True, stop=True),
                reads=[sg2b, cb["ONESB"][1]], writes=[ppqb], tag="nm_mmq")
        yield
        S.do("scalar", lambda e, tn1=tn1, nk=nk: e.activation(
            tn1[:, 0:16], nk[:], AF.Ln, bias=ceps),
            reads=[nkb, pb.cbuf], writes=[tn1b], tag="nm_lnk")
        S.do("scalar", lambda e, tn1=tn1, ppq=ppq: e.activation(
            tn1[:, 16:32], ppq[:, 0:16], AF.Ln, bias=ceps),
            reads=[ppqb, pb.cbuf], accwrites=[tn1b], tag="nm_lnq")
        S.do("scalar", lambda e, tn1=tn1, hsc=hsc: e.activation(
            hsc["rnk"][0][:], tn1[:, 0:16], AF.Exp, scale=cmh),
            reads=[tn1b, pb.cbuf], writes=[hsc["rnk"][1]], tag="nm_ek")
        S.do("scalar", lambda e, tn1=tn1, hsc=hsc: e.activation(
            hsc["rnq"][0][:], tn1[:, 16:32], AF.Exp, scale=cmh),
            reads=[tn1b, pb.cbuf], writes=[hsc["rnq"][1]], tag="nm_eq")
        bsl = beta_t[:, h * 16:(h + 1) * 16]
        S.do("scalar", lambda e, bsl=bsl, tn1=tn1: e.activation(
            tn1[:, 0:16], bsl, AF.Ln, bias=c0),
            reads=[beta_b, pb.cbuf], writes=[tn1b], tag="bd_ln")
        S.do("scalar", lambda e, tn1=tn1, hsc=hsc: e.activation(
            hsc["sqb"][0][:], tn1[:, 0:16], AF.Exp, scale=ch_),
            reads=[tn1b, pb.cbuf], writes=[hsc["sqb"][1]], tag="bd_sqb")
        S.do("scalar", lambda e, tn1=tn1, hsc=hsc: e.activation(
            hsc["isb"][0][:], tn1[:, 0:16], AF.Exp, scale=cmh),
            reads=[tn1b, pb.cbuf], writes=[hsc["isb"][1]], tag="bd_isb")
        S.do("vector", lambda e, hsc=hsc: e.tensor_scalar(
            hsc["bm12n"][0][:], hsc["sqb"][0][:], cm1, None, ALU.mult),
            reads=[hsc["sqb"][1], pb.cbuf], writes=[hsc["bm12n"][1]],
            tag="bd_neg")
        S.do("vector", lambda e, hsc=hsc: e.tensor_tensor(
            hsc["sk"][0][:], hsc["rnk"][0][:], hsc["sqb"][0][:], ALU.mult),
            reads=[hsc["rnk"][1], hsc["sqb"][1]], writes=[hsc["sk"][1]],
            tag="bd_sk")
        S.do("vector", lambda e, hsc=hsc: e.tensor_scalar(
            hsc["alpha"][0][:], hsc["rnq"][0][:], calpha, None, ALU.mult),
            reads=[hsc["rnq"][1], pb.cbuf], writes=[hsc["alpha"][1]],
            tag="bd_al")
        S.do("vector", lambda e, sc2=sc2, hsc=hsc: e.tensor_copy(
            sc2[:, 0:16], hsc["bm12n"][0][:]),
            reads=[hsc["bm12n"][1]], writes=[sc2b], tag="sp_s1")
        S.do("vector", lambda e, sc2=sc2, hsc=hsc: e.tensor_copy(
            sc2[:, 16:32], hsc["alpha"][0][:]),
            reads=[hsc["alpha"][1]], accwrites=[sc2b], tag="sp_s2")
        stsc = S.do("sync", lambda e, h=h, sc2=sc2: e.dma_start(
            out=scal.ap()[h], in_=sc2[:]),
            reads=[sc2b], accwrites=[scb], is_dma=True, tag="sp_sc")
        S.mark_output(stsc)
        yield
        for c in range(NCH):
            sl = slice(c * 128, (c + 1) * 128)
            S.do("scalar", lambda e, kkt=kkt, c=c, sl=sl, hsc=hsc:
                 e.activation(kkt[:, sl], kkt[:, sl], AF.Copy,
                              scale=hsc["sk"][0][:, c:c + 1]),
                 reads=[hsc["sk"][1], kkb], accwrites=[kkb], tag="sc_kk")
            S.do("scalar", lambda e, vpt=vpt, c=c, sl=sl, hsc=hsc:
                 e.activation(vpt[:, sl], vpt[:, sl], AF.Copy,
                              scale=hsc["sqb"][0][:, c:c + 1]),
                 reads=[hsc["sqb"][1], vpb], accwrites=[vpb], tag="sc_vp")
            if c % 4 == 3:
                yield
        yield

    def chain(hi, h):
        L = P[hi % PLN]
        C_ = CL[hi % CLN]
        hsc = L["hsc"]
        qc_t, qc_b = L["cvo"][0]
        kc_t, kc_b = L["cvo"][1]
        kkt, kkb = L["kk"]
        vpt, vpb = L["vp"]
        gg, ggt = C_["gg"], C_["ggt"]
        yp, ypb = C_["yp"]
        alow, alowb = C_["alow"]
        mup, mupb = yp, ypb
        mlow, mlowb = gg[1]
        upair, upairb = C_["upair"]
        stsb, stsbb = C_["stsb"]
        stsf, stsfb = C_["stsf"]
        pX, pXb = C_["pX"]
        pY, pYb = C_["pY"]
        pZ, pZb = C_["pZ"]
        araw4, araw4b = C_["araw4"]
        diag4, diag4b = C_["diag4"]
        a4, a4b = C_["a4"]

        def chain_step(c):
            pkt, pkb = C_["pk"][c % PKD]
            rhs_s, rhs_sb = (s01, s01b) if c == 0 else (stsb, stsbb)
            S.do("tensor", lambda e, pkt=pkt, rhs_s=rhs_s, pZ=pZ: e.matmul(
                pZ[:, 256:512], pkt[:, 128:256], rhs_s[:], start=True,
                stop=True),
                reads=[pkb, rhs_sb], writes=[pZb], tag="ch_ws")
            S.do("vector", lambda e, pkt=pkt, c=c, hsc=hsc, upair=upair,
                 pZ=pZ: e.scalar_tensor_tensor(
                     upair[:, 0:128], pZ[:, 256:384],
                     hsc["bm12n"][0][:, c:c + 1],
                     pkt[:, 256:384], ALU.mult, ALU.add),
                 reads=[pZb, hsc["bm12n"][1], pkb], writes=[upairb],
                 tag="ch_up0")
            S.do("vector", lambda e, c=c, hsc=hsc, upair=upair, pZ=pZ:
                 e.tensor_scalar(
                     upair[:, 128:256], pZ[:, 384:512],
                     hsc["bm12n"][0][:, c:c + 1], None, ALU.mult),
                 reads=[pZb, hsc["bm12n"][1]], accwrites=[upairb],
                 tag="ch_up1")
            S.do("tensor", lambda e, pkt=pkt, upair=upair, pZ=pZ: e.matmul(
                pZ[:, 0:256], pkt[:, 0:128], upair[:], start=True,
                stop=True),
                reads=[pkb, upairb], writes=[pZb], tag="ch_st")
            acc_in, acc_inb = (s01f, s01fb) if c == 0 else (stsf, stsfb)
            S.do("vector", lambda e, acc_in=acc_in, stsf=stsf, pZ=pZ:
                 e.tensor_tensor(stsf[:], acc_in[:], pZ[:, 0:256], ALU.add),
                 reads=[acc_inb, pZb], writes=[stsfb], tag="ch_acc")
            S.do("scalar", lambda e, stsb=stsb, stsf=stsf: e.copy(
                stsb[:], stsf[:]),
                reads=[stsfb], writes=[stsbb], tag="ch_ev")

        pend = []
        for g in range(4):
            cs = [4 * g + j for j in range(4)]
            for j, c in enumerate(cs):
                col = HALO + c * 128
                jj = slice(j * 128, (j + 1) * 128)
                S.do("tensor", lambda e, kc_t=kc_t, col=col, jj=jj, pX=pX:
                     e.matmul(pX[:, jj], kc_t[:, col:col + 128],
                              kc_t[:, col:col + 128], start=True, stop=True),
                     reads=[kc_b], writes=[pXb], tag="A_mm")
                S.do("scalar", lambda e, c=c, jj=jj, hsc=hsc, diag4=diag4:
                     e.activation(diag4[:, jj], cb["IDENTB"][0][:], AF.Copy,
                                  scale=hsc["sk"][0][:, c:c + 1]),
                     reads=[cb["IDENTB"][1], hsc["sk"][1]],
                     accwrites=[diag4b], tag="dg_b")
            S.do("vector", lambda e, araw4=araw4, pX=pX: e.tensor_copy(
                araw4[:], pX[:]),
                reads=[pXb], writes=[araw4b], tag="A_ev")
            for j, c in enumerate(cs):
                jj = slice(j * 128, (j + 1) * 128)
                S.do("tensor", lambda e, araw4=araw4, diag4=diag4, jj=jj,
                     pY=pY: e.matmul(pY[:, jj], araw4[:, jj], diag4[:, jj],
                                     start=True, stop=True),
                     reads=[araw4b, diag4b], writes=[pYb], tag="AD_mm")
                S.do("scalar", lambda e, c=c, jj=jj, hsc=hsc, a4=a4, pY=pY:
                     e.activation(a4[:, jj], pY[:, jj], AF.Copy,
                                  scale=hsc["sk"][0][:, c:c + 1]),
                     reads=[pYb, hsc["sk"][1]], accwrites=[a4b], tag="A_sc")
            S.do("vector", lambda e, mup=mup, a4=a4: e.tensor_tensor(
                mup[:], a4[:], wide["TRIUS"][0][:], ALU.mult),
                reads=[a4b, wide["TRIUS"][1]], writes=[mupb], tag="nb_up")
            S.do("vector", lambda e, mlow=mlow, a4=a4: e.tensor_tensor(
                mlow[:], a4[:], wide["TRILS"][0][:], ALU.mult),
                reads=[a4b, wide["TRILS"][1]], writes=[mlowb], tag="nb_lo")
            g0t, g0tb = gg[0]
            gt0t, gt0tb = ggt[0]
            S.do("vector", lambda e, g0t=g0t, mup=mup: e.tensor_tensor(
                g0t[:], wide["IDENTB"][0][:], mup[:], ALU.subtract),
                reads=[mupb, wide["IDENTB"][1]], writes=[g0tb], tag="nb_g0")
            S.do("vector", lambda e, gt0t=gt0t, mlow=mlow: e.tensor_tensor(
                gt0t[:], wide["IDENTB"][0][:], mlow[:], ALU.subtract),
                reads=[mlowb, wide["IDENTB"][1]], writes=[gt0tb], tag="nb_gt0")
            S.do("vector", lambda e, alow=alow, mlow=mlow: e.tensor_tensor(
                alow[:], wide["IDENTB"][0][:], mlow[:], ALU.add),
                reads=[mlowb, wide["IDENTB"][1]], writes=[alowb], tag="nb_al")
            yield
            for it in range(6):
                gsrc, gsrcb = gg[it % 2] if it > 0 else gg[0]
                gtsrc, gtsrcb = ggt[it % 2] if it > 0 else ggt[0]
                gdst, gdstb = gg[(it + 1) % 2]
                gtdst, gtdstb = ggt[(it + 1) % 2]
                for j in range(4):
                    jj = slice(j * 128, (j + 1) * 128)
                    S.do("tensor", lambda e, gsrc=gsrc, alow=alow, jj=jj,
                         pX=pX: e.matmul(pX[:, jj], alow[:, jj], gsrc[:, jj],
                                         start=True, stop=True),
                         reads=[alowb, gsrcb], writes=[pXb], tag="nw_a")
                S.do("vector", lambda e, yp=yp, pX=pX: e.tensor_tensor(
                    yp[:], twoI4[:], pX[:], ALU.subtract),
                    reads=[pXb, twoI4b], writes=[ypb], tag="nw_y")
                for j in range(4):
                    jj = slice(j * 128, (j + 1) * 128)
                    S.do("tensor", lambda e, gtsrc=gtsrc, yp=yp, jj=jj,
                         pY=pY: e.matmul(pY[:, jj], gtsrc[:, jj], yp[:, jj],
                                         start=True, stop=True),
                         reads=[gtsrcb, ypb], writes=[pYb], tag="nw_b")
                S.do("scalar", lambda e, gdst=gdst, pY=pY: e.copy(
                    gdst[:], pY[:]),
                    reads=[pYb], writes=[gdstb], tag="nw_be")
                if it < 5:
                    for j in range(4):
                        jj = slice(j * 128, (j + 1) * 128)
                        S.do("tensor", lambda e, gtsrc=gtsrc, yp=yp, jj=jj,
                             pX=pX: e.matmul(pX[:, jj], yp[:, jj],
                                             gtsrc[:, jj],
                                             start=True, stop=True),
                             reads=[ypb, gtsrcb], writes=[pXb], tag="nw_c")
                    S.do("vector", lambda e, gtdst=gtdst, pX=pX:
                         e.tensor_copy(gtdst[:], pX[:]),
                         reads=[pXb], writes=[gtdstb], tag="nw_ce")
                if 1 <= it <= 4 and pend:
                    chain_step(pend.pop(0))
                yield
            gfin, gfinb = gg[0]
            for j, c in enumerate(cs):
                sl = slice(c * 128, (c + 1) * 128)
                jj = slice(j * 128, (j + 1) * 128)
                pkt, pkb = C_["pk"][c % PKD]
                S.do("vector", lambda e, pkt=pkt, kkt=kkt, c=c, sl=sl,
                     hsc=hsc: e.tensor_scalar(
                         pkt[:, 0:128], kkt[:, sl],
                         hsc["isb"][0][:, c:c + 1], None, ALU.mult),
                     reads=[kkb, hsc["isb"][1]], writes=[pkb], tag="pk_kn")
                S.do("tensor", lambda e, gfin=gfin, vpt=vpt, sl=sl, jj=jj,
                     pX=pX: e.matmul(pX[:, jj], gfin[:, jj], vpt[:, sl],
                                     start=True, stop=True),
                     reads=[gfinb, vpb], writes=[pXb], tag="u0_mm")
                S.do("tensor", lambda e, kkt=kkt, gfin=gfin, sl=sl, jj=jj,
                     pY=pY: e.matmul(pY[:, jj], kkt[:, sl], gfin[:, jj],
                                     start=True, stop=True),
                     reads=[kkb, gfinb], writes=[pYb], tag="wd_mm")
            yield
            for j, c in enumerate(cs):
                jj = slice(j * 128, (j + 1) * 128)
                pkt, pkb = C_["pk"][c % PKD]
                S.do("scalar", lambda e, pkt=pkt, c=c, hsc=hsc, jj=jj, pX=pX:
                     e.activation(pkt[:, 256:384], pX[:, jj], AF.Copy,
                                  scale=hsc["sqb"][0][:, c:c + 1]),
                     reads=[pXb, hsc["sqb"][1]], accwrites=[pkb], tag="u0_ev")
                S.do("scalar", lambda e, pkt=pkt, jj=jj, pY=pY: e.copy(
                    pkt[:, 128:256], pY[:, jj]),
                    reads=[pYb], accwrites=[pkb], tag="wd_ev")
            for j, c in enumerate(cs):
                col = HALO + c * 128
                jj = slice(j * 128, (j + 1) * 128)
                S.do("tensor", lambda e, kc_t=kc_t, qc_t=qc_t, col=col,
                     jj=jj, pX=pX: e.matmul(
                         pX[:, jj], kc_t[:, col:col + 128],
                         qc_t[:, col:col + 128], start=True, stop=True),
                     reads=[kc_b, qc_b], writes=[pXb], tag="lq_mm")
            yield
            for j, c in enumerate(cs):
                jj = slice(j * 128, (j + 1) * 128)
                pkt, pkb = C_["pk"][c % PKD]
                S.do("vector", lambda e, pkt=pkt, c=c, hsc=hsc, jj=jj,
                     pX=pX: e.scalar_tensor_tensor(
                         pkt[:, 384:512], pX[:, jj],
                         hsc["rnk"][0][:, c:c + 1],
                         cb["TRIUI"][0][:], ALU.mult, ALU.mult),
                     reads=[pXb, hsc["rnk"][1], cb["TRIUI"][1]],
                     accwrites=[pkb], tag="lq_ev")
                stp = S.do("sync", lambda e, pkt=pkt, h=h, c=c: e.dma_start(
                    out=spill.ap()[h, c], in_=pkt[:]),
                    reads=[pkb], accwrites=[spb], is_dma=True, tag="pk_sp")
                S.mark_output(stp)
            pend.extend(cs)
            yield
        while pend:
            chain_step(pend.pop(0))
            yield
        bf_t, bf_b = C_["abf"][0]
        af_t, af_b = C_["abf"][1]
        S.do("vector", lambda e, bf_t=bf_t, stsf=stsf: e.tensor_copy(
            bf_t[:], stsf[:, 0:128]),
            reads=[stsfb], writes=[bf_b], tag="ab_b")
        S.do("vector", lambda e, af_t=af_t, stsf=stsf: e.tensor_copy(
            af_t[:], stsf[:, 128:256]),
            reads=[stsfb], writes=[af_b], tag="ab_a")
        S.do("tensor", lambda e, af_t=af_t, pX=pX: e.transpose(
            pX[:, 0:128], af_t[:], cb["IDENTF"][0][:]),
            reads=[af_b, cb["IDENTF"][1]], writes=[pXb], tag="ab_tp")
        S.do("vector", lambda e, af_t=af_t, pX=pX: e.tensor_copy(
            af_t[:], pX[:, 0:128]),
            reads=[pXb], writes=[af_b], tag="ab_te")
        sta = S.do("sync", lambda e, af_t=af_t, h=h: e.dma_start(
            out=abg.ap()[h * 128:(h + 1) * 128, 0:128], in_=af_t[:]),
            reads=[af_b], accwrites=[abb], is_dma=True, tag="ab_oa")
        stb2 = S.do("sync", lambda e, bf_t=bf_t, h=h: e.dma_start(
            out=abg.ap()[h * 128:(h + 1) * 128, 128:256], in_=bf_t[:]),
            reads=[bf_b], accwrites=[abb], is_dma=True, tag="ab_ob")
        S.mark_output(sta)
        S.mark_output(stb2)
        yield

    n = len(heads)
    preps = [prep(i, heads[i]) for i in range(n)]
    chains = [chain(i, heads[i]) for i in range(n)]
    for _ in preps[0]:
        pass
    for h in range(n):
        finish = chains[h - 1] if h >= 1 else None
        cur = chains[h]
        nxt = preps[h + 1] if h + 1 < n else None
        alive = {0: finish is not None, 1: True, 2: nxt is not None}
        gens = {0: finish, 1: cur, 2: nxt}
        while alive[0] or alive[2]:
            for j in (0, 1, 2):
                if alive[j]:
                    try:
                        next(gens[j])
                    except StopIteration:
                        alive[j] = False
    for _ in chains[n - 1]:
        pass
    pb.pop_scope()




def p_moe(pb):
    S = pb.S
    xnd = pb.state["XN_D"]
    xin = pb.io("XIN", [HTOK, D], F32, "in")
    xinb = pb.dbuf("XIN")
    gw = pb.io("L0_ffn_gate_w", [D, NE], F32, "in")
    gb_ = pb.io("L0_ffn_gate_b", [1, NE], F32, "in")
    ew = pb.io("L0_ffn_expert_w", [NE, D, D], F32, "in")
    eb_ = pb.io("L0_ffn_expert_b", [1, NE * D], F32, "in")
    x1p = pb.io("X1P", [TOK, D], F32, "out")
    x1pb = pb.dbuf("X1P")
    onesr = pb.io("ONESR", [1, 128], BF16, "in")

    pb.push_scope()
    gwt, gwtb = pb.sb("gw", [128, 16 * NE], BF16)
    S.do("gpsimd", lambda e: e.dma_start(
        out=gwt[:].rearrange("p (k h) -> p k h", k=16),
        in_=gw.ap().rearrange("(k p) h -> p k h", p=128)),
        reads=[pb.dbuf("L0_ffn_gate_w")], writes=[gwtb], is_dma=True,
        tag="mg_w")
    gbt, gbtb = pb.sb("gb", [1, NE], BF16)
    S.do("gpsimd", lambda e: e.dma_start(out=gbt[:], in_=gb_.ap()),
         reads=[pb.dbuf("L0_ffn_gate_b")], writes=[gbtb], is_dma=True,
         tag="mg_b")
    ebt, ebtb = pb.sb("eb", [1, NE * D], BF16)
    S.do("gpsimd", lambda e: e.dma_start(out=ebt[:], in_=eb_.ap()),
         reads=[pb.dbuf("L0_ffn_expert_b")], writes=[ebtb], is_dma=True,
         tag="mg_eb")
    onr, onrb = pb.sb("onr", [1, 128], BF16)
    S.do("sync", lambda e: e.dma_start(out=onr[:], in_=onesr.ap()),
         reads=[pb.dbuf("ONESR")], writes=[onrb], is_dma=True, tag="mg_on")
    pg, pgb = pb.ps("pg", [128, NE])
    wsel = [pb.sb(f"ws{tt}", [128, NE], F32) for tt in range(NCH)]
    gt1, gt1b = pb.sb("gt1", [128, NE], F32)
    gt2, gt2b = pb.sb("gt2", [128, NE], F32)
    gs, gsb = pb.sb("gs", [128, 4], F32)
    cm1 = pb.cv(-1.0)
    c1 = pb.cv(1.0)
    for tt in range(NCH):
        for k in range(16):
            S.do("tensor", lambda e, k=k, tt=tt: e.matmul(
                pg[:, 0:NE],
                xnd[k][0][:, HALO + tt * 128: HALO + (tt + 1) * 128],
                gwt[:, k * NE:(k + 1) * NE],
                start=(k == 0), stop=False),
                reads=[xnd[k][1], gwtb], writes=[pgb], tag="mg_mm")
        S.do("tensor", lambda e: e.matmul(
            pg[:, 0:NE], onr[:], gbt[:], start=False, stop=True),
            reads=[onrb, gbtb], writes=[pgb], tag="mg_bias")
        S.do("vector", lambda e: e.reduce_max(gs[:, 0:1], pg[:, 0:NE],
                                              axis=AX.X),
             reads=[pgb], writes=[gsb], tag="mg_mx")
        S.do("vector", lambda e: e.tensor_scalar(gs[:, 1:2], gs[:, 0:1], cm1,
                                                 None, ALU.mult),
             reads=[gsb, pb.cbuf], accwrites=[gsb], tag="mg_nmx")
        S.do("scalar", lambda e: e.activation(gt1[:], pg[:, 0:NE], AF.Exp,
                                              bias=gs[:, 1:2]),
             reads=[pgb, gsb], writes=[gt1b], tag="mg_exp")
        S.do("vector", lambda e: e.reduce_sum(gs[:, 2:3], gt1[:], axis=AX.X),
             reads=[gt1b], writes=[gsb], tag="mg_sm")
        S.do("vector", lambda e: e.reciprocal(gs[:, 3:4], gs[:, 2:3]),
             reads=[gsb], accwrites=[gsb], tag="mg_rc")
        S.do("vector", lambda e: e.tensor_scalar(gt1[:], gt1[:], gs[:, 3:4],
                                                 None, ALU.mult),
             reads=[gt1b, gsb], writes=[gt1b], tag="mg_pr")
        S.do("vector", lambda e: e.reduce_max(gs[:, 0:1], gt1[:], axis=AX.X),
             reads=[gt1b], writes=[gsb], tag="mt_m1")
        S.do("vector", lambda e: e.tensor_scalar(gt2[:], gt1[:], gs[:, 0:1],
                                                 None, ALU.is_ge),
             reads=[gt1b, gsb], writes=[gt2b], tag="mt_ge1")
        S.do("vector", lambda e: e.tensor_scalar(gt2[:], gt2[:], cm1, c1,
                                                 ALU.mult, ALU.add),
             reads=[gt2b, pb.cbuf], writes=[gt2b], tag="mt_inv")
        S.do("vector", lambda e: e.tensor_tensor(gt2[:], gt1[:], gt2[:],
                                                 ALU.mult),
             reads=[gt1b, gt2b], writes=[gt2b], tag="mt_pm")
        S.do("vector", lambda e: e.reduce_max(gs[:, 1:2], gt2[:], axis=AX.X),
             reads=[gt2b], writes=[gsb], tag="mt_m2")
        S.do("vector", lambda e: e.tensor_scalar(gt2[:], gt1[:], gs[:, 1:2],
                                                 None, ALU.is_ge),
             reads=[gt1b, gsb], writes=[gt2b], tag="mt_ge2")
        S.do("vector", lambda e, tt=tt: e.tensor_tensor(
            wsel[tt][0][:], gt1[:], gt2[:], ALU.mult),
            reads=[gt1b, gt2b], writes=[wsel[tt][1]], tag="mt_w")
    ewt = [pb.sb("ew", [128, 16 * 512], BF16) for _ in range(2)]
    acc = [pb.sb(f"acc{tt}", [128, 512], F32) for tt in range(NCH)]
    xt = [pb.sb("xt", [128, 512], F32) for _ in range(2)]
    pe_ = [pb.ps("pe", [128, 512]) for _ in range(2)]
    for oq in range(4):
        oc0 = oq * 512
        for ei in range(NE):
            wt, wtb = ewt[ei % 2]
            S.do("gpsimd", lambda e, wt=wt, ei=ei, oc0=oc0: e.dma_start(
                out=wt[:].rearrange("p (k o) -> p k o", k=16),
                in_=ew.ap()[ei, :, oc0:oc0 + 512].rearrange(
                    "(k p) o -> p k o", p=128)),
                reads=[pb.dbuf("L0_ffn_expert_w")], writes=[wtb], is_dma=True,
                tag="me_w")
            for tt in range(NCH):
                pp, ppb = pe_[tt % 2]
                for k in range(16):
                    S.do("tensor", lambda e, pp=pp, wt=wt, k=k, tt=tt:
                         e.matmul(pp[:],
                                  xnd[k][0][:, HALO + tt * 128:
                                            HALO + (tt + 1) * 128],
                                  wt[:, k * 512:(k + 1) * 512],
                                  start=(k == 0), stop=False),
                         reads=[xnd[k][1], wtb], writes=[ppb], tag="me_mm")
                S.do("tensor", lambda e, pp=pp, ei=ei, oc0=oc0: e.matmul(
                    pp[:], onr[:], ebt[:, ei * D + oc0:ei * D + oc0 + 512],
                    start=False, stop=True),
                    reads=[onrb, ebtb], writes=[ppb], tag="me_b")
                at, ab = acc[tt]
                if ei == 0:
                    xtt, xttb = xt[tt % 2]
                    S.do("sync", lambda e, xtt=xtt, tt=tt, oc0=oc0:
                         e.dma_start(out=xtt[:], in_=xin.ap()[
                             HALO + tt * 128:HALO + (tt + 1) * 128,
                             oc0:oc0 + 512]),
                         reads=[xinb], writes=[xttb], is_dma=True, tag="me_x")
                    S.do("vector", lambda e, at=at, pp=pp, xtt=xtt, tt=tt, ei=ei:
                         e.scalar_tensor_tensor(
                             at[:], pp[:], wsel[tt][0][:, ei:ei + 1], xtt[:],
                             ALU.mult, ALU.add),
                         reads=[ppb, wsel[tt][1], xttb], writes=[ab],
                         tag="me_a0")
                else:
                    S.do("vector", lambda e, at=at, pp=pp, tt=tt, ei=ei:
                         e.scalar_tensor_tensor(
                             at[:], pp[:], wsel[tt][0][:, ei:ei + 1], at[:],
                             ALU.mult, ALU.add),
                         reads=[ppb, wsel[tt][1], ab], writes=[ab],
                         tag="me_acc")
        for tt in range(NCH):
            at, ab = acc[tt]
            sto = S.do("sync", lambda e, at=at, tt=tt, oc0=oc0: e.dma_start(
                out=x1p.ap()[tt * 128:(tt + 1) * 128, oc0:oc0 + 512],
                in_=at[:]),
                reads=[ab], accwrites=[x1pb], is_dma=True, tag="me_o")
            S.mark_output(sto)
    pb.pop_scope()



def p_gproj(pb):
    S = pb.S
    xnd = pb.state["XN_D"]
    wg = pb.io("L0_attn_Wg", [D, D], F32, "in")
    gtd = pb.io("GT", [NCH, 128, D], BF16, "out")
    gtdb = pb.dbuf("GT")
    pb.push_scope()
    gt = []
    for tt in range(NCH):
        t, b = pb.sb(f"gt{tt}", [128, D], BF16)
        gt.append((t, b))
    wgt2 = [pb.sb("wg2", [128, 16 * 512], BF16) for _ in range(2)]
    pp_ = [pb.ps("pgj", [128, 512]) for _ in range(2)]
    for oc in range(4):
        wt, wtb = wgt2[oc % 2]
        S.do("gpsimd", lambda e, wt=wt, oc=oc: e.dma_start(
            out=wt[:].rearrange("p (k o) -> p k o", k=16),
            in_=wg.ap()[:, oc * 512:(oc + 1) * 512].rearrange(
                "(k p) o -> p k o", p=128)),
            reads=[pb.dbuf("L0_attn_Wg")], writes=[wtb], is_dma=True,
            tag="gp_w2")
        for tt in range(NCH):
            pp, ppb = pp_[tt % 2]
            for k in range(16):
                S.do("tensor", lambda e, pp=pp, wt=wt, k=k, tt=tt: e.matmul(
                    pp[:],
                    xnd[k][0][:, HALO + tt * 128: HALO + (tt + 1) * 128],
                    wt[:, k * 512:(k + 1) * 512],
                    start=(k == 0), stop=(k == 15)),
                    reads=[xnd[k][1], wtb], writes=[ppb], tag="gp_mm")
            t, b = gt[tt]
            S.do("scalar", lambda e, t=t, pp=pp, oc=oc: e.copy(
                t[:, oc * 512:(oc + 1) * 512], pp[:]),
                reads=[ppb], accwrites=[b], tag="gp_ev")
    for tt in range(NCH):
        t, b = gt[tt]
        sg = S.do("sync", lambda e, t=t, tt=tt: e.dma_start(
            out=gtd.ap()[tt], in_=t[:]),
            reads=[b], accwrites=[gtdb], is_dma=True, tag="gp_out")
        S.mark_output(sg)
    pb.pop_scope()




def p_prefix(pb, heads=range(NH0), dump=False):
    S = pb.S
    aba = pb.io("AB_ALL", [4, NH0 * 128, 256], F32, "in")
    abab = pb.dbuf("AB_ALL")
    sel4 = pb.io("SEL4", [128, 4], F32, "in")
    sel4b_ = pb.dbuf("SEL4")
    sins = []
    for h in range(NH0):
        t, b = pb.sb(f"sin{h}", [128, 128], BF16)
        sins.append((t, b))
    pb.state["SIN"] = sins
    pb.push_scope()
    selt, selb = pb.sb("sel", [128, 4], F32)
    S.do("sync", lambda e: e.dma_start(out=selt[:], in_=sel4.ap()),
         reads=[sel4b_], writes=[selb], is_dma=True, tag="px_sel")
    ab = [pb.sb(f"ab{p}", [128, 256], F32) for p in range(3)]
    tf = [pb.sb(f"tf{q}", [128, 128], F32) for q in range(3)]
    sf, sfb = pb.sb("sf", [128, 128], F32)
    pp, ppb = pb.ps("ppx", [128, 128])
    for h in heads:
        for p in range(3):
            t, b = ab[p]
            S.do("sync", lambda e, t=t, p=p, h=h: e.dma_start(
                out=t[:], in_=aba.ap()[p, h * 128:(h + 1) * 128, :]),
                reads=[abab], writes=[b], is_dma=True, tag="px_ld")
        S.do("vector", lambda e: e.tensor_copy(tf[0][0][:], ab[0][0][:, 128:256]),
             reads=[ab[0][1]], writes=[tf[0][1]], tag="px_t1")
        for q in (1, 2):
            S.do("tensor", lambda e, q=q: e.matmul(
                pp[:, 0:128], ab[q][0][:, 0:128], tf[q - 1][0][:], start=True,
                stop=True),
                reads=[ab[q][1], tf[q - 1][1]], writes=[ppb], tag="px_mm")
            S.do("vector", lambda e, q=q: e.tensor_tensor(
                tf[q][0][:], ab[q][0][:, 128:256], pp[:, 0:128], ALU.add),
                reads=[ab[q][1], ppb], writes=[tf[q][1]], tag="px_t")
        S.do("vector", lambda e: e.tensor_scalar(
            sf[:], tf[0][0][:], selt[:, 1:2], None, ALU.mult),
            reads=[tf[0][1], selb], writes=[sfb], tag="px_s1")
        for q in (1, 2):
            S.do("vector", lambda e, q=q: e.scalar_tensor_tensor(
                sf[:], tf[q][0][:], selt[:, q + 1:q + 2], sf[:],
                ALU.mult, ALU.add),
                reads=[tf[q][1], selb, sfb], writes=[sfb], tag=f"px_s{q+1}")
        t, b = sins[h]
        S.do("vector", lambda e, t=t: e.tensor_copy(t[:], sf[:]),
             reads=[sfb], writes=[b], tag="px_out")
        if dump:
            sd = pb.io("SIN_DBG", [NH0, 128, 128], F32, "out")
            std = S.do("sync", lambda e, h=h: e.dma_start(
                out=sd.ap()[h], in_=sf[:]),
                reads=[sfb], accwrites=[pb.dbuf("SIN_DBG")], is_dma=True,
                tag="px_dbg")
            S.mark_output(std)
    pb.pop_scope()



def drive(gens):
    gens = [g for g in gens if g is not None]
    while gens:
        nxt = []
        for g in gens:
            try:
                next(g)
                nxt.append(g)
            except StopIteration:
                pass
        gens = nxt


def p_delta2(pb, heads=range(NH0)):
    S = pb.S
    spill = pb.io("SPILL", [NH0, NCH, 128, 512], BF16, "in")
    spb = pb.dbuf("SPILL")
    qsp = pb.io("QSP", [NH0, 128, TOK], BF16, "in")
    qspb = pb.dbuf("QSP")
    scal = pb.io("SCAL", [NH0, 128, 32], F32, "in")
    scb = pb.dbuf("SCAL")
    ogd = pb.io("OG_D", [NH0, 128, TOK], BF16, "out")
    ogdb = pb.dbuf("OG_D")
    sins = pb.state["SIN"]
    gtd = pb.io("GT", [NCH, 128, D], BF16, "in")
    gtdb = pb.dbuf("GT")

    pb.push_scope()
    gt = []
    for tt in range(NCH):
        t, b = pb.sb(f"gtl{tt}", [128, D], BF16)
        S.do("sync", lambda e, t=t, tt=tt: e.dma_start(
            out=t[:], in_=gtd.ap()[tt]),
            reads=[gtdb], writes=[b], is_dma=True, tag="gt_ld")
        gt.append((t, b))
    LN = 2
    lanes = []
    for _ in range(LN):
        ln = dict(
            pk=[pb.sb("pk2", [128, 512], BF16) for _ in range(2)],
            qc2=pb.sb("qc2", [128, TOK], BF16),
            sc=pb.sb("sc", [128, 32], F32),
            ssf=pb.sb("ssf", [128, 128], F32),
            ssb=pb.sb("ssb", [128, 128], BF16),
            ut=pb.sb("ut", [128, 128], BF16),
            oall=pb.sb("oall", [128, TOK], BF16),
            nrm=pb.sb("nrm", [128, 16], F32),
            sqs=pb.sb("sqs", [128, 128], F32),
            fsc=pb.sb("fsc", [128, 16], F32),
            e1=pb.sb("e1", [128, 128], F32),
            e2=pb.sb("e2", [128, 128], F32),
            ogt=pb.sb("ogt", [128, 128], BF16),
            ogdt=pb.sb("ogdt", [128, TOK], BF16),
            pW=pb.ps("pW2", [128, 128]),
            pS=pb.ps("pS2", [128, 128]),
            pO=pb.ps("pO2", [128, 128]),
        )
        lanes.append(ln)
    cm1 = pb.cv(-1.0)
    c1 = pb.cv(1.0)
    ceps = pb.cv(1e-5)
    cmh = pb.cv(-0.5)
    ci128 = pb.cv(1.0 / 128.0)

    def run_head(hi, h):
        ln = lanes[hi % LN]
        qc2, qc2b = ln["qc2"]
        sc, scb2 = ln["sc"]
        ssf, ssfb = ln["ssf"]
        ssb, ssbb = ln["ssb"]
        ut, utb = ln["ut"]
        oall, oallb = ln["oall"]
        nrm, nrmb = ln["nrm"]
        sqs, sqsb = ln["sqs"]
        fsc, fscb = ln["fsc"]
        e1, e1b = ln["e1"]
        e2, e2b = ln["e2"]
        ogt, ogtb = ln["ogt"]
        ogdt, ogdtb = ln["ogdt"]
        pW, pWb = ln["pW"]
        pS, pSb = ln["pS"]
        pO, pOb = ln["pO"]
        S.do("sync", lambda e, h=h, qc2=qc2: e.dma_start(out=qc2[:],
                                                         in_=qsp.ap()[h]),
             reads=[qspb], writes=[qc2b], is_dma=True, tag="d2_q")
        S.do("sync", lambda e, h=h, sc=sc: e.dma_start(out=sc[:],
                                                       in_=scal.ap()[h]),
             reads=[scb], writes=[scb2], is_dma=True, tag="d2_sc")
        sint, sinb = sins[h]
        for c in range(NCH):
            pkt, pkb = ln["pk"][c % 2]
            S.do("sync", lambda e, pkt=pkt, h=h, c=c: e.dma_start(
                out=pkt[:], in_=spill.ap()[h, c]),
                reads=[spb], writes=[pkb], is_dma=True, tag="d2_pk")
            s0t, s0b = (sint, sinb) if c == 0 else (ssb, ssbb)
            S.do("tensor", lambda e, pkt=pkt, s0t=s0t, pW=pW: e.matmul(
                pW[:, 0:128], pkt[:, 128:256], s0t[:], start=True, stop=True),
                reads=[pkb, s0b], writes=[pWb], tag="d2_ws")
            S.do("vector", lambda e, pkt=pkt, c=c, ut=ut, pW=pW, sc=sc:
                 e.scalar_tensor_tensor(
                     ut[:], pW[:, 0:128], sc[:, c:c + 1], pkt[:, 256:384],
                     ALU.mult, ALU.add),
                 reads=[pWb, scb2, pkb], writes=[utb], tag="d2_u")
            S.do("tensor", lambda e, s0t=s0t, c=c, qc2=qc2, pO=pO: e.matmul(
                pO[:, 0:128], qc2[:, c * 128:(c + 1) * 128], s0t[:],
                start=True, stop=False),
                reads=[qc2b, s0b], writes=[pOb], tag="d2_o1")
            S.do("tensor", lambda e, pkt=pkt, ut=ut, pO=pO: e.matmul(
                pO[:, 0:128], pkt[:, 384:512], ut[:], start=False, stop=True),
                reads=[pkb, utb], writes=[pOb], tag="d2_o2")
            S.do("tensor", lambda e, pkt=pkt, ut=ut, pS=pS: e.matmul(
                pS[:, 0:128], pkt[:, 0:128], ut[:], start=True, stop=True),
                reads=[pkb, utb], writes=[pSb], tag="d2_su")
            if c == 0:
                S.do("vector", lambda e, sint=sint, ssf=ssf: e.tensor_copy(
                    ssf[:], sint[:]),
                    reads=[sinb], writes=[ssfb], tag="d2_wid")
            S.do("vector", lambda e, ssf=ssf, pS=pS: e.tensor_tensor(
                ssf[:], ssf[:], pS[:, 0:128], ALU.add),
                reads=[ssfb, pSb], writes=[ssfb], tag="d2_sa")
            S.do("scalar", lambda e, ssb=ssb, ssf=ssf: e.copy(ssb[:], ssf[:]),
                 reads=[ssfb], writes=[ssbb], tag="d2_se")
            S.do("scalar", lambda e, c=c, oall=oall, pO=pO: e.copy(
                oall[:, c * 128:(c + 1) * 128], pO[:, 0:128]),
                reads=[pOb], accwrites=[oallb], tag="d2_oe")
            S.do("vector", lambda e, c=c, oall=oall, sqs=sqs: e.tensor_tensor(
                sqs[:], oall[:, c * 128:(c + 1) * 128],
                oall[:, c * 128:(c + 1) * 128], ALU.mult),
                reads=[oallb], writes=[sqsb], tag="d2_nr")
            S.do("vector", lambda e, c=c, sqs=sqs, nrm=nrm: e.reduce_sum(
                nrm[:, c:c + 1], sqs[:], axis=AX.X),
                reads=[sqsb], accwrites=[nrmb], tag="d2_nr2")
            yield
        S.do("vector", lambda e, fsc=fsc, sc=sc: e.tensor_tensor(
            fsc[:], sc[:, 16:32], sc[:, 16:32], ALU.mult),
            reads=[scb2], writes=[fscb], tag="d2_a2")
        S.do("vector", lambda e, fsc=fsc, nrm=nrm: e.tensor_tensor(
            fsc[:], fsc[:], nrm[:], ALU.mult),
            reads=[fscb, nrmb], writes=[fscb], tag="d2_m2")
        S.do("vector", lambda e, fsc=fsc: e.tensor_scalar(
            fsc[:], fsc[:], ci128, None, ALU.mult),
            reads=[fscb, pb.cbuf], writes=[fscb], tag="d2_d128")
        S.do("scalar", lambda e, fsc=fsc: e.activation(fsc[:], fsc[:], AF.Ln,
                                                       bias=ceps),
             reads=[fscb, pb.cbuf], writes=[fscb], tag="d2_ln")
        S.do("scalar", lambda e, fsc=fsc: e.activation(fsc[:], fsc[:], AF.Exp,
                                                       scale=cmh),
             reads=[fscb, pb.cbuf], writes=[fscb], tag="d2_ex")
        S.do("vector", lambda e, fsc=fsc, sc=sc: e.tensor_tensor(
            fsc[:], fsc[:], sc[:, 16:32], ALU.mult),
            reads=[fscb, scb2], writes=[fscb], tag="d2_fa")
        for c in range(NCH):
            gtt, gtb = gt[c]
            S.do("scalar", lambda e, gtt=gtt, h=h, e1=e1: e.activation(
                e1[:], gtt[:, h * 128:(h + 1) * 128], AF.Exp, scale=cm1),
                reads=[gtb, pb.cbuf], writes=[e1b], tag="d2_ge")
            S.do("scalar", lambda e, e1=e1, e2=e2: e.activation(
                e2[:], e1[:], AF.Ln, bias=c1),
                reads=[e1b, pb.cbuf], writes=[e2b], tag="d2_ga")
            S.do("scalar", lambda e, e1=e1, e2=e2: e.activation(
                e1[:], e2[:], AF.Exp, scale=cm1),
                reads=[e2b, pb.cbuf], writes=[e1b], tag="d2_gr")
            S.do("vector", lambda e, c=c, ogt=ogt, oall=oall, fsc=fsc, e1=e1:
                 e.scalar_tensor_tensor(
                     ogt[:], oall[:, c * 128:(c + 1) * 128], fsc[:, c:c + 1],
                     e1[:], ALU.mult, ALU.mult),
                 reads=[oallb, fscb, e1b], writes=[ogtb], tag="d2_og")
            S.do("sync",
                 lambda e, c=c, ogdt=ogdt, ogt=ogt:
                 e.dma_start_transpose(
                     ogdt[:, c * 128:(c + 1) * 128], ogt[:]),
                 reads=[ogtb], accwrites=[ogdtb], is_dma=True, tag="d2_tp")
            if c % 4 == 3:
                yield
        sto = S.do("sync", lambda e, h=h, ogdt=ogdt: e.dma_start(
            out=ogd.ap()[h], in_=ogdt[:]),
            reads=[ogdtb], accwrites=[ogdb], is_dma=True, tag="d2_out")
        S.mark_output(sto)
        yield

    heads = list(heads)
    for i in range(0, len(heads), LN):
        drive([run_head(i + j, heads[i + j])
               for j in range(LN) if i + j < len(heads)])
    pb.pop_scope()




def p_wo0(pb):
    S = pb.S
    ogd = pb.io("OG_D", [NH0, 128, TOK], BF16, "in")
    ogdb = pb.dbuf("OG_D")
    wo = pb.io("L0_attn_Wo", [D, D], F32, "in")
    nw = pb.io("L0_attn_norm_w", [128, 1], F32, "in")
    x1p = pb.io("X1P", [TOK, D], F32, "in")
    x1pb = pb.dbuf("X1P")
    x1 = pb.io("X1", [TOK, D], F32, "out")
    x1b = pb.dbuf("X1")

    pb.push_scope()
    nwt, nwtb = pb.sb("nw", [128, 1], F32)
    S.do("sync", lambda e: e.dma_start(out=nwt[:], in_=nw.ap()),
         reads=[pb.dbuf("L0_attn_norm_w")], writes=[nwtb], is_dma=True,
         tag="wo_nw")
    og = []
    for k in range(NH0):
        t, b = pb.sb(f"og{k}", [128, TOK], BF16)
        S.do("sync", lambda e, t=t, k=k: e.dma_start(out=t[:],
                                                     in_=ogd.ap()[k]),
             reads=[ogdb], writes=[b], is_dma=True, tag="wo_og")
        og.append((t, b))
    wot = []
    for k in range(16):
        t, b = pb.sb(f"wo{k}", [128, D], BF16)
        S.do("gpsimd", lambda e, t=t, k=k: e.dma_start(
            out=t[:], in_=wo.ap()[k * 128:(k + 1) * 128, :]),
            reads=[pb.dbuf("L0_attn_Wo")], writes=[b], is_dma=True,
            tag="wo_w")
        S.do("vector", lambda e, t=t: e.tensor_scalar(
            t[:], t[:], nwt[:, 0:1], None, ALU.mult),
            reads=[nwtb], writes=[b], tag="wo_nws")
        wot.append((t, b))
    pp_ = [pb.ps("pwo", [128, 512]) for _ in range(2)]
    xpt = [pb.sb("xp", [128, 512], F32) for _ in range(2)]
    x1t = [pb.sb("x1t", [128, 512], F32) for _ in range(2)]
    for tt in range(NCH):
        for oc in range(4):
            i = (tt * 4 + oc) % 2
            pp, ppb = pp_[i]
            xpt_t, xptb = xpt[i]
            x1t_t, x1tb = x1t[i]
            for k in range(16):
                wt, wtb = wot[k]
                ot, ob = og[k]
                S.do("tensor", lambda e, pp=pp, ot=ot, wt=wt, k=k, tt=tt,
                     oc=oc: e.matmul(
                         pp[:], ot[:, tt * 128:(tt + 1) * 128],
                         wt[:, oc * 512:(oc + 1) * 512],
                         start=(k == 0), stop=(k == 15)),
                     reads=[ob, wtb], writes=[ppb], tag="wo_mm")
            S.do("sync", lambda e, xpt_t=xpt_t, tt=tt, oc=oc: e.dma_start(
                out=xpt_t[:], in_=x1p.ap()[tt * 128:(tt + 1) * 128,
                                           oc * 512:(oc + 1) * 512]),
                reads=[x1pb], writes=[xptb], is_dma=True, tag="wo_xp")
            S.do("vector", lambda e, x1t_t=x1t_t, pp=pp, xpt_t=xpt_t:
                 e.tensor_tensor(x1t_t[:], pp[:], xpt_t[:], ALU.add),
                 reads=[ppb, xptb], writes=[x1tb], tag="wo_add")
            sto = S.do("sync", lambda e, x1t_t=x1t_t, tt=tt, oc=oc:
                       e.dma_start(
                           out=x1.ap()[tt * 128:(tt + 1) * 128,
                                       oc * 512:(oc + 1) * 512],
                           in_=x1t_t[:]),
                       reads=[x1tb], accwrites=[x1b], is_dma=True,
                       tag="wo_x1")
            S.mark_output(sto)
    pb.pop_scope()




def p_qkv1(pb):
    S = pb.S
    xnd = pb.state["XN1_D"]
    wqT = pb.io("L1_WqT", [D, D], F32, "in")
    wkT = pb.io("L1_WkT", [D, 512], F32, "in")
    wvT = pb.io("L1_WvT", [D, 512], F32, "in")
    rcos = pb.io("RCOS", [TOK, 128], F32, "in")
    rsin = pb.io("RSIN", [TOK, 128], F32, "in")
    e127 = pb.io("E127", [128, 1], F32, "in")
    onesf = pb.io("ONESF", [1, 128], F32, "in")
    triuIb_ = pb.io("TRIUI", [128, 128], BF16, "in")
    qd1 = pb.io("QD1", [16, 128, TOK], BF16, "out")
    kd1 = pb.io("KD1", [4, 128, TOK], BF16, "out")
    vtok = pb.io("VTOK", [16, 128, 512], BF16, "out")
    kvcat = pb.io("KVCAT", [1280, 512], BF16, "out")
    vsum = pb.io("VSUM", [1, 512], F32, "out")
    pfx = pb.io("PFX", [16, 128, 512], F32, "out")

    pb.push_scope()
    tri, trib = pb.sb("tri", [128, 128], BF16)
    S.do("sync", lambda e: e.dma_start(out=tri[:], in_=triuIb_.ap()),
         reads=[pb.dbuf("TRIUI")], writes=[trib], is_dma=True, tag="q1_tri")
    e127t, e127b = pb.sb("e127", [128, 1], F32)
    S.do("sync", lambda e: e.dma_start(out=e127t[:], in_=e127.ap()),
         reads=[pb.dbuf("E127")], writes=[e127b], is_dma=True, tag="q1_e127")
    onft, onfb = pb.sb("onf", [1, 128], F32)
    S.do("sync", lambda e: e.dma_start(out=onft[:], in_=onesf.ap()),
         reads=[pb.dbuf("ONESF")], writes=[onfb], is_dma=True, tag="q1_onf")
    wsl = [pb.sb("w1", [128, 16 * 512], BF16) for _ in range(2)]
    cst_, cstb_ = pb.sb("rc", [128, 128], F32)
    snt, sntb = pb.sb("rs", [128, 128], F32)
    pe1 = [pb.ps("pq1", [128, 512]) for _ in range(2)]
    ev = [pb.sb("ev", [128, 512], F32) for _ in range(2)]
    rp, rpb = pb.sb("rp", [128, 512], F32)
    rtmp, rtmpb = pb.sb("rtmp", [128, 128], F32)
    sqv, sqvb = pb.sb("sqv", [128, 256], F32)
    lnst, lnstb = pb.sb("lnst", [128, 16], F32)
    outb_ = [pb.sb("ob", [128, 512], BF16) for _ in range(2)]
    tps, tpsb = pb.sb("tps", [128, 512], BF16)
    pfxp, pfxpb = pb.ps("ppfx", [128, 512])
    pfxt, pfxtb = pb.sb("pfxt", [128, 512], F32)
    offr, offrb = pb.sb("offr", [1, 512], F32)
    prow, prowb = pb.ps("prow", [1, 512])
    c1 = pb.cv(1.0)
    cmh = pb.cv(-0.5)
    c256i = pb.cv(1.0 / 256.0)
    ceps6 = pb.cv(1e-6)
    c256 = pb.cv(256.0)

    S.do("vector", lambda e: e.memset(offr[:], 0.0), writes=[offrb],
         tag="q1_off0")

    def ln_unit(src, srcb, dst, dstb, hslice, scale_q, tag):
        h0 = hslice
        S.do("vector", lambda e: e.reduce_sum(
            lnst[:, 0:1], src[:, h0:h0 + 256], axis=AX.X),
            reads=[srcb], writes=[lnstb], tag=tag + "_s")
        S.do("vector", lambda e: e.tensor_tensor(
            sqv[:], src[:, h0:h0 + 256], src[:, h0:h0 + 256], ALU.mult),
            reads=[srcb], writes=[sqvb], tag=tag + "_q")
        S.do("vector", lambda e: e.reduce_sum(
            lnst[:, 1:2], sqv[:], axis=AX.X),
            reads=[sqvb], writes=[lnstb], tag=tag + "_q2")
        S.do("vector", lambda e: e.tensor_scalar(
            lnst[:, 2:3], lnst[:, 0:1], c256i, None, ALU.mult),
            reads=[lnstb, pb.cbuf], writes=[lnstb], tag=tag + "_mu")
        S.do("vector", lambda e: e.tensor_scalar(
            lnst[:, 3:4], lnst[:, 1:2], c256i, None, ALU.mult),
            reads=[lnstb, pb.cbuf], writes=[lnstb], tag=tag + "_m2")
        S.do("vector", lambda e: e.tensor_tensor(
            lnst[:, 4:5], lnst[:, 2:3], lnst[:, 2:3], ALU.mult),
            reads=[lnstb], writes=[lnstb], tag=tag + "_u2")
        S.do("vector", lambda e: e.tensor_tensor(
            lnst[:, 5:6], lnst[:, 3:4], lnst[:, 4:5], ALU.subtract),
            reads=[lnstb], writes=[lnstb], tag=tag + "_v")
        S.do("scalar", lambda e: e.activation(
            lnst[:, 6:7], lnst[:, 5:6], AF.Ln, bias=ceps6),
            reads=[lnstb, pb.cbuf], writes=[lnstb], tag=tag + "_l")
        S.do("scalar", lambda e: e.activation(
            lnst[:, 7:8], lnst[:, 6:7], AF.Exp, scale=cmh),
            reads=[lnstb, pb.cbuf], writes=[lnstb], tag=tag + "_e")
        if scale_q:
            S.do("vector", lambda e: e.tensor_scalar(
                lnst[:, 7:8], lnst[:, 7:8], c256, None, ALU.mult),
                reads=[lnstb, pb.cbuf], writes=[lnstb], tag=tag + "_x")
        S.do("vector", lambda e: e.tensor_scalar(
            dst[:, h0:h0 + 256], src[:, h0:h0 + 256], lnst[:, 2:3],
            lnst[:, 7:8], ALU.subtract, ALU.mult),
            reads=[srcb, lnstb], writes=[dstb], tag=tag + "_a")

    passes = [("q", oc) for oc in range(4)] + [("k", 0), ("v", 0)]
    for pi, (kind, oc) in enumerate(passes):
        wsrc = {"q": wqT, "k": wkT, "v": wvT}[kind]
        wname = {"q": "L1_WqT", "k": "L1_WkT", "v": "L1_WvT"}[kind]
        wt, wtb = wsl[pi % 2]
        col0 = oc * 512
        S.do("gpsimd", lambda e, wt=wt, wsrc=wsrc, col0=col0: e.dma_start(
            out=wt[:].rearrange("p (k o) -> p k o", k=16),
            in_=wsrc.ap()[:, col0:col0 + 512].rearrange(
                "(k p) o -> p k o", p=128)),
            reads=[pb.dbuf(wname)], writes=[wtb], is_dma=True, tag="q1_w")
        for tt in range(NCH):
            i = tt % 2
            pp, ppb = pe1[i]
            evt, evb = ev[i]
            obt, obb = outb_[i]
            for k in range(16):
                S.do("tensor", lambda e, pp=pp, wt=wt, k=k, tt=tt: e.matmul(
                    pp[:], xnd[k][0][:, tt * 128:(tt + 1) * 128],
                    wt[:, k * 512:(k + 1) * 512],
                    start=(k == 0), stop=(k == 15)),
                    reads=[xnd[k][1], wtb], writes=[ppb], tag="q1_mm")
            S.do("scalar", lambda e, evt=evt, pp=pp: e.copy(evt[:], pp[:]),
                 reads=[ppb], writes=[evb], tag="q1_ev")
            if kind in ("q", "k"):
                S.do("sync", lambda e, tt=tt: e.dma_start(
                    out=cst_[:], in_=rcos.ap()[tt * 128:(tt + 1) * 128, :]),
                    reads=[pb.dbuf("RCOS")], writes=[cstb_], is_dma=True,
                    tag="q1_c")
                S.do("sync", lambda e, tt=tt: e.dma_start(
                    out=snt[:], in_=rsin.ap()[tt * 128:(tt + 1) * 128, :]),
                    reads=[pb.dbuf("RSIN")], writes=[sntb], is_dma=True,
                    tag="q1_s")
                for hh in range(2):
                    b0 = hh * 256
                    S.do("vector", lambda e, evt=evt, b0=b0: e.tensor_tensor(
                        rtmp[:], evt[:, b0 + 128:b0 + 256], snt[:], ALU.mult),
                        reads=[evb, sntb], writes=[rtmpb], tag="q1_r1")
                    S.do("vector", lambda e, evt=evt, b0=b0: e.tensor_tensor(
                        rp[:, b0:b0 + 128], evt[:, b0:b0 + 128], cst_[:],
                        ALU.mult),
                        reads=[evb, cstb_], accwrites=[rpb], tag="q1_r2")
                    S.do("vector", lambda e, b0=b0: e.tensor_tensor(
                        rp[:, b0:b0 + 128], rp[:, b0:b0 + 128], rtmp[:],
                        ALU.subtract),
                        reads=[rpb, rtmpb], accwrites=[rpb], tag="q1_r3")
                    S.do("vector", lambda e, evt=evt, b0=b0: e.tensor_tensor(
                        rtmp[:], evt[:, b0:b0 + 128], snt[:], ALU.mult),
                        reads=[evb, sntb], writes=[rtmpb], tag="q1_r4")
                    S.do("vector", lambda e, evt=evt, b0=b0: e.tensor_tensor(
                        rp[:, b0 + 128:b0 + 256], evt[:, b0 + 128:b0 + 256],
                        cst_[:], ALU.mult),
                        reads=[evb, cstb_], accwrites=[rpb], tag="q1_r5")
                    S.do("vector", lambda e, b0=b0: e.tensor_tensor(
                        rp[:, b0 + 128:b0 + 256], rp[:, b0 + 128:b0 + 256],
                        rtmp[:], ALU.add),
                        reads=[rpb, rtmpb], accwrites=[rpb], tag="q1_r6")
                src, srcb = rp, rpb
            else:
                src, srcb = evt, evb
            for hh in range(2):
                ln_unit(src, srcb, obt, obb, hh * 256, kind == "q",
                        f"q1ln{pi}_{tt}_{hh}")
            if kind == "v":
                sto = S.do("sync", lambda e, obt=obt, tt=tt: e.dma_start(
                    out=vtok.ap()[tt], in_=obt[:]),
                    reads=[obb], accwrites=[pb.dbuf("VTOK")], is_dma=True,
                    tag="q1_vo")
                S.mark_output(sto)
                S.do("tensor", lambda e, obt=obt: e.matmul(
                    pfxp[:], tri[:], obt[:], start=True, stop=False),
                    reads=[trib, obb], writes=[pfxpb], tag="q1_pm")
                S.do("tensor", lambda e: e.matmul(
                    pfxp[:], onft[:], offr[:], start=False, stop=True),
                    reads=[onfb, offrb], writes=[pfxpb], tag="q1_po")
                S.do("vector", lambda e: e.tensor_copy(pfxt[:], pfxp[:]),
                     reads=[pfxpb], writes=[pfxtb], tag="q1_pe")
                sto = S.do("sync", lambda e, tt=tt: e.dma_start(
                    out=pfx.ap()[tt], in_=pfxt[:]),
                    reads=[pfxtb], accwrites=[pb.dbuf("PFX")], is_dma=True,
                    tag="q1_pfo")
                S.mark_output(sto)
                S.do("tensor", lambda e: e.matmul(
                    prow[:, 0:512], e127t[:], pfxt[:], start=True, stop=True),
                    reads=[e127b, pfxtb], writes=[prowb], tag="q1_pr")
                S.do("vector", lambda e: e.tensor_copy(offr[:], prow[:, 0:512]),
                     reads=[prowb], writes=[offrb], tag="q1_oe")
                if tt >= 11:
                    sto = S.do("sync", lambda e, obt=obt, tt=tt:
                               e.dma_start(
                                   out=kvcat.ap()[640 + (tt - 11) * 128:
                                                  640 + (tt - 10) * 128, :],
                                   in_=obt[:]),
                               reads=[obb], accwrites=[pb.dbuf("KVCAT")],
                               is_dma=True, tag="q1_vt")
                    S.mark_output(sto)
            else:
                for pc in range(4):
                    S.do("sync",
                         lambda e, obt=obt, pc=pc: e.dma_start_transpose(
                        tps[:, pc * 128:(pc + 1) * 128],
                        obt[:, pc * 128:(pc + 1) * 128]),
                        reads=[obb], accwrites=[tpsb], is_dma=True,
                        tag="q1_tp")
                dst = qd1 if kind == "q" else kd1
                dstb = pb.dbuf("QD1" if kind == "q" else "KD1")
                for pc in range(4):
                    ch = oc * 4 + pc
                    sto = S.do("sync", lambda e, tps=tps, dst=dst, ch=ch,
                               tt=tt, pc=pc: e.dma_start(
                                   out=dst.ap()[ch][:,
                                                    tt * 128:(tt + 1) * 128],
                                   in_=tps[:, pc * 128:(pc + 1) * 128]),
                               reads=[tpsb], accwrites=[dstb], is_dma=True,
                               tag="q1_do")
                    S.mark_output(sto)
                if kind == "k" and tt >= 11:
                    sto = S.do("sync", lambda e, obt=obt, tt=tt: e.dma_start(
                        out=kvcat.ap()[(tt - 11) * 128:(tt - 10) * 128, :],
                        in_=obt[:]),
                        reads=[obb], accwrites=[pb.dbuf("KVCAT")],
                        is_dma=True, tag="q1_kt")
                    S.mark_output(sto)
    sto = S.do("sync", lambda e: e.dma_start(out=vsum.ap(), in_=offr[:]),
               reads=[offrb], writes=[pb.dbuf("VSUM")], is_dma=True,
               tag="q1_vs")
    S.mark_output(sto)
    pb.pop_scope()



def p_load_xnd(pb, name, ntok):
    S = pb.S
    src = pb.io(name, [16, 128, ntok], BF16, "in")
    srcb = pb.dbuf(name)
    tiles = []
    for c in range(16):
        t, b = pb.sb(f"{name.lower()}{c}", [128, ntok], BF16)
        S.do("sync", lambda e, t=t, c=c: e.dma_start(out=t[:],
                                                     in_=src.ap()[c]),
             reads=[srcb], writes=[b], is_dma=True, tag=f"ldx{c}")
        tiles.append((t, b))
    pb.state[name] = tiles
    return tiles




def p_ffn1(pb):
    S = pb.S
    xnd = pb.state["XN1_D"]
    gwT = pb.io("L1_gateT", [D, INTER], F32, "in")
    uwT = pb.io("L1_upT", [D, INTER], F32, "in")
    dwT = pb.io("L1_downT", [INTER, D], F32, "in")
    gs_ = pb.io("L1_gate_s", [128, NIC], F32, "in")
    us_ = pb.io("L1_up_s", [128, NIC], F32, "in")
    ds_ = pb.io("L1_down_s", [128, 16], F32, "in")
    ffnd = pb.io("FFN_D", [16, 128, TOK], BF16, "out")
    ffndb = pb.dbuf("FFN_D")

    pb.push_scope()
    gst, gstb = pb.sb("gs1", [128, NIC], F32)
    S.do("sync", lambda e: e.dma_start(out=gst[:], in_=gs_.ap()),
         reads=[pb.dbuf("L1_gate_s")], writes=[gstb], is_dma=True, tag="f_gs")
    gsn, gsnb = pb.sb("gsn", [128, NIC], F32)
    cm1 = pb.cv(-1.0)
    c1 = pb.cv(1.0)
    S.do("vector", lambda e: e.tensor_scalar(gsn[:], gst[:], cm1, None,
                                             ALU.mult),
         reads=[gstb, pb.cbuf], writes=[gsnb], tag="f_gsn")
    ust, ustb = pb.sb("us1", [128, NIC], F32)
    S.do("sync", lambda e: e.dma_start(out=ust[:], in_=us_.ap()),
         reads=[pb.dbuf("L1_up_s")], writes=[ustb], is_dma=True, tag="f_us")
    dst_, dstb_ = pb.sb("ds1", [128, 16], F32)
    S.do("sync", lambda e: e.dma_start(out=dst_[:], in_=ds_.ap()),
         reads=[pb.dbuf("L1_down_s")], writes=[dstb_], is_dma=True,
         tag="f_ds")
    hsb = [pb.sb(f"h{ic}", [128, 1024], BF16) for ic in range(NIC)]
    wgu = [pb.sb("wgu", [128, 16 * 128], BF16) for _ in range(4)]
    wdn = [pb.sb("wdn", [128, NIC * 128], BF16) for _ in range(2)]
    pg_ = [pb.ps("pfg", [128, 1024], F32) for _ in range(2)]
    pu_, pub_ = pb.ps("pfu", [128, 1024], F32)
    pd_ = [pb.ps("pfd", [128, 512]) for _ in range(2)]
    ex1, ex1b = pb.sb("ex1", [128, 1024], F32)
    ex2, ex2b = pb.sb("ex2", [128, 1024], F32)
    gg_, ggb_ = pb.sb("ggf", [128, 1024], BF16)
    dev = [pb.sb("dev", [128, 512], BF16) for _ in range(2)]
    for half in range(2):
        t0 = half * 1024
        for ic in range(NIC):
            wtg, wtgb = wgu[(2 * ic) % 4]
            wtu, wtub = wgu[(2 * ic + 1) % 4]
            S.do("gpsimd", lambda e, wtg=wtg, ic=ic: e.dma_start(
                out=wtg[:].rearrange("p (k o) -> p k o", k=16),
                in_=gwT.ap()[:, ic * 128:(ic + 1) * 128].rearrange(
                    "(k p) o -> p k o", p=128)),
                reads=[pb.dbuf("L1_gateT")], writes=[wtgb], is_dma=True,
                tag="f_wg")
            S.do("gpsimd", lambda e, wtu=wtu, ic=ic: e.dma_start(
                out=wtu[:].rearrange("p (k o) -> p k o", k=16),
                in_=uwT.ap()[:, ic * 128:(ic + 1) * 128].rearrange(
                    "(k p) o -> p k o", p=128)),
                reads=[pb.dbuf("L1_upT")], writes=[wtub], is_dma=True,
                tag="f_wu")
            pgt, pgtb = pg_[ic % 2]
            for nt in range(2):
                for k in range(16):
                    S.do("tensor", lambda e, pgt=pgt, wtg=wtg, k=k, nt=nt,
                         t0=t0: e.matmul(
                             pgt[:, nt * 512:(nt + 1) * 512],
                             wtg[:, k * 128:(k + 1) * 128],
                             xnd[k][0][:, t0 + nt * 512:t0 + (nt + 1) * 512],
                             start=(k == 0), stop=(k == 15)),
                         reads=[wtgb, xnd[k][1]], writes=[pgtb], tag="f_gmm")
            for nt in range(2):
                for k in range(16):
                    S.do("tensor", lambda e, wtu=wtu, k=k, nt=nt, t0=t0:
                         e.matmul(
                             pu_[:, nt * 512:(nt + 1) * 512],
                             wtu[:, k * 128:(k + 1) * 128],
                             xnd[k][0][:, t0 + nt * 512:t0 + (nt + 1) * 512],
                             start=(k == 0), stop=(k == 15)),
                         reads=[wtub, xnd[k][1]], writes=[pub_], tag="f_umm")
            S.do("scalar", lambda e, pgt=pgt, ic=ic: e.activation(
                ex1[:], pgt[:, 0:1024], AF.Exp, scale=gsn[:, ic:ic + 1]),
                reads=[pgtb, gsnb], writes=[ex1b], tag="f_se")
            S.do("vector", lambda e: e.tensor_scalar(
                ex2[:], ex1[:], c1, None, ALU.add),
                reads=[ex1b, pb.cbuf], writes=[ex2b], tag="f_sa")
            S.do("vector", lambda e: e.reciprocal(ex1[:], ex2[:]),
                 reads=[ex2b], writes=[ex1b], tag="f_sr")
            S.do("vector", lambda e, pgt=pgt, ic=ic: e.tensor_scalar(
                gg_[:], pgt[:, 0:1024], gst[:, ic:ic + 1], None, ALU.mult),
                reads=[pgtb, gstb], writes=[ggb_], tag="f_gg")
            S.do("vector", lambda e: e.tensor_tensor(
                gg_[:], gg_[:], ex1[:], ALU.mult),
                reads=[ggb_, ex1b], writes=[ggb_], tag="f_sm")
            ht, hb = hsb[ic]
            S.do("vector", lambda e, ht=ht, ic=ic: e.scalar_tensor_tensor(
                ht[:], pu_[:, 0:1024], ust[:, ic:ic + 1], gg_[:],
                ALU.mult, ALU.mult),
                reads=[pub_, ustb, ggb_], writes=[hb], tag="f_h")
        for od in range(16):
            wt, wtb = wdn[od % 2]
            S.do("gpsimd", lambda e, wt=wt, od=od: e.dma_start(
                out=wt[:].rearrange("p (k o) -> p k o", k=NIC),
                in_=dwT.ap()[:, od * 128:(od + 1) * 128].rearrange(
                    "(k p) o -> p k o", p=128)),
                reads=[pb.dbuf("L1_downT")], writes=[wtb], is_dma=True,
                tag="f_wd")
            for nt in range(2):
                pdt, pdtb = pd_[nt]
                for ic in range(NIC):
                    ht, hb = hsb[ic]
                    S.do("tensor", lambda e, pdt=pdt, wt=wt, ht=ht, ic=ic,
                         nt=nt: e.matmul(
                             pdt[:], wt[:, ic * 128:(ic + 1) * 128],
                             ht[:, nt * 512:(nt + 1) * 512],
                             start=(ic == 0), stop=(ic == NIC - 1)),
                         reads=[wtb, hb], writes=[pdtb], tag="f_dmm")
                dt_, db_ = dev[nt]
                S.do("vector", lambda e, dt_=dt_, pdt=pdt, od=od:
                     e.tensor_scalar(dt_[:], pdt[:], dst_[:, od:od + 1],
                                     None, ALU.mult),
                     reads=[pdtb, dstb_], writes=[db_], tag="f_dev")
                sto = S.do("sync", lambda e, dt_=dt_, od=od, half=half,
                           nt=nt: e.dma_start(
                               out=ffnd.ap()[od][:, half * 1024 + nt * 512:
                                                 half * 1024 + (nt + 1) * 512],
                               in_=dt_[:]),
                           reads=[db_], accwrites=[ffndb], is_dma=True,
                           tag="f_do")
                S.mark_output(sto)
    pb.pop_scope()



def p_att1(pb):
    S = pb.S
    qd1 = pb.io("QD1", [16, 128, TOK], BF16, "in")
    kd1 = pb.io("KD1", [4, 128, TOK], BF16, "in")
    vtok = pb.io("VTOK", [16, 128, 512], BF16, "in")
    pfxd = pb.io("PFX", [16, 128, 512], F32, "in")
    kvall = pb.io("KV_ALL", [4, 1280, 512], BF16, "in")
    vsall = pb.io("VS_ALL", [4, 1, 512], F32, "in")
    selp = pb.io("SELP", [128, 4], F32, "in")
    sels = pb.io("SELS", [1, 4], F32, "in")
    masks = pb.io("MASKS", [16, 128, 640], BF16, "in")
    onesf = pb.io("ONESF", [1, 128], F32, "in")
    e127 = pb.io("E127", [128, 1], F32, "in")
    triuIf = pb.io("TRIUIF", [128, 128], F32, "in")
    pb.io("TRIUI", [128, 128], BF16, "in")

    qd = p_load_xnd(pb, "QD1", TOK)
    pb.state["ATT_D"] = qd
    pb.push_scope()
    kf = [[pb.sb(f"kf{kv}{dc}", [128, 2688], BF16) for dc in range(2)]
          for kv in range(2)]
    vf = [pb.sb(f"vf{t}", [128, 512], BF16) for t in range(21)]
    pfx = [pb.sb(f"px{t}", [128, 512], F32) for t in range(21)]
    selt, selb = pb.sb("selp", [128, 4], F32)
    selst, selsb = pb.sb("sels", [1, 4], F32)
    onft, onfb = pb.sb("onf1", [1, 128], F32)
    e127t, e127b = pb.sb("e1271", [128, 1], F32)
    trit, trib = pb.sb("trif", [128, 128], F32)
    trib16, trib16b = pb.sb("trib16", [128, 128], BF16)
    for t, b, srcn in [(selt, selb, "SELP"), (selst, selsb, "SELS"),
                       (onft, onfb, "ONESF"),
                       (e127t, e127b, "E127"), (trit, trib, "TRIUIF"),
                       (trib16, trib16b, "TRIUI")]:
        S.do("sync", lambda e, t=t, srcn=srcn: e.dma_start(
            out=t[:], in_=pb.dram[srcn][0].ap()),
            reads=[pb.dbuf(srcn)], writes=[b], is_dma=True, tag=f"a_{srcn}")
    htmp, htmpb = pb.sb("htmp", [128, 512], BF16)
    hacc, haccb = pb.sb("hacc", [128, 512], F32)
    hbf, hbfb = pb.sb("hbf", [128, 512], BF16)
    rows, rowsb = pb.sb("rows", [1, 512 * 3], F32)
    prow2, prow2b = pb.ps("prow2", [1, 512])
    pb1, pb1b = pb.ps("pb1", [128, 512])
    for kv_v, base in (("k", 0), ("v", 640)):
        for t5 in range(5):
            for p in range(4):
                tg = f"a_h{kv_v}{t5}_{p}"
                if p == 0:
                    S.do("vector", lambda e: e.memset(hacc[:], 0.0),
                         writes=[haccb], tag=tg + "z")
                S.do("sync", lambda e, p=p, base=base, t5=t5: e.dma_start(
                    out=htmp[:], in_=kvall.ap()[p, base + t5 * 128:
                                                base + (t5 + 1) * 128, :]),
                    reads=[pb.dbuf("KV_ALL")], writes=[htmpb], is_dma=True,
                    tag=tg)
                S.do("vector", lambda e, p=p: e.scalar_tensor_tensor(
                    hacc[:], htmp[:], selt[:, p:p + 1], hacc[:],
                    ALU.mult, ALU.add),
                    reads=[htmpb, selb, haccb], writes=[haccb], tag=tg + "a")
            S.do("vector", lambda e: e.tensor_copy(hbf[:], hacc[:]),
                 reads=[haccb], writes=[hbfb], tag=tg + "c")
            if kv_v == "k":
                for kv in range(2):
                    for dc in range(2):
                        kt, kb = kf[kv][dc]
                        S.do("sync",
                             lambda e, kt=kt, kv=kv, dc=dc, t5=t5:
                             e.dma_start_transpose(
                                 kt[:, t5 * 128:(t5 + 1) * 128],
                                 hbf[:, kv * 256 + dc * 128:
                                     kv * 256 + (dc + 1) * 128]),
                             reads=[hbfb], accwrites=[kb], is_dma=True,
                             tag=tg + "t")
            else:
                vt, vb = vf[t5]
                S.do("vector", lambda e, vt=vt: e.tensor_copy(vt[:], hbf[:]),
                     reads=[hbfb], writes=[vb], tag=tg + "v")
    for kv in range(2):
        for dc in range(2):
            kt, kb = kf[kv][dc]
            S.do("sync", lambda e, kt=kt, kv=kv, dc=dc: e.dma_start(
                out=kt[:, 640:2688], in_=kd1.ap()[kv * 2 + dc]),
                reads=[pb.dbuf("KD1")], accwrites=[kb], is_dma=True,
                tag="a_ko")
    for tt in range(16):
        vt, vb = vf[5 + tt]
        S.do("sync", lambda e, vt=vt, tt=tt: e.dma_start(
            out=vt[:], in_=vtok.ap()[tt]),
            reads=[pb.dbuf("VTOK")], writes=[vb], is_dma=True, tag="a_vo")
    S.do("vector", lambda e: e.memset(rows[:], 0.0), writes=[rowsb],
         tag="a_r0")
    for p in range(4):
        S.do("sync", lambda e, p=p: e.dma_start(
            out=rows[:, 1024:1536], in_=vsall.ap()[p]),
            reads=[pb.dbuf("VS_ALL")], accwrites=[rowsb], is_dma=True,
            tag=f"a_vs{p}")
        S.do("vector", lambda e, p=p: e.scalar_tensor_tensor(
            rows[:, 0:512], rows[:, 1024:1536], selst[:, p:p + 1],
            rows[:, 0:512], ALU.mult, ALU.add),
            reads=[rowsb, selsb], accwrites=[rowsb], tag=f"a_go{p}")
        S.do("vector", lambda e, p=p: e.tensor_tensor(
            rows[:, 512:1024], rows[:, 512:1024], rows[:, 1024:1536],
            ALU.add),
            reads=[rowsb], accwrites=[rowsb], tag=f"a_vt{p}")
    offh, offhb = pb.sb("offh", [1, 512], F32)
    S.do("vector", lambda e: e.memset(offh[:], 0.0), writes=[offhb],
         tag="a_oh0")
    for t5 in range(5):
        vt, vb = vf[t5]
        pt, ptb = pfx[t5]
        S.do("vector", lambda e, vt=vt: e.tensor_copy(htmp[:], vt[:]),
             reads=[vb], writes=[htmpb], tag=f"a_pw{t5}")
        S.do("tensor", lambda e: e.matmul(
            pb1[:], trib16[:], htmp[:], start=True, stop=False),
            reads=[trib16b, htmpb], writes=[pb1b], tag=f"a_pm{t5}")
        S.do("tensor", lambda e: e.matmul(
            pb1[:], onft[:], offh[:], start=False, stop=True),
            reads=[onfb, offhb], writes=[pb1b], tag=f"a_po{t5}")
        S.do("vector", lambda e, pt=pt: e.tensor_copy(pt[:], pb1[:]),
             reads=[pb1b], writes=[ptb], tag=f"a_pe{t5}")
        S.do("tensor", lambda e, pt=pt: e.matmul(
            prow2[:, 0:512], e127t[:], pt[:], start=True, stop=True),
            reads=[e127b, ptb], writes=[prow2b], tag=f"a_pr{t5}")
        S.do("vector", lambda e: e.tensor_copy(offh[:], prow2[:, 0:512]),
             reads=[prow2b], writes=[offhb], tag=f"a_oe{t5}")
    S.do("vector", lambda e: e.tensor_tensor(
        rows[:, 1024:1536], rows[:, 0:512], offh[:], ALU.subtract),
        reads=[rowsb, offhb], accwrites=[rowsb], tag="a_o2")
    S.do("tensor", lambda e: e.matmul(
        pb1[:], onft[:], rows[:, 1024:1536], start=True, stop=True),
        reads=[onfb, rowsb], writes=[pb1b], tag="a_o2b")
    ob2, ob2b = pb.sb("ob2", [128, 512], F32)
    S.do("vector", lambda e: e.tensor_copy(ob2[:], pb1[:]),
         reads=[pb1b], writes=[ob2b], tag="a_o2e")
    for t5 in range(5):
        pt, ptb = pfx[t5]
        S.do("vector", lambda e, pt=pt: e.tensor_tensor(
            pt[:], pt[:], ob2[:], ALU.add),
            reads=[ptb, ob2b], writes=[ptb], tag=f"a_ph{t5}")
    S.do("tensor", lambda e: e.matmul(
        pb1[:], onft[:], rows[:, 0:512], start=True, stop=True),
        reads=[onfb, rowsb], writes=[pb1b], tag="a_gob")
    S.do("vector", lambda e: e.tensor_copy(ob2[:], pb1[:]),
         reads=[pb1b], writes=[ob2b], tag="a_gobe")
    for tt in range(16):
        pt, ptb = pfx[5 + tt]
        S.do("sync", lambda e, pt=pt, tt=tt: e.dma_start(
            out=pt[:], in_=pfxd.ap()[tt]),
            reads=[pb.dbuf("PFX")], writes=[ptb], is_dma=True, tag="a_pl")
        S.do("vector", lambda e, pt=pt: e.tensor_tensor(
            pt[:], pt[:], ob2[:], ALU.add),
            reads=[ptb, ob2b], writes=[ptb], tag="a_pa")
    vtb, vtbb = pb.sb("vtb", [128, 512], F32)
    S.do("tensor", lambda e: e.matmul(
        pb1[:], onft[:], rows[:, 512:1024], start=True, stop=True),
        reads=[onfb, rowsb], writes=[pb1b], tag="a_vtb")
    S.do("vector", lambda e: e.tensor_copy(vtb[:], pb1[:]),
         reads=[pb1b], writes=[vtbb], tag="a_vtbe")

    pSC, pSCb = pb.ps("pSC", [128, 512])
    pS2, pS2b = pb.ps("pS2a", [128, 512])
    pOV, pOVb = pb.ps("pOV", [128, 512])
    tsc, tscb = pb.sb("tsc", [128, 640], F32)
    zt, ztb = pb.sb("zt", [128, 640], F32)
    mrow, mrowb = pb.sb("mrow", [128, 8], F32)
    Pt, Ptb = pb.sb("Pt", [128, 640], BF16)
    PT_, PTb_ = pb.sb("PT", [128, 640], BF16)
    mkt = [pb.sb("mk", [128, 640], BF16) for _ in range(2)]
    vct, vctb = pb.sb("vct", [128, 512], F32)
    oat, oatb = pb.sb("oat", [128, 256], F32)
    oab, oabb = pb.sb("oab", [128, 256], BF16)
    c30 = pb.cv(30.0)
    cm30 = pb.cv(-30.0)
    c1_480 = pb.cv(1.0 / 480.0)
    c1 = pb.cv(1.0)
    c7552 = pb.cv(float(SEQ - 640))
    att = qd
    pb.state["ATT_D"] = att
    for qt in range(16):
        mt, mtb = mkt[qt % 2]
        S.do("sync", lambda e, mt=mt, qt=qt: e.dma_start(
            out=mt[:], in_=masks.ap()[qt]),
            reads=[pb.dbuf("MASKS")], writes=[mtb], is_dma=True, tag="a_ml")
        phi, phib = pfx[5 + qt]
        plo, plob = pfx[qt]
        S.do("tensor", lambda e, phi=phi: e.matmul(
            prow2[:, 0:512], e127t[:], phi[:], start=True, stop=True),
            reads=[e127b, phib], writes=[prow2b], tag="a_vr1")
        S.do("vector", lambda e: e.tensor_tensor(
            rows[:, 1024:1536], rows[:, 512:1024], prow2[:, 0:512],
            ALU.subtract),
            reads=[rowsb, prow2b], accwrites=[rowsb], tag="a_vr2")
        S.do("tensor", lambda e, plo=plo: e.matmul(
            prow2[:, 0:512], e127t[:], plo[:], start=True, stop=True),
            reads=[e127b, plob], writes=[prow2b], tag="a_vr3")
        S.do("vector", lambda e: e.tensor_tensor(
            rows[:, 1024:1536], rows[:, 1024:1536], prow2[:, 0:512],
            ALU.add),
            reads=[rowsb, prow2b], accwrites=[rowsb], tag="a_vr4")
        S.do("tensor", lambda e: e.matmul(
            pb1[:], onft[:], rows[:, 1024:1536], start=True, stop=True),
            reads=[onfb, rowsb], writes=[pb1b], tag="a_vr5")
        S.do("vector", lambda e: e.tensor_copy(vct[:], pb1[:]),
            reads=[pb1b], writes=[vctb], tag="a_vr6")
        for h in range(H1):
            kv = h // 4
            c0 = 128 + qt * 128
            for dc in range(2):
                qtl, qlb = qd[2 * h + dc]
                kt, kb = kf[kv][dc]
                S.do("tensor", lambda e, qtl=qtl, kt=kt, dc=dc, c0=c0, qt=qt:
                     e.matmul(pSC[:], qtl[:, qt * 128:(qt + 1) * 128],
                              kt[:, c0:c0 + 512],
                              start=(dc == 0), stop=(dc == 1)),
                     reads=[qlb, kb], writes=[pSCb], tag="a_s1")
            for dc in range(2):
                qtl, qlb = qd[2 * h + dc]
                kt, kb = kf[kv][dc]
                S.do("tensor", lambda e, qtl=qtl, kt=kt, dc=dc, c0=c0, qt=qt:
                     e.matmul(pS2[:, 0:128],
                              qtl[:, qt * 128:(qt + 1) * 128],
                              kt[:, c0 + 512:c0 + 640],
                              start=(dc == 0), stop=(dc == 1)),
                     reads=[qlb, kb], writes=[pS2b], tag="a_s2")
            S.do("scalar", lambda e: e.activation(
                tsc[:, 0:512], pSC[:], AF.Tanh, scale=c1_480),
                reads=[pSCb, pb.cbuf], writes=[tscb], tag="a_th1")
            S.do("scalar", lambda e: e.activation(
                tsc[:, 512:640], pS2[:, 0:128], AF.Tanh, scale=c1_480),
                reads=[pS2b, pb.cbuf], accwrites=[tscb], tag="a_th2")
            S.do("vector", lambda e, mt=mt: e.scalar_tensor_tensor(
                zt[:], tsc[:], c1, mt[:], ALU.add, ALU.mult),
                reads=[tscb, mtb, pb.cbuf], writes=[ztb], tag="a_z")
            S.do("vector", lambda e: e.reduce_max(
                mrow[:, 0:1], zt[:], axis=AX.X),
                reads=[ztb], writes=[mrowb], tag="a_mx")
            S.do("vector", lambda e: e.tensor_scalar(
                mrow[:, 1:2], mrow[:, 0:1], cm30, None, ALU.mult),
                reads=[mrowb, pb.cbuf], accwrites=[mrowb], tag="a_mb")
            S.do("scalar", lambda e: e.activation(
                Pt[:], zt[:], AF.Exp, bias=mrow[:, 1:2], scale=c30),
                reads=[ztb, mrowb, pb.cbuf], writes=[Ptb], tag="a_ex")
            S.do("scalar", lambda e: e.activation(
                mrow[:, 2:3], mrow[:, 0:1], AF.Exp, scale=cm30),
                reads=[mrowb, pb.cbuf], accwrites=[mrowb], tag="a_pv")
            S.do("vector", lambda e: e.reduce_sum(
                mrow[:, 3:4], Pt[:], axis=AX.X),
                reads=[Ptb], writes=[mrowb], tag="a_rs")
            S.do("vector", lambda e: e.scalar_tensor_tensor(
                mrow[:, 4:5], mrow[:, 2:3], c7552,
                mrow[:, 3:4], ALU.mult, ALU.add),
                reads=[mrowb, pb.cbuf], accwrites=[mrowb], tag="a_den")
            S.do("vector", lambda e: e.reciprocal(
                mrow[:, 5:6], mrow[:, 4:5]),
                reads=[mrowb], accwrites=[mrowb], tag="a_rd")
            for kt5 in range(5):
                S.do("sync",
                     lambda e, kt5=kt5: e.dma_start_transpose(
                    PT_[:, kt5 * 128:(kt5 + 1) * 128],
                    Pt[:, kt5 * 128:(kt5 + 1) * 128]),
                    reads=[Ptb], accwrites=[PTb_], is_dma=True, tag="a_pt")
            for kt5 in range(5):
                vt, vb = vf[qt + 1 + kt5]
                S.do("tensor", lambda e, vt=vt, kt5=kt5, kv=kv: e.matmul(
                    pOV[:, 0:256], PT_[:, kt5 * 128:(kt5 + 1) * 128],
                    vt[:, kv * 256:(kv + 1) * 256],
                    start=(kt5 == 0), stop=(kt5 == 4)),
                    reads=[PTb_, vb], writes=[pOVb], tag="a_pv2")
            S.do("vector", lambda e, kv=kv: e.scalar_tensor_tensor(
                oat[:], vct[:, kv * 256:(kv + 1) * 256], mrow[:, 2:3],
                pOV[:, 0:256], ALU.mult, ALU.add),
                reads=[vctb, mrowb, pOVb], writes=[oatb], tag="a_oc")
            S.do("vector", lambda e: e.tensor_scalar(
                oab[:], oat[:], mrow[:, 5:6], None, ALU.mult),
                reads=[oatb, mrowb], writes=[oabb], tag="a_on")
            for dc in range(2):
                at, ab = att[2 * h + dc]
                S.do("sync",
                     lambda e, at=at, dc=dc, qt=qt: e.dma_start_transpose(
                    at[:, qt * 128:(qt + 1) * 128],
                    oab[:, dc * 128:(dc + 1) * 128]),
                    reads=[oabb], accwrites=[ab], is_dma=True, tag="a_ot")
    pb.pop_scope()




def p_wo1(pb):
    S = pb.S
    att = pb.state["ATT_D"]
    woT = pb.io("L1_WoT", [D, D], F32, "in")
    x1 = pb.io("X1", [TOK, D], F32, "in")
    x1b = pb.dbuf("X1")
    ffnd = pb.io("FFN_D", [16, 128, TOK], BF16, "in")
    x2 = pb.io("X2", [TOK, D], F32, "out")
    x2b = pb.dbuf("X2")

    pb.push_scope()
    fd = p_load_xnd(pb, "FFN_D", TOK)
    wsl = [pb.sb("wo1", [128, 16 * 512], BF16) for _ in range(2)]
    pp_ = [pb.ps("pw1", [128, 512]) for _ in range(2)]
    ftok, ftokb = pb.sb("ftok", [128, 512], BF16)
    x1t = [pb.sb("x1l", [128, 512], F32) for _ in range(2)]
    x2t = [pb.sb("x2t", [128, 512], F32) for _ in range(2)]
    for hc in range(4):
        wt, wtb = wsl[hc % 2]
        S.do("gpsimd", lambda e, wt=wt, hc=hc: e.dma_start(
            out=wt[:].rearrange("p (k o) -> p k o", k=16),
            in_=woT.ap()[:, hc * 512:(hc + 1) * 512].rearrange(
                "(k p) o -> p k o", p=128)),
            reads=[pb.dbuf("L1_WoT")], writes=[wtb], is_dma=True, tag="w1_w")
        for tt in range(16):
            i = tt % 2
            pp, ppb = pp_[i]
            for k in range(16):
                at, ab = att[k]
                S.do("tensor", lambda e, pp=pp, at=at, wt=wt, k=k, tt=tt:
                     e.matmul(pp[:], at[:, tt * 128:(tt + 1) * 128],
                              wt[:, k * 512:(k + 1) * 512],
                              start=(k == 0), stop=(k == 15)),
                     reads=[ab, wtb], writes=[ppb], tag="w1_mm")
            for j in range(4):
                ch = hc * 4 + j
                ft, fb = fd[ch]
                S.do("sync",
                     lambda e, ft=ft, j=j, tt=tt: e.dma_start_transpose(
                    ftok[:, j * 128:(j + 1) * 128],
                    ft[:, tt * 128:(tt + 1) * 128]),
                    reads=[fb], accwrites=[ftokb], is_dma=True, tag="w1_ft")
            xt, xtb = x1t[i]
            S.do("sync", lambda e, xt=xt, tt=tt, hc=hc: e.dma_start(
                out=xt[:], in_=x1.ap()[tt * 128:(tt + 1) * 128,
                                       hc * 512:(hc + 1) * 512]),
                reads=[x1b], writes=[xtb], is_dma=True, tag="w1_x1")
            x2tt, x2tb = x2t[i]
            S.do("vector", lambda e, x2tt=x2tt, pp=pp, xt=xt: e.tensor_tensor(
                x2tt[:], pp[:], xt[:], ALU.add),
                reads=[ppb, xtb], writes=[x2tb], tag="w1_a1")
            S.do("vector", lambda e, x2tt=x2tt: e.tensor_tensor(
                x2tt[:], x2tt[:], ftok[:], ALU.add),
                reads=[x2tb, ftokb], writes=[x2tb], tag="w1_a2")
            sto = S.do("sync", lambda e, x2tt=x2tt, tt=tt, hc=hc: e.dma_start(
                out=x2.ap()[tt * 128:(tt + 1) * 128,
                            hc * 512:(hc + 1) * 512], in_=x2tt[:]),
                reads=[x2tb], accwrites=[x2b], is_dma=True, tag="w1_o")
            S.mark_output(sto)
    pb.pop_scope()

SEQ = 8192


def build_L1():
    pb = PB("L1")
    pb.boot()
    p_layernorm(pb, "XIN", "XN_D", HTOK)
    p_beta(pb)
    p_delta1(pb)
    pb.S.emit()
    return pb


def build_L2():
    pb = PB("L2")
    pb.boot()
    p_load_xnd(pb, "XN_D", HTOK)
    p_moe(pb)
    p_gproj(pb)
    p_prefix(pb)
    p_delta2(pb)
    pb.S.emit()
    return pb


def build_L3():
    pb = PB("L3")
    pb.boot()
    p_wo0(pb)
    p_layernorm(pb, "X1", "XN1_D", TOK)
    p_qkv1(pb)
    pb.S.emit()
    return pb


def build_L4():
    pb = PB("L4")
    pb.boot()
    pb.push_scope()
    p_load_xnd(pb, "XN1_D", TOK)
    p_ffn1(pb)
    pb.pop_scope()
    p_att1(pb)
    p_wo1(pb)
    pb.S.emit()
    return pb




def make_consts():
    import ml_dtypes
    bf = ml_dtypes.bfloat16
    c = {}
    c["IDENTB"] = np.eye(128, dtype=bf)
    c["IDENTF"] = np.eye(128, dtype=np.float32)
    c["TRILS"] = np.tril(np.ones((128, 128), np.float32), -1).astype(bf)
    c["TRIUS"] = np.triu(np.ones((128, 128), np.float32), 1).astype(bf)
    c["TRIUI"] = np.triu(np.ones((128, 128), np.float32), 0).astype(bf)
    c["TRIUIF"] = np.triu(np.ones((128, 128), np.float32), 0)
    c["ONESB"] = np.ones((128, 1), dtype=bf)
    c["ONESR"] = np.ones((1, 128), dtype=bf)
    c["ONESF"] = np.ones((1, 128), dtype=np.float32)
    e = np.zeros((128, 1), np.float32)
    e[127, 0] = 1.0
    c["E127"] = e
    return c


def make_rope(t0):
    inv = 1.0 / (1e6 ** (np.arange(0, DH1, 2, dtype=np.float32) / DH1))
    pos = np.arange(t0, t0 + TOK, dtype=np.float32)
    fr = pos[:, None] * inv[None, :]
    return np.cos(fr).astype(np.float32), np.sin(fr).astype(np.float32)


def make_masks(t0, seq):
    import ml_dtypes
    out = np.zeros((16, 128, 640), np.float32)
    for qt in range(16):
        i = t0 + qt * 128 + np.arange(128)[:, None]
        j = t0 + qt * 128 - 512 + np.arange(640)[None, :]
        out[qt] = ((j <= i) & (j > i - 512) & (j >= 0)).astype(np.float32)
    return out.astype(ml_dtypes.bfloat16)


def stage_core(inputs, core):
    g = core % 4
    b = core // 4
    t0 = g * TOK
    d = {}
    x = np.asarray(inputs["L0_attn_hidden_states"])[b]
    xs = np.zeros((HTOK, D), np.float32)
    xs[HALO:] = x[t0:t0 + TOK]
    if g > 0:
        xs[:HALO] = x[t0 - HALO:t0]
    d["XIN"] = xs
    for nm in ("L0_attn_Wq", "L0_attn_Wk", "L0_attn_Wv", "L0_attn_Wg",
               "L0_attn_Wb", "L0_attn_conv_wq", "L0_attn_conv_wk",
               "L0_attn_conv_wv", "L0_ffn_gate_w", "L0_ffn_expert_w",
               "L0_attn_Wo"):
        d[nm] = np.ascontiguousarray(np.asarray(inputs[nm]), dtype=np.float32)
    d["L0_ffn_gate_b"] = np.asarray(inputs["L0_ffn_gate_b"]).reshape(1, NE)
    d["L0_ffn_expert_b"] = np.ascontiguousarray(
        np.asarray(inputs["L0_ffn_expert_b"])).reshape(1, NE * D)
    d["L0_attn_norm_w"] = np.asarray(
        inputs["L0_attn_norm_w"]).reshape(128, 1).astype(np.float32)
    d["L1_WqT"] = np.ascontiguousarray(np.asarray(inputs["L1_attn_Wq"]).T)
    d["L1_WkT"] = np.ascontiguousarray(np.asarray(inputs["L1_attn_Wk"]).T)
    d["L1_WvT"] = np.ascontiguousarray(np.asarray(inputs["L1_attn_Wv"]).T)
    d["L1_WoT"] = np.ascontiguousarray(np.asarray(inputs["L1_attn_Wo"]).T)
    d["L1_gateT"] = np.ascontiguousarray(np.asarray(inputs["L1_ffn_gate_w"]).T)
    d["L1_upT"] = np.ascontiguousarray(np.asarray(inputs["L1_ffn_up_w"]).T)
    d["L1_downT"] = np.ascontiguousarray(np.asarray(inputs["L1_ffn_down_w"]).T)
    d["L1_gate_s"] = np.ascontiguousarray(
        np.asarray(inputs["L1_ffn_gate_s"]).reshape(NIC, 128).T)
    d["L1_up_s"] = np.ascontiguousarray(
        np.asarray(inputs["L1_ffn_up_s"]).reshape(NIC, 128).T)
    d["L1_down_s"] = np.ascontiguousarray(
        np.asarray(inputs["L1_ffn_down_s"]).reshape(16, 128).T)
    d.update(make_consts())
    rc, rs = make_rope(t0)
    d["RCOS"], d["RSIN"] = rc, rs
    d["MASKS"] = make_masks(t0, SEQ)
    sel4 = np.zeros((128, 4), np.float32)
    sel4[:, g] = 1.0
    d["SEL4"] = sel4
    selp = np.zeros((128, 4), np.float32)
    if g > 0:
        selp[:, g - 1] = 1.0
    d["SELP"] = selp
    sels = np.zeros((1, 4), np.float32)
    sels[0, :g] = 1.0
    d["SELS"] = sels
    return d


def _in_map_for(pb, staged, relayed):
    m = {}
    for name, kind in pb.dram_kind.items():
        if kind != "in":
            continue
        if name in relayed:
            m[name] = relayed[name]
        else:
            m[name] = staged[name]
    return m


_PROGS = None


def kernel_unfused(**inputs):
    global _PROGS
    if _PROGS is None:
        _PROGS = (build_L1(), build_L2(), build_L3(), build_L4())
    L1p, L2p, L3p, L4p = _PROGS
    staged = [stage_core(inputs, c) for c in range(8)]
    relays = [dict() for _ in range(8)]
    outs_all = None
    for li, pbp in enumerate((L1p, L2p, L3p, L4p)):
        in_maps = [_in_map_for(pbp, staged[c], relays[c]) for c in range(8)]
        res = run_bass_kernel_spmd(pbp.nc, in_maps, core_ids=list(range(8)))
        for c in range(8):
            for k, v in res.results[c].items():
                relays[c][k] = v
        if li == 0:
            for c in range(8):
                g0 = (c // 4) * 4
                relays[c]["AB_ALL"] = np.stack(
                    [relays[g0 + p]["ABG"] for p in range(4)])
        if li == 2:
            for c in range(8):
                g0 = (c // 4) * 4
                relays[c]["KV_ALL"] = np.stack(
                    [relays[g0 + p]["KVCAT"] for p in range(4)])
                relays[c]["VS_ALL"] = np.stack(
                    [relays[g0 + p]["VSUM"] for p in range(4)])
        outs_all = res
    out = np.zeros((2, SEQ, D), np.float32)
    for c in range(8):
        g = c % 4
        b = c // 4
        out[b, g * TOK:(g + 1) * TOK] = relays[c]["X2"]
    return out



def p_collective(pb, kind, src_name, dst_name, src_shape, dst_shape, dtype):
    S = pb.S
    src = pb.io(src_name, src_shape, dtype, "internal")
    dst = pb.io(dst_name, dst_shape, dtype, "internal")

    def to2d(ap):
        if len(ap.shape) == 3:
            return ap.rearrange("a b c -> (a b) c")
        return ap

    def fn(e):
        return e.collective_compute(
            kind, mybir.AluOpType.bypass, replica_groups=GROUPS,
            ins=[to2d(src.ap()).opt()], outs=[to2d(dst.ap()).opt()])
    S.do("gpsimd", fn, reads=[pb.dbuf(src_name)], writes=[pb.dbuf(dst_name)],
         tag=f"coll_{src_name}")


def build_fused():
    pb = PB("FUSED")
    pb.boot()
    internal = ["XN_D", "BETA", "SPILL", "QSP", "SCAL", "ABG", "AB_ALL",
                "X1P", "OG_D", "GT", "X1", "XN1_D", "QD1", "KD1", "VTOK",
                "KVCAT", "VSUM", "PFX", "KV_ALL", "VS_ALL", "FFN_D"]
    shapes = {
        "XN_D": ([16, 128, HTOK], BF16), "BETA": ([128, NH0 * NCH], F32),
        "SPILL": ([NH0, NCH, 128, 512], BF16), "QSP": ([NH0, 128, TOK], BF16),
        "SCAL": ([NH0, 128, 32], F32), "ABG": ([NH0 * 128, 256], F32),
        "AB_ALL": ([4, NH0 * 128, 256], F32), "X1P": ([TOK, D], F32),
        "OG_D": ([NH0, 128, TOK], BF16), "GT": ([NCH, 128, D], BF16),
        "X1": ([TOK, D], F32), "XN1_D": ([16, 128, TOK], BF16),
        "QD1": ([16, 128, TOK], BF16), "KD1": ([4, 128, TOK], BF16),
        "VTOK": ([16, 128, 512], BF16), "KVCAT": ([1280, 512], BF16),
        "VSUM": ([1, 512], F32), "PFX": ([16, 128, 512], F32),
        "KV_ALL": ([4, 1280, 512], BF16), "VS_ALL": ([4, 1, 512], F32),
        "FFN_D": ([16, 128, TOK], BF16),
    }
    for nm in internal:
        pb.io(nm, shapes[nm][0], shapes[nm][1], "internal")
    pb.push_scope()
    p_layernorm(pb, "XIN", "XN_D", HTOK)
    p_beta(pb)
    p_delta1(pb)
    p_collective(pb, "AllGather", "ABG", "AB_ALL",
                 [NH0 * 128, 256], [4 * NH0 * 128, 256], F32)
    p_moe(pb)
    p_gproj(pb)
    pb.pop_scope()
    pb.push_scope()
    p_prefix(pb)
    p_delta2(pb)
    pb.pop_scope()
    p_wo0(pb)
    pb.push_scope()
    p_layernorm(pb, "X1", "XN1_D", TOK)
    p_qkv1(pb)
    pb.pop_scope()
    p_collective(pb, "AllGather", "KVCAT", "KV_ALL",
                 [1280, 512], [4 * 1280, 512], BF16)
    p_collective(pb, "AllGather", "VSUM", "VS_ALL",
                 [1, 512], [4, 512], F32)
    pb.push_scope()
    p_load_xnd(pb, "XN1_D", TOK)
    p_ffn1(pb)
    pb.pop_scope()
    p_att1(pb)
    p_wo1(pb)
    pb.S.emit()
    return pb


_FUSED = None


def kernel(**inputs):
    global _FUSED
    if _FUSED is None:
        _FUSED = build_fused()
    pbp = _FUSED
    staged = [stage_core(inputs, c) for c in range(8)]
    in_maps = [_in_map_for(pbp, staged[c], {}) for c in range(8)]
    res = run_bass_kernel_spmd(pbp.nc, in_maps, core_ids=list(range(8)))
    out = np.zeros((2, SEQ, D), np.float32)
    for c in range(8):
        g = c % 4
        b = c // 4
        out[b, g * TOK:(g + 1) * TOK] = res.results[c]["X2"]
    return out
